# version 04
# baseline (speedup 1.0000x reference)
import numpy as np
from contextlib import ExitStack

import concourse.bass as bass
import concourse.mybir as mybir
from concourse.bass_utils import run_bass_kernel_spmd

F32 = mybir.dt.float32
BF16 = mybir.dt.bfloat16
AF = mybir.ActivationFunctionType
ALU = mybir.AluOpType
AX = mybir.AxisListType

N_CORES = 8
T_FULL = 8192
T_CORE = T_FULL // N_CORES
D = 2048
NH = 16
HD = 128
EPS = 1e-5

COMPUTE_ENGINES = ("tensor", "vector", "scalar", "gpsimd")


class Op:
    __slots__ = ("idx", "engine", "fn", "deps", "is_dma", "inc_amt", "marked",
                 "cum", "tag", "sem_key", "slot")

    def __init__(self, idx, engine, fn, deps, is_dma, tag):
        self.idx = idx
        self.engine = engine
        self.fn = fn
        self.deps = deps
        self.is_dma = is_dma
        self.inc_amt = 16 if is_dma else 1
        self.marked = is_dma
        self.cum = None
        self.tag = tag
        self.sem_key = engine
        self.slot = None


class Prog:

    def __init__(self, nc):
        self.nc = nc
        self.ops = []

    barrier_deps = ()

    def op(self, engine, fn, deps=(), dma=False, tag=""):
        deps = tuple(deps) + tuple(self.barrier_deps)
        for d in deps:
            assert isinstance(d, Op) and d.idx < len(self.ops)
        o = Op(len(self.ops), engine, fn, deps, dma, tag)
        for d in o.deps:
            d.marked = True
        self.ops.append(o)
        return o

    def dma(self, engine, fn, deps=(), tag=""):
        return self.op(engine, fn, deps, dma=True, tag=tag)

    def barrier_all(self, engines=COMPUTE_ENGINES + ("sync",)):
        last = {}
        for o in self.ops:
            last[o.engine] = o
        return [last[e] for e in engines if e in last]

    N_DMA_SLOTS = 6

    def emit(self, block, sems, dma_sems):
        counts = {}
        dma_counts = {}
        for o in self.ops:
            if o.is_dma:
                k = dma_counts.get(o.engine, 0)
                dma_counts[o.engine] = k + 1
                o.slot = k % self.N_DMA_SLOTS
                o.sem_key = (o.engine, o.slot)
            c = counts.get(o.sem_key, 0)
            if o.marked:
                c += o.inc_amt
            o.cum = c
            counts[o.sem_key] = c
        streams = {}
        for o in self.ops:
            streams.setdefault(o.engine, []).append(o)

        def get_sem(key):
            if isinstance(key, tuple):
                return dma_sems[key[0]][key[1]]
            return sems[key]

        def run_stream(eng_name, eng):
            hw = {}
            def wait(key, val):
                if hw.get(key, -1) >= val:
                    return
                eng.wait_ge(get_sem(key), val)
                hw[key] = val
            for o in streams.get(eng_name, ()):
                for d in o.deps:
                    assert d.marked, f"dep {d.idx} {d.tag} of {o.idx} {o.tag} unmarked"
                    wait(d.sem_key, d.cum)
                if o.is_dma and o.cum > 16:
                    wait(o.sem_key, o.cum - 16)
                inst = o.fn(eng)
                if o.marked:
                    assert inst is not None, f"op {o.idx} {o.tag} marked but returned no inst"
                    inst.then_inc(get_sem(o.sem_key), o.inc_amt)

        for name in ("sync", "gpsimd", "tensor", "vector", "scalar"):
            if name not in streams:
                continue
            getattr(block, name)(lambda eng, n=name: run_stream(n, eng))

    def finalize_with_outputs(self, out_dmas):
        for o in out_dmas:
            o.marked = True
        def waiter(eng):
            return None
        w = self.op("sync", waiter, deps=tuple(out_dmas), tag="final_wait")
        return w


def ceil_div(a, b):
    return (a + b - 1) // b


def pack_lhsT(Wt, mw=128):
    K_in, M_out = Wt.shape
    nk, nm = K_in // 128, M_out // mw
    return np.ascontiguousarray(
        Wt.reshape(nk, 128, nm, mw).transpose(2, 1, 0, 3).reshape(nm, 128, nk * mw)
    )


class ProgBarrier:
    pass


def _add_barrier(prog):
    last = {}
    for o in prog.ops:
        last[o.engine] = o
    prog.barrier_deps = tuple(last.values())


class CoreBuild:

    def __init__(self, prog, nc, ctx):
        self.p = prog
        self.nc = nc
        self.ctx = ctx
        self._uid = 0
        self.id_bf = None
        self.id_f32 = None
        self.id_ready = None

    def uid(self, base):
        self._uid += 1
        return f"{base}_{self._uid}"

    def sb(self, base, shape, dt=F32):
        return self.ctx.enter_context(self.nc.sbuf_tensor(self.uid(base), shape, dt))

    def ps(self, base, shape, dt=F32):
        return self.ctx.enter_context(self.nc.psum_tensor(self.uid(base), shape, dt))

    def load_identity(self, ident_f32_ap, ident_bf_ap):
        self.id_f32 = self.sb("idf", [128, 128], F32)
        self.id_bf = self.sb("idb", [128, 128], BF16)
        d1 = self.p.dma("sync", lambda e: e.dma_start(out=self.id_f32[:], in_=ident_f32_ap), tag="id32")
        d2 = self.p.dma("sync", lambda e: e.dma_start(out=self.id_bf[:], in_=ident_bf_ap), tag="idbf")
        self.id_ready = [d1, d2]

    def transpose(self, dst_ap, src_ap, pslot_ap, deps, bf16=True, evac_engine="vector",
                  war=()):
        ident = self.id_bf if bf16 else self.id_f32
        tr = self.p.op("tensor",
                       lambda e: e.transpose(pslot_ap, src_ap, ident[:]),
                       deps=tuple(deps) + tuple(war) + tuple(self.id_ready), tag="tr")
        ev = self.p.op(evac_engine,
                       lambda e: (e.tensor_copy(dst_ap, pslot_ap) if evac_engine == "vector"
                                  else e.copy(dst_ap, pslot_ap)),
                       deps=(tr,), tag="trev")
        return ev, tr

    def emit_ln_to_fm(self, x_tiles, x_deps, xn_fm, n_t, scratch=None,
                      stats_out=None):
        p = self.p
        nd = D // 128
        if scratch is None:
            scratch = self.sb("lnscr", [128, D], F32)
        sums_all = self.sb("lnsum", [128, 4 * n_t], F32)
        if stats_out is None:
            stats_out = self.sb("lnstats", [128, 2 * n_t], F32)
        xn_tm = self.sb("xntm", [128, D], BF16)
        ptr = self.ps("lntr", [128, 1024], BF16)
        ready = []
        prev_sq = None
        prev_evs = []
        prev_norm = None
        for t in range(n_t):
            xt = x_tiles[t]
            sums = sums_all.ap()[:, 4 * t:4 * t + 4]
            dsum = p.op("vector", lambda e, xt=xt, sums=sums: e.reduce_sum(
                sums[:, 0:1], xt[:], axis=AX.X), deps=x_deps[t], tag=f"ln_sum{t}")
            sq_deps = list(x_deps[t]) + ([prev_sq] if prev_sq else [])
            dsq = p.op("scalar", lambda e, xt=xt, sums=sums: e.activation(
                scratch[:], xt[:], AF.Square, accum_out=sums[:, 1:2]),
                deps=sq_deps, tag=f"ln_sq{t}")
            prev_sq = dsq
            mu = p.op("vector", lambda e, t=t, sums=sums: e.tensor_scalar_mul(
                stats_out[:, 2 * t:2 * t + 1], sums[:, 0:1], 1.0 / D),
                deps=(dsum,), tag="ln_mu")
            musq = p.op("vector", lambda e, t=t, sums=sums: e.tensor_mul(
                sums[:, 3:4], stats_out[:, 2 * t:2 * t + 1], stats_out[:, 2 * t:2 * t + 1]),
                deps=(mu,), tag="ln_musq")
            var = p.op("vector", lambda e, sums=sums: e.tensor_scalar(
                sums[:, 2:3], sums[:, 1:2], 1.0 / D, None, ALU.mult),
                deps=(dsq,), tag="ln_var0")
            var2 = p.op("vector", lambda e, sums=sums: e.tensor_scalar(
                sums[:, 2:3], sums[:, 2:3], sums[:, 3:4], EPS,
                ALU.subtract, ALU.add),
                deps=(var, musq), tag="ln_var")
            ln_ = p.op("scalar", lambda e, sums=sums: e.activation(
                sums[:, 3:4], sums[:, 2:3], AF.Ln),
                deps=(var2, dsq, musq), tag="ln_ln")
            rstd = p.op("scalar", lambda e, t=t, sums=sums: e.activation(
                stats_out[:, 2 * t + 1:2 * t + 2], sums[:, 3:4], AF.Exp, scale=-0.5),
                deps=(ln_,), tag="ln_rstd")
            nrm_deps = [mu, rstd] + prev_evs
            nrm = p.op("vector", lambda e, xt=xt, t=t: e.tensor_scalar(
                xn_tm[:], xt[:], stats_out[:, 2 * t:2 * t + 1],
                stats_out[:, 2 * t + 1:2 * t + 2], ALU.subtract, ALU.mult),
                deps=nrm_deps, tag=f"ln_nrm{t}")
            prev_norm = nrm
            xnv = xn_fm.ap().rearrange("p (nd tt) -> p nd tt", nd=nd)
            prev_evs = []
            for half in range(2):
                trs = []
                for i in range(8):
                    dti = half * 8 + i
                    _, tr = self._tr_only(ptr[:, i * 128:(i + 1) * 128],
                                          xn_tm[:, dti * 128:(dti + 1) * 128],
                                          deps=(nrm,) + ((prev_half_ev,) if half or t else ()))
                    trs.append(tr)
                ev = p.op("vector", lambda e, half=half, t=t: e.tensor_copy(
                    xnv[:, half * 8:(half + 1) * 8, t * 128:(t + 1) * 128], ptr[:, 0:1024]),
                    deps=tuple(trs), tag="ln_trev")
                prev_half_ev = ev
                prev_evs.append(ev)
                ready.append(ev)
        return ready, stats_out, ptr

    def _tr_only(self, pslot_ap, src_ap, deps, bf16=True, npart=128):
        ident = self.id_bf if bf16 else self.id_f32
        tr = self.p.op("tensor",
                       lambda e: e.transpose(pslot_ap, src_ap, ident[0:npart, 0:npart]),
                       deps=tuple(deps) + tuple(self.id_ready), tag="tr")
        return None, tr

    def emit_ffn0(self, x_tiles, xn_fm, xn_ready, wg_pk, wu_pk, wd_pk, n_t,
                  coeff=0.5, dma_engine="gpsimd"):
        p = self.p
        T = n_t * 128
        n_ch = T // 512
        xnv = xn_fm.ap().rearrange("p (nd tt) -> p nd tt", nd=D // 128)
        hs = self.sb("hs", [128, 16 * T], BF16)
        hsv = hs.ap().rearrange("p (nf tt) -> p nf tt", nf=16)
        wgbuf = [self.sb("wgbuf", [128, D], BF16) for _ in range(2)]
        wubuf = [self.sb("wubuf", [128, D], BF16) for _ in range(2)]
        wdbuf = [self.sb("wdbuf", [128, D], BF16) for _ in range(2)]
        silu_tmp = self.sb("silu", [128, 1024], F32)
        dtmp = self.sb("dtmp", [128, 1024], BF16)
        pg = [self.ps("pg", [128, 512], F32) for _ in range(2)]
        pu = [self.ps("pu", [128, 512], F32) for _ in range(2)]
        pd = [self.ps("pd", [128, 512], F32) for _ in range(2)]
        ptrD = self.ps("ptrD", [128, 1024], BF16)
        dstg = self.sb("dstg", [128, 1024], BF16)
        ptrD_free = None
        dstg_war = [[], []]

        add_ops = [[] for _ in range(n_t)]
        wg_last_mm = [None, None]
        wu_last_mm = [None, None]
        wd_last_mm = [None, None]
        silu_last_tt = [None, None]
        pg_last_read = [None, None]
        pu_last_read = [None, None]
        pd_last_read = [None, None]
        dtmp_last_read = [None, None]
        h_last_writes_prev_range = []

        for r in range(4):
            h_writes = []
            for ft in range(16):
                gft = r * 16 + ft
                b = ft % 2
                war_g = (wg_last_mm[b],) if wg_last_mm[b] else ()
                dg = p.dma(dma_engine, lambda e, gft=gft, b=b: e.dma_start(
                    out=wgbuf[b][:], in_=wg_pk[gft]), deps=war_g, tag=f"wg{gft}")
                war_u = (wu_last_mm[b],) if wu_last_mm[b] else ()
                du = p.dma(dma_engine, lambda e, gft=gft, b=b: e.dma_start(
                    out=wubuf[b][:], in_=wu_pk[gft]), deps=war_u, tag=f"wu{gft}")
                for c in range(2):
                    xr = tuple(xn_ready[8 * c:8 * c + 8])
                    mmg = None
                    war = (silu_last_tt[c],) if silu_last_tt[c] else ()
                    war += (pg_last_read[c],) if pg_last_read[c] else ()
                    for k in range(16):
                        mmg = p.op("tensor", lambda e, b=b, k=k, c=c: e.matmul(
                            pg[c][:, 0:512], wgbuf[b][:, k * 128:(k + 1) * 128],
                            xnv[:, k, c * 512:(c + 1) * 512],
                            start=(k == 0), stop=(k == 15)),
                            deps=(dg,) + xr + (war if k == 0 else ()),
                            tag=f"mmg{gft}_{c}")
                    mmu = None
                    war = (silu_last_tt[c],) if silu_last_tt[c] else ()
                    war += (pu_last_read[c],) if pu_last_read[c] else ()
                    for k in range(16):
                        mmu = p.op("tensor", lambda e, b=b, k=k, c=c: e.matmul(
                            pu[c][:, 0:512], wubuf[b][:, k * 128:(k + 1) * 128],
                            xnv[:, k, c * 512:(c + 1) * 512],
                            start=(k == 0), stop=(k == 15)),
                            deps=(du,) + xr + (war if k == 0 else ()),
                            tag=f"mmu{gft}_{c}")
                    wg_last_mm[b] = mmg
                    wu_last_mm[b] = mmu
                    sil = p.op("scalar", lambda e, c=c: e.activation(
                        silu_tmp[:, c * 512:(c + 1) * 512], pg[c][:, 0:512], AF.Silu),
                        deps=(mmg,) + ((silu_last_tt[c],) if silu_last_tt[c] else ()),
                        tag="silu")
                    pg_last_read[c] = sil
                    tt = p.op("vector", lambda e, c=c, ft=ft: e.tensor_mul(
                        hsv[:, ft, c * 512:(c + 1) * 512],
                        silu_tmp[:, c * 512:(c + 1) * 512], pu[c][:, 0:512]),
                        deps=(sil, mmu) + tuple(h_last_writes_prev_range), tag="hmul")
                    silu_last_tt[c] = tt
                    pu_last_read[c] = tt
                    h_writes.append(tt)
            new_h_guard = []
            for d in range(16):
                b = d % 2
                war_d = (wd_last_mm[b],) if wd_last_mm[b] else ()
                dd = p.dma(dma_engine, lambda e, d=d, r=r, b=b: e.dma_start(
                    out=wdbuf[b][:], in_=wd_pk[d][:, r * 2048:(r + 1) * 2048]),
                    deps=war_d, tag=f"wd{r}_{d}")
                for c in range(2):
                    mmd = None
                    war = (pd_last_read[c],) if pd_last_read[c] else ()
                    for k in range(16):
                        mmd = p.op("tensor", lambda e, b=b, k=k, c=c: e.matmul(
                            pd[c][:, 0:512], wdbuf[b][:, k * 128:(k + 1) * 128],
                            hsv[:, k, c * 512:(c + 1) * 512],
                            start=(k == 0), stop=(k == 15)),
                            deps=(dd,) + tuple(h_writes) + (war if k == 0 else ()),
                            tag=f"mmd{r}_{d}_{c}")
                    wd_last_mm[b] = mmd
                    ev = p.op("vector", lambda e, c=c: e.tensor_copy(
                        dtmp[:, c * 512:(c + 1) * 512], pd[c][:, 0:512]),
                        deps=(mmd,) + ((dtmp_last_read[c],) if dtmp_last_read[c] else ()),
                        tag="dev")
                    pd_last_read[c] = ev
                    tr_ops = []
                    for i in range(4):
                        slot = c * 4 + i
                        war_t = (ptrD_free,) if (i == 0 and ptrD_free) else ()
                        _, tr = self._tr_only(
                            ptrD[:, slot * 128:(slot + 1) * 128],
                            dtmp[:, c * 512 + i * 128: c * 512 + (i + 1) * 128],
                            deps=(ev,) + war_t)
                        tr_ops.append(tr)
                    ev4 = p.op("vector", lambda e, c=c: e.tensor_copy(
                        dstg[:, c * 512:(c + 1) * 512],
                        ptrD[:, c * 512:(c + 1) * 512]),
                        deps=tuple(tr_ops) + tuple(dstg_war[c]), tag="dev4")
                    ptrD_free = ev4
                    stts = []
                    for i in range(4):
                        t_idx = c * 4 + i
                        stt = p.op("vector", lambda e, i=i, c=c, t_idx=t_idx, d=d:
                                   e.scalar_tensor_tensor(
                                       x_tiles[t_idx][:, d * 128:(d + 1) * 128],
                                       dstg[:, c * 512 + i * 128:c * 512 + (i + 1) * 128],
                                       coeff,
                                       x_tiles[t_idx][:, d * 128:(d + 1) * 128],
                                       ALU.mult, ALU.add),
                                   deps=(ev4,), tag=f"xadd{r}_{d}_{t_idx}")
                        stts.append(stt)
                        add_ops[t_idx].append(stt)
                    dstg_war[c] = stts
                    dtmp_last_read[c] = tr_ops[-1]
            h_last_writes_prev_range = [wd_last_mm[0], wd_last_mm[1]]
        return add_ops

    def emit_l0attn_A(self, x_src, xh3, xh_dep, colpack, colpack_dep,
                      wk_pk, wv_pk, wig_pk, wq_pk, wg_pk, bg, bg_dep, onehot,
                      onehot_dep, mem_dram, qhat_buf, xc_buf, gamma_sb,
                      totalP_sb, Mrow_sb, NCP=10):
        p = self.p
        n_t = 8
        W = 1028
        xnx = self.sb("xnx", [128, 16 * W], BF16)
        xnv = xnx.ap().rearrange("p (nd w) -> p nd w", nd=16)
        nd = 16
        scratch = self.sb("lnscr", [128, D], F32)
        sums_all = self.sb("lnsum", [128, 4 * n_t], F32)
        stats = self.sb("lnstats", [128, 2 * n_t], F32)
        xn_tm = self.sb("xntm", [128, D], BF16)
        ptr = self.ps("lntr", [128, 1024], BF16)
        xn_ready = []
        prev_sq = None
        prev_evs = []
        prev_half_ev = None
        xbuf = self.sb("xbuf", [128, 2 * D], F32)
        ptrB_stage = self.sb("ptrBst", [128, 2048], BF16)
        stage_war = [[], []]
        xslot = [xbuf.ap()[:, 0:D], xbuf.ap()[:, D:2 * D]]
        xslot_readers = [[], []]
        for t in range(n_t):
            xt = xslot[t % 2]
            dl = p.dma("sync", lambda e, t=t, xt=xt: e.dma_start(
                out=xt, in_=x_src[t * 128:(t + 1) * 128, :]),
                deps=tuple(xslot_readers[t % 2]), tag=f"xload{t}")
            sums = sums_all.ap()[:, 4 * t:4 * t + 4]
            dsum = p.op("vector", lambda e, xt=xt, sums=sums: e.reduce_sum(
                sums[:, 0:1], xt[:], axis=AX.X), deps=(dl,), tag="ln_sum")
            dsq = p.op("scalar", lambda e, xt=xt, sums=sums: e.activation(
                scratch[:], xt[:], AF.Square, accum_out=sums[:, 1:2]),
                deps=[dl] + ([prev_sq] if prev_sq else []), tag="ln_sq")
            prev_sq = dsq
            mu = p.op("vector", lambda e, t=t, sums=sums: e.tensor_scalar_mul(
                stats[:, 2 * t:2 * t + 1], sums[:, 0:1], 1.0 / D), deps=(dsum,))
            musq = p.op("vector", lambda e, sums=sums, t=t: e.tensor_mul(
                sums[:, 3:4], stats[:, 2 * t:2 * t + 1], stats[:, 2 * t:2 * t + 1]),
                deps=(mu,))
            var = p.op("vector", lambda e, sums=sums: e.tensor_scalar(
                sums[:, 2:3], sums[:, 1:2], 1.0 / D, None, ALU.mult), deps=(dsq,))
            var2 = p.op("vector", lambda e, sums=sums: e.tensor_scalar(
                sums[:, 2:3], sums[:, 2:3], sums[:, 3:4], EPS, ALU.subtract, ALU.add),
                deps=(var, musq))
            ln_ = p.op("scalar", lambda e, sums=sums: e.activation(
                sums[:, 3:4], sums[:, 2:3], AF.Ln), deps=(var2, musq))
            rstd = p.op("scalar", lambda e, t=t, sums=sums: e.activation(
                stats[:, 2 * t + 1:2 * t + 2], sums[:, 3:4], AF.Exp, scale=-0.5),
                deps=(ln_,))
            nrm = p.op("vector", lambda e, xt=xt, t=t: e.tensor_scalar(
                xn_tm[:], xt[:], stats[:, 2 * t:2 * t + 1],
                stats[:, 2 * t + 1:2 * t + 2], ALU.subtract, ALU.mult),
                deps=[mu, rstd] + prev_evs, tag="ln_nrm")
            xslot_readers[t % 2] = [nrm, dsq, dsum]
            prev_evs = []
            for half in range(2):
                trs = []
                for i in range(8):
                    dti = half * 8 + i
                    dps = (nrm,) if (half == 0 and t == 0) else (nrm, prev_half_ev)
                    _, tr = self._tr_only(ptr[:, i * 128:(i + 1) * 128],
                                          xn_tm[:, dti * 128:(dti + 1) * 128],
                                          deps=[d for d in dps if d is not None])
                    trs.append(tr)
                ev = p.op("vector", lambda e, half=half, t=t: e.tensor_copy(
                    xnv[:, half * 8:(half + 1) * 8, 4 + t * 128:4 + (t + 1) * 128],
                    ptr[:, 0:1024]), deps=tuple(trs), tag="ln_trev")
                prev_half_ev = ev
                prev_evs.append(ev)
                xn_ready.append(ev)
        hsum = self.sb("hsum", [128, 4], F32)
        hstat = self.sb("hstat", [128, 2], F32)
        htm = xn_tm
        hd = p.op("vector", lambda e: e.reduce_sum(hsum[0:3, 0:1], xh3[:], axis=AX.X),
                  deps=(xh_dep,), tag="halo_sum")
        hq = p.op("scalar", lambda e: e.activation(scratch[0:3, :], xh3[:], AF.Square,
                                                   accum_out=hsum[0:3, 1:2]),
                  deps=(xh_dep, prev_sq), tag="halo_sq")
        hmu = p.op("vector", lambda e: e.tensor_scalar_mul(hstat[0:3, 0:1],
                   hsum[0:3, 0:1], 1.0 / D), deps=(hd,))
        hms = p.op("vector", lambda e: e.tensor_mul(hsum[0:3, 3:4], hstat[0:3, 0:1],
                   hstat[0:3, 0:1]), deps=(hmu,))
        hv = p.op("vector", lambda e: e.tensor_scalar(hsum[0:3, 2:3], hsum[0:3, 1:2],
                  1.0 / D, None, ALU.mult), deps=(hq,))
        hv2 = p.op("vector", lambda e: e.tensor_scalar(hsum[0:3, 2:3], hsum[0:3, 2:3],
                   hsum[0:3, 3:4], EPS, ALU.subtract, ALU.add), deps=(hv, hms))
        hln = p.op("scalar", lambda e: e.activation(hsum[0:3, 3:4], hsum[0:3, 2:3],
                   AF.Ln), deps=(hv2, hms))
        hrs = p.op("scalar", lambda e: e.activation(hstat[0:3, 1:2], hsum[0:3, 3:4],
                   AF.Exp, scale=-0.5), deps=(hln,))
        hnrm = p.op("vector", lambda e: e.tensor_scalar(
            htm.ap()[0:3, :], xh3[:], hstat[0:3, 0:1], hstat[0:3, 1:2],
            ALU.subtract, ALU.mult),
            deps=(hmu, hrs, prev_half_ev), tag="halo_nrm")
        halo_evs = []
        for dti in range(16):
            _, tr = self._tr_only(ptr[:, 0:3],
                                  htm.ap()[0:3, dti * 128:(dti + 1) * 128],
                                  deps=(hnrm, prev_half_ev), npart=3)
            ev = p.op("vector", lambda e, dti=dti: e.tensor_copy(
                xnv[:, dti, 1:4], ptr[:, 0:3]), deps=(tr,), tag="halo_ev")
            prev_half_ev = ev
            halo_evs.append(ev)
        xn_ready += halo_evs
        cscr = self.sb("cscr", [128, 1024], F32)
        cscr2 = self.sb("cscr2", [128, 1024], F32)
        xcv = xc_buf.ap().rearrange("p (nd w) -> p nd w", nd=16)
        xc_ready = []
        prev_c = []
        for dt in range(16):
            wc = [colpack.ap()[:, dt * NCP + j:dt * NCP + j + 1] for j in range(4)]
            cb = colpack.ap()[:, dt * NCP + 4:dt * NCP + 5]
            dps = tuple(xn_ready) + (colpack_dep,) + tuple(prev_c)
            c1 = p.op("vector", lambda e, dt=dt, wc=wc: e.tensor_scalar_mul(
                cscr[:], xnv[:, dt, 1:1025], wc[0]), deps=dps, tag="conv1")
            c2 = p.op("vector", lambda e, dt=dt, wc=wc: e.scalar_tensor_tensor(
                cscr2[:], xnv[:, dt, 2:1026], wc[1], cscr[:], ALU.mult, ALU.add),
                deps=(c1,) + tuple(prev_c), tag="conv2")
            c3 = p.op("vector", lambda e, dt=dt, wc=wc: e.scalar_tensor_tensor(
                cscr[:], xnv[:, dt, 3:1027], wc[2], cscr2[:], ALU.mult, ALU.add),
                deps=(c2,), tag="conv3")
            c4 = p.op("vector", lambda e, dt=dt, wc=wc: e.scalar_tensor_tensor(
                cscr2[:], xnv[:, dt, 4:1028], wc[3], cscr[:], ALU.mult, ALU.add),
                deps=(c3,), tag="conv4")
            sil = p.op("scalar", lambda e, dt=dt, cb=cb: e.activation(
                xcv[:, dt, 0:1024], cscr2[:], AF.Silu, bias=cb),
                deps=(c4,), tag="conv_silu")
            prev_c = [sil, c4]
            xc_ready.append(sil)
        wgslab = self.sb("wgslab", [128, 256], BF16)
        dwg = p.dma("gpsimd", lambda e: e.dma_start(out=wgslab[:], in_=wg_pk[0]),
                    tag="wg_dma")
        pwg = self.ps("pwg", [16, 512], F32)
        zer16 = scratch.ap()[0:16, 0:1024]
        cumP = scratch.ap()[0:16, 1024:2048]
        zmem = p.op("vector", lambda e: e.memset(zer16, 0.0),
                    deps=(hq,), tag="z16")
        gam_evs = []
        for c in range(2):
            mmg = None
            for k in range(16):
                mmg = p.op("tensor", lambda e, k=k, c=c: e.matmul(
                    pwg[:, 0:512], wgslab[:, k * 16:(k + 1) * 16],
                    xcv[:, k, c * 512:(c + 1) * 512],
                    start=(k == 0), stop=(k == 15)),
                    deps=(dwg,) + tuple(xc_ready) + tuple(gam_evs), tag="mm_wg")
            sg = p.op("scalar", lambda e, c=c: e.activation(
                gamma_sb[:, c * 512:(c + 1) * 512], pwg[0:16, 0:512], AF.Sigmoid,
                bias=bg[:]), deps=(mmg, bg_dep), tag="gamma_sig")
            gam_evs.append(sg)
        cpr = p.op("vector", lambda e: e.tensor_tensor_scan(
            cumP, gamma_sb.ap()[0:16, :], zer16, 1.0, ALU.mult, ALU.add),
            deps=tuple(gam_evs) + (zmem,), tag="cumP")
        tp = p.op("vector", lambda e: e.tensor_copy(totalP_sb[:], cumP[:, 1023:1024]),
                  deps=(cpr,), tag="totalP")
        pK = self.ps("pK", [128, 512], F32)
        pV = self.ps("pV", [128, 512], F32)
        pIG = self.ps("pIG", [128, 512], F32)
        pQ = self.ps("pQ", [128, 512], F32)
        ptrA = self.ps("ptrA", [128, 512], F32)
        pGB = self.ps("pGB", [128, 512], F32)
        k_ts = [self.sb("k_t", [128, 1024], F32),
                scratch.ap()[:, 0:1024]]
        v_ts = [self.sb("v_t", [128, 1024], BF16),
                xn_tm.ap()[:, 0:1024]]
        ig_ts = [self.sb("ig_t", [128, 1024], BF16),
                 xn_tm.ap()[:, 1024:2048]]
        qfs = [self.sb("qf", [128, 1024], F32),
               scratch.ap()[:, 1024:2048]]
        gb = self.sb("gb", [128, 1024], F32)
        memwork = self.sb("memw", [128, 2 * 1024], BF16)
        memwork_free = [[], []]
        stA = self.sb("stA", [128, 512], F32)
        stB = self.sb("stB", [128, 512], F32)
        sqsc = self.sb("sqsc", [128, 128], F32)
        l2c = self.sb("l2c", [128, 12], F32)
        Mcols = self.sb("Mcols", [128, 16], F32)
        slabs = {}
        for nm in ("k", "v"):
            slabs[nm] = [self.sb(f"{nm}slab", [128, 2048], BF16) for _ in range(2)]
        for nm in ("ig", "q"):
            b0 = self.sb(f"{nm}slab", [128, 2048], BF16)
            slabs[nm] = [b0, b0]
        pk_src = {"k": wk_pk, "v": wv_pk, "ig": wig_pk, "q": wq_pk}
        last_mm = {("k", 0): None, ("k", 1): None, ("v", 0): None, ("v", 1): None,
                   ("ig", 0): None, ("ig", 1): None, ("q", 0): None, ("q", 1): None}
        psum_of = {"k": pK, "v": pV, "ig": pIG, "q": pQ}
        ev_of_prev = {}
        slot_last = [None] * 4
        scan_prev = {0: None, 1: None}
        mem_evs = []
        ptrA_free = None
        stA_war = []
        stB_war = []
        sq_prev = None
        reader_war = {(n, par): [tp, xn_ready[-1]]
                      for n in ("k", "v", "ig", "q") for par in (0, 1)}
        for h in range(16):
            par = h % 2
            k_t, v_t, ig_t, qf = k_ts[par], v_ts[par], ig_ts[par], qfs[par]
            b_scr = qf
            dmas = {}
            bs_of = {nm: (h % 2 if nm in ("k", "v") else 0) for nm in ("k", "v", "ig", "q")}
            for nm in ("k", "v", "ig", "q"):
                bslot = bs_of[nm]
                war = (last_mm[(nm, bslot)],) if last_mm[(nm, bslot)] else ()
                dmas[nm] = p.dma("gpsimd", lambda e, nm=nm, h=h, bslot=bslot:
                                 e.dma_start(out=slabs[nm][bslot][:], in_=pk_src[nm][h]),
                                 deps=war, tag=f"slab_{nm}{h}")
            evs = {}
            for c in range(2):
                for nm in ("k", "v", "q", "ig"):
                    mm = None
                    bslot = bs_of[nm]
                    war = (ev_of_prev[nm],) if nm in ev_of_prev else ()
                    rhs_v, rhs_off = (xcv, 0) if nm == "ig" else (xnv, 4)
                    rdy = tuple(xc_ready) if nm == "ig" else tuple(xn_ready)
                    for k in range(16):
                        mm = p.op("tensor", lambda e, nm=nm, k=k, c=c, bslot=bslot,
                                  rhs_v=rhs_v, rhs_off=rhs_off:
                                  e.matmul(psum_of[nm][:, 0:512],
                                           slabs[nm][bslot][:, k * 128:(k + 1) * 128],
                                           rhs_v[:, k, rhs_off + c * 512:rhs_off + (c + 1) * 512],
                                           start=(k == 0), stop=(k == 15)),
                                  deps=(dmas[nm],) + rdy + (war if k == 0 else ()),
                                  tag=f"mm_{nm}{h}_{c}")
                    last_mm[(nm, bslot)] = mm
                    if nm == "ig":
                        bigcol = colpack.ap()[:, h * NCP + 5:h * NCP + 6]
                        ev = p.op("scalar", lambda e, c=c, bigcol=bigcol, ig_t=ig_t:
                            e.activation(
                            ig_t[:, c * 512:(c + 1) * 512], pIG[:, 0:512], AF.Sigmoid,
                            bias=bigcol),
                            deps=(mm, colpack_dep) + tuple(reader_war[(nm, par)]),
                            tag="ig_sig")
                    else:
                        dst = {"k": k_t, "v": v_t, "q": qf}[nm]
                        ev = p.op("vector", lambda e, c=c, dst=dst, nm=nm: e.tensor_copy(
                            dst[:, c * 512:(c + 1) * 512], psum_of[nm][:, 0:512]),
                            deps=(mm,) + tuple(reader_war[(nm, par)]), tag=f"ev_{nm}")
                    ev_of_prev[nm] = ev
                    evs[(nm, c)] = ev
            kq_done = []
            for nm in ("k", "q"):
                src = k_t if nm == "k" else qf
                for ch in range(2):
                    trs = []
                    for i in range(4):
                        t = ch * 4 + i
                        war = (ptrA_free,) if (i == 0 and ptrA_free) else ()
                        _, tr = self._tr_only(ptrA[:, i * 128:(i + 1) * 128],
                                              src[:, t * 128:(t + 1) * 128],
                                              deps=(evs[(nm, t // 4)],) + war,
                                              bf16=False)
                        trs.append(tr)
                    ev4 = p.op("vector", lambda e: e.tensor_copy(stA[:], ptrA[:, 0:512]),
                               deps=tuple(trs) + tuple(stA_war), tag="l2_ev4")
                    ptrA_free = ev4
                    sqs = []
                    prev = sq_prev
                    for i in range(4):
                        sq = p.op("scalar", lambda e, i=i: e.activation(
                            sqsc[:], stA[:, i * 128:(i + 1) * 128], AF.Square,
                            accum_out=l2c[:, i:i + 1]),
                            deps=(ev4,) + ((prev,) if prev else ()), tag="l2_sq")
                        prev = sq
                        sqs.append(sq)
                    sq_prev = prev
                    sqr = p.op("scalar", lambda e: e.activation(
                        l2c[:, 4:8], l2c[:, 0:4], AF.Sqrt), deps=tuple(sqs), tag="l2_sqrt")
                    mx = p.op("vector", lambda e: e.tensor_scalar_max(
                        l2c[:, 8:12], l2c[:, 4:8], 1e-12), deps=(sqr,))
                    rc = p.op("vector", lambda e: e.reciprocal(
                        l2c[:, 8:12], l2c[:, 8:12]), deps=(mx,))
                    if nm == "k":
                        nss = []
                        for i in range(4):
                            ns = p.op("vector", lambda e, i=i: e.tensor_scalar_mul(
                                stB[:, i * 128:(i + 1) * 128],
                                stA[:, i * 128:(i + 1) * 128], l2c[:, 8 + i:9 + i]),
                                deps=(rc,) + tuple(stB_war), tag="l2_mul")
                            nss.append(ns)
                        stA_war = nss
                        trbs = []
                        for i in range(4):
                            war = (ptrA_free,) if (i == 0) else ()
                            _, trb = self._tr_only(ptrA[:, i * 128:(i + 1) * 128],
                                                   stB[:, i * 128:(i + 1) * 128],
                                                   deps=(nss[i],) + war, bf16=False)
                            trbs.append(trb)
                        evb = p.op("vector", lambda e, ch=ch, k_t=k_t: e.tensor_copy(
                            k_t[:, ch * 512:(ch + 1) * 512], ptrA[:, 0:512]),
                            deps=tuple(trbs), tag="l2_evb")
                        ptrA_free = evb
                        stB_war = [evb]
                        kq_done.append(evb)
                    else:
                        nss = []
                        for i in range(4):
                            t = ch * 4 + i
                            ns = p.op("vector", lambda e, t=t, i=i, h=h: e.tensor_scalar_mul(
                                qhat_buf.ap()[:, t * D + h * 128:t * D + (h + 1) * 128],
                                stA[:, i * 128:(i + 1) * 128], l2c[:, 8 + i:9 + i]),
                                deps=(rc,), tag="qhat_mul")
                            nss.append(ns)
                        stA_war = nss
                        kq_done += nss
            b1 = p.op("vector", lambda e, b_scr=b_scr, ig_t=ig_t, k_t=k_t:
                      e.tensor_mul(b_scr[:], ig_t[:], k_t[:]),
                      deps=tuple(kq_done) + (evs[("ig", 0)], evs[("ig", 1)]) +
                      ((scan_prev[par],) if scan_prev[par] else ()), tag="b1")
            b2 = p.op("vector", lambda e, b_scr=b_scr, v_t=v_t:
                      e.tensor_mul(b_scr[:], b_scr[:], v_t[:]),
                      deps=(b1, evs[("v", 0)], evs[("v", 1)]), tag="b2")
            gbe = []
            for c in range(2):
                mmb = p.op("tensor", lambda e, c=c, h=h: e.matmul(
                    pGB[:, 0:512],
                    onehot[:, h * 128:(h + 1) * 128],
                    gamma_sb.ap()[0:16, c * 512:(c + 1) * 512], start=True, stop=True),
                    deps=tuple(gam_evs) + (onehot_dep,) +
                    ((gb_prev,) if h > 0 or c > 0 else ()), tag="mm_gb")
                ge = p.op("vector", lambda e, c=c: e.tensor_copy(
                    gb[:, c * 512:(c + 1) * 512], pGB[:, 0:512]), deps=(mmb,), tag="gb_ev")
                gb_prev = ge
                gbe.append(ge)
            dbg_ops = []
            ms = h % 2
            war_m = tuple(memwork_free[ms])
            sc = p.op("vector", lambda e, h=h, ms=ms, b_scr=b_scr: e.tensor_tensor_scan(
                memwork.ap()[:, ms * 1024:(ms + 1) * 1024], gb[:], b_scr[:], 0.0,
                ALU.mult, ALU.add), deps=(b2,) + tuple(gbe) + war_m, tag=f"scan{h}")
            scan_prev[par] = sc
            mc = p.op("vector", lambda e, h=h, ms=ms: e.tensor_copy(
                Mcols[:, h:h + 1],
                memwork.ap()[:, ms * 1024 + 1023:ms * 1024 + 1024]),
                deps=(sc,), tag="Mcol")
            dmem = p.dma("sync", lambda e, h=h, ms=ms: e.dma_start(
                out=mem_dram[:, h * 1024:(h + 1) * 1024],
                in_=memwork.ap()[:, ms * 1024:(ms + 1) * 1024]),
                deps=(sc,), tag=f"mem_st{h}")
            memwork_free[ms] = [mc, dmem]
            mem_evs.append(mc)
            mem_evs.append(dmem)
            reader_war[("k", par)] = [b1] + dbg_ops
            reader_war[("v", par)] = [b2] + dbg_ops
            reader_war[("ig", par)] = [b1] + dbg_ops
            reader_war[("q", par)] = [sc] + dbg_ops
        if not mem_evs:
            mem_evs = [tp]
        _, trM = self._tr_only(pwg[:, 0:128], Mcols[:], deps=tuple(mem_evs), bf16=False)
        evM = p.op("vector", lambda e: e.tensor_copy(Mrow_sb[:], pwg[0:16, 0:128]),
                   deps=(trM,), tag="Mrow")
        return {"mem": mem_evs, "qhat": mem_evs, "xc": xc_ready,
                "gamma": gam_evs, "totalP": [tp], "Mrow": [evM]}

    def emit_l0attn_B(self, x_src, x_dst, mem_buf, mem_dep, qhat_buf, qhat_dep,
                      xc_buf, xc_dep, gamma_sb, gamma_dep, allP, allP_dep,
                      allM, allM_dep, mmask, mmask_dep, ommask, ommask_dep,
                      onehot, onehot_dep, ones8, ones8_dep, colpack, colpack_dep,
                      mnw, mnw_dep, mnb, mnb_dep, wog_pk, wo_pk, NCP=10):
        p = self.p
        T = T_CORE
        ptrA = self.ps("ptrA", [128, 512], F32)
        pGB = self.ps("pGB", [128, 512], F32)
        pO = [self.ps("pO", [128, 512], F32) for _ in range(2)]
        pOG = self.ps("pOG", [128, 512], F32)
        pW = self.ps("pW", [128, 512], F32)
        ptrB = self.ps("ptrB", [128, 1024], BF16)
        P_T = self.sb("P_T", [16, 8], F32)
        S = self.sb("S", [16, 9], F32)
        W = self.sb("W", [8, D], F32)
        carry = self.sb("carry", [128, 16], F32)
        gb = self.sb("gb", [128, 1024], F32)
        z = self.sb("z", [128, 1024], F32)
        zeros = self.sb("zeros", [128, 1024], BF16)
        og2 = [self.sb("og", [128, 1024], BF16) for _ in range(2)]
        stA = self.sb("stA", [128, 512], F32)
        stB = self.sb("stB", [128, 512], F32)
        sqsc = self.sb("sqsc", [128, 128], F32)
        g1c = self.sb("g1c", [128, 40], F32)
        ofm = xc_buf
        wogbuf = [self.sb("wogb", [128, D], BF16) for _ in range(2)]
        wobuf = [self.sb("wob", [128, D], BF16) for _ in range(2)]
        xbuf = self.sb("xbuf", [128, 2 * D], F32)
        ptrB_stage = self.sb("ptrBst", [128, 2048], BF16)
        stage_war = [[], []]

        _, trP = self._tr_only(pW[0:16, 0:8], allP[0:8, 0:16], deps=(allP_dep,),
                               bf16=False, npart=8)
        evP = p.op("vector", lambda e: e.tensor_copy(P_T[:], pW[0:16, 0:8]),
                   deps=(trP,), tag="evP")
        q1 = p.op("vector", lambda e: e.tensor_mul(S[:, 0:8], P_T[:], mmask[:]),
                  deps=(evP, mmask_dep), tag="Q1")
        q2 = p.op("vector", lambda e: e.tensor_add(S[:, 0:8], S[:, 0:8], ommask[:]),
                  deps=(q1, ommask_dep), tag="Q2")
        sset = p.op("vector", lambda e: e.memset(S[:, 8:9], 1.0), tag="S1")
        prev = q2
        sufs = [sset, q2]
        for j in range(7, -1, -1):
            prev = p.op("vector", lambda e, j=j: e.tensor_mul(
                S[:, j:j + 1], S[:, j + 1:j + 2],
                S[:, j:j + 1]), deps=(prev, sset), tag=f"suf{j}")
            sufs.append(prev)
        selc = p.op("vector", lambda e: e.tensor_mul(P_T[:], S[:, 1:9], mmask[:]),
                    deps=(prev, mmask_dep), tag="selc")
        wmm = []
        prev_evW = None
        for c4 in range(4):
            mmw = p.op("tensor", lambda e, c4=c4: e.matmul(
                pW[0:8, 0:512], P_T[0:16, 0:8], onehot[:, c4 * 512:(c4 + 1) * 512],
                start=True, stop=True),
                deps=(selc, onehot_dep, trP) + ((prev_evW,) if prev_evW else ()),
                tag="mmW")
            evw = p.op("vector", lambda e, c4=c4: e.tensor_copy(
                W.ap()[:, c4 * 512:(c4 + 1) * 512], pW[0:8, 0:512]),
                deps=(mmw,), tag="evW")
            prev_evW = evw
            wmm.append(evw)
        wm = p.op("vector", lambda e: e.tensor_mul(W[:], W[:], allM[:]),
                  deps=tuple(wmm) + (allM_dep,), tag="WM")
        ccm = []
        for h in range(16):
            mmc = p.op("tensor", lambda e, h=h: e.matmul(
                pW[0:128, h:h + 1], W.ap()[0:8, h * 128:(h + 1) * 128], ones8[:],
                start=True, stop=True),
                deps=(wm, ones8_dep) + ((prev_evW,) if h == 0 else ()), tag="mmcc")
            ccm.append(mmc)
        evcc = p.op("vector", lambda e: e.tensor_copy(carry[:], pW[:, 0:16]),
                    deps=tuple(ccm), tag="evcc")
        zm = p.op("vector", lambda e: e.memset(zeros[:], 0.0), tag="zeros")

        wog_last = [None, None]
        wo_last = [None, None]
        gb_prev = None
        ptrA_free = None
        stA_war = []
        stB_war = []
        og_war = [[], []]
        pog_rd = []
        sq_prev = None
        prewo_done = []
        mem_free = {}
        z_readers = []
        for h in range(16):
            bs = h % 2
            war = (wog_last[bs],) if wog_last[bs] else ()
            dwog = p.dma("gpsimd", lambda e, h=h, bs=bs: e.dma_start(
                out=wogbuf[bs][:], in_=wog_pk[h]), deps=war, tag=f"wog{h}")
            og = og2[h % 2]
            oge = []
            og_prev_sig = og_war[h % 2]
            for c in range(2):
                mm = None
                for k in range(16):
                    mm = p.op("tensor", lambda e, k=k, c=c, bs=bs: e.matmul(
                        pOG[:, 0:512], wogbuf[bs][:, k * 128:(k + 1) * 128],
                        xc_buf.ap().rearrange("p (nd w) -> p nd w", nd=16)[
                            :, k, c * 512:(c + 1) * 512],
                        start=(k == 0), stop=(k == 15)),
                        deps=(dwog, xc_dep) +
                        ((tuple(og_prev_sig) + tuple(pog_rd)) if k == 0 else ()),
                        tag=f"mmog{h}_{c}")
                bogcol = colpack.ap()[:, h * NCP + 6:h * NCP + 7]
                sg = p.op("scalar", lambda e, c=c, bogcol=bogcol, og=og: e.activation(
                    og[:, c * 512:(c + 1) * 512], pOG[:, 0:512], AF.Sigmoid,
                    bias=bogcol), deps=(mm, colpack_dep), tag="og_sig")
                pog_rd = [sg]
                oge.append(sg)
            wog_last[bs] = oge[-1]
            gbe = []
            for c in range(2):
                mmb = p.op("tensor", lambda e, c=c, h=h: e.matmul(
                    pGB[:, 0:512], onehot[:, h * 128:(h + 1) * 128],
                    gamma_sb.ap()[0:16, c * 512:(c + 1) * 512], start=True, stop=True),
                    deps=(gamma_dep, onehot_dep) + ((gb_prev,) if gb_prev else ()),
                    tag="mm_gb")
                ge = p.op("vector", lambda e, c=c: e.tensor_copy(
                    gb[:, c * 512:(c + 1) * 512], pGB[:, 0:512]), deps=(mmb,),
                    tag="gb_ev")
                gb_prev = ge
                gbe.append(ge)
            zdeps = tuple(gbe) + (evcc, zm) + tuple(z_readers)
            zs = p.op("vector", lambda e, h=h: e.tensor_tensor_scan(
                z[:], gb[:], zeros[:], carry[:, h:h + 1], ALU.mult, ALU.add),
                deps=zdeps, tag=f"zscan{h}")
            mf = p.op("vector", lambda e, h=h: e.tensor_add(
                z[:], mem_buf.ap()[:, h * 1024:(h + 1) * 1024], z[:]),
                deps=(zs, mem_dep), tag="memf")
            uch = []
            for ch in range(2):
                trs = []
                for i in range(4):
                    t = ch * 4 + i
                    war = (ptrA_free,) if (i == 0 and ptrA_free) else ()
                    _, tr = self._tr_only(ptrA[:, i * 128:(i + 1) * 128],
                                          z[:, t * 128:(t + 1) * 128],
                                          deps=(mf,) + war, bf16=False)
                    trs.append(tr)
                ev4 = p.op("vector", lambda e: e.tensor_copy(stA[:], ptrA[:, 0:512]),
                           deps=tuple(trs) + tuple(stA_war), tag="gn_ev4")
                ptrA_free = ev4
                if ch == 0:
                    z_readers = []
                z_readers.append(ev4)
                sums = []
                prevq = sq_prev
                for i in range(4):
                    ds = p.op("vector", lambda e, i=i: e.reduce_sum(
                        g1c[:, i:i + 1], stA[:, i * 128:(i + 1) * 128], axis=AX.X),
                        deps=(ev4,), tag="gn1_sum")
                    dq = p.op("scalar", lambda e, i=i: e.activation(
                        sqsc[:], stA[:, i * 128:(i + 1) * 128], AF.Square,
                        accum_out=g1c[:, 4 + i:5 + i]),
                        deps=(ev4,) + ((prevq,) if prevq else ()), tag="gn1_sq")
                    prevq = dq
                    sums += [ds, dq]
                sq_prev = prevq
                st1 = self._gn_stats(g1c, 0, sums)
                ops_i = []
                for i in range(4):
                    t = ch * 4 + i
                    n1 = p.op("vector", lambda e, i=i: e.tensor_scalar(
                        stB[:, i * 128:(i + 1) * 128], stA[:, i * 128:(i + 1) * 128],
                        g1c[:, 8 + i:9 + i], g1c[:, 12 + i:13 + i],
                        ALU.subtract, ALU.mult),
                        deps=(st1,) + tuple(stB_war), tag="gn1_n")
                    a1 = p.op("vector", lambda e, i=i, h=h: e.tensor_mul(
                        stB[:, i * 128:(i + 1) * 128], stB[:, i * 128:(i + 1) * 128],
                        mnw.ap()[:, h * 128:(h + 1) * 128]), deps=(n1, mnw_dep))
                    a2 = p.op("vector", lambda e, i=i, h=h: e.tensor_add(
                        stB[:, i * 128:(i + 1) * 128], stB[:, i * 128:(i + 1) * 128],
                        mnb.ap()[:, h * 128:(h + 1) * 128]), deps=(a1, mnb_dep))
                    qm = p.op("vector", lambda e, i=i, t=t, h=h: e.tensor_mul(
                        stB[:, i * 128:(i + 1) * 128], stB[:, i * 128:(i + 1) * 128],
                        qhat_buf.ap()[:, t * D + h * 128:t * D + (h + 1) * 128]),
                        deps=(a2, qhat_dep), tag="qmul")
                    ops_i.append(qm)
                sums2 = []
                prevq = sq_prev
                for i in range(4):
                    ds = p.op("vector", lambda e, i=i: e.reduce_sum(
                        g1c[:, 16 + i:17 + i], stB[:, i * 128:(i + 1) * 128],
                        axis=AX.X), deps=(ops_i[i],), tag="gn2_sum")
                    dq = p.op("scalar", lambda e, i=i: e.activation(
                        sqsc[:], stB[:, i * 128:(i + 1) * 128], AF.Square,
                        accum_out=g1c[:, 20 + i:21 + i]),
                        deps=(ops_i[i],) + ((prevq,) if prevq else ()), tag="gn2_sq")
                    prevq = dq
                    sums2 += [ds, dq]
                sq_prev = prevq
                st2 = self._gn_stats(g1c, 16, sums2)
                trbs = []
                for i in range(4):
                    n2 = p.op("vector", lambda e, i=i: e.tensor_scalar(
                        stB[:, i * 128:(i + 1) * 128], stB[:, i * 128:(i + 1) * 128],
                        g1c[:, 24 + i:25 + i], g1c[:, 28 + i:29 + i],
                        ALU.subtract, ALU.mult), deps=(st2,), tag="gn2_n")
                    war = (ptrA_free,) if (i == 0) else ()
                    _, trb = self._tr_only(ptrA[:, i * 128:(i + 1) * 128],
                                           stB[:, i * 128:(i + 1) * 128],
                                           deps=(n2,) + war, bf16=False)
                    trbs.append(trb)
                gnw = colpack.ap()[:, h * NCP + 8:h * NCP + 9]
                gnb = colpack.ap()[:, h * NCP + 9:h * NCP + 10]
                evb = p.op("scalar", lambda e, ch=ch, gnw=gnw, gnb=gnb: e.activation(
                    stA[:, 0:512], ptrA[:, 0:512], AF.Identity,
                    bias=gnb, scale=gnw),
                    deps=tuple(trbs) + (colpack_dep,), tag="gn_aff")
                ptrA_free = evb
                stA_war = [evb]
                stB_war = [evb]
                pw = p.op("vector", lambda e, ch=ch, h=h, og=og: e.tensor_mul(
                    mem_buf.ap()[:, h * 1024 + ch * 512:h * 1024 + (ch + 1) * 512],
                    stA[:, 0:512], og[:, ch * 512:(ch + 1) * 512]),
                    deps=(evb, oge[ch], mf), tag="prewo")
                stA_war.append(pw)
                uch.append(pw)
            mem_free[h] = uch
            og_war[h % 2] = oge + uch
            prewo_done += uch
        ofm_done = []
        pO_read = [None, None]
        for d in range(16):
            bs = d % 2
            war = (wo_last[bs],) if wo_last[bs] else ()
            dwo = p.dma("gpsimd", lambda e, d=d, bs=bs: e.dma_start(
                out=wobuf[bs][:], in_=wo_pk[d]), deps=war, tag=f"wo{d}")
            for c in range(2):
                mm = None
                for k in range(16):
                    mm = p.op("tensor", lambda e, k=k, c=c, bs=bs: e.matmul(
                        pO[c][:, 0:512], wobuf[bs][:, k * 128:(k + 1) * 128],
                        mem_buf.ap()[:, k * 1024 + c * 512:k * 1024 + (c + 1) * 512],
                        start=(k == 0), stop=(k == 15)),
                        deps=(dwo,) + tuple(prewo_done) +
                        ((pO_read[c],) if (k == 0 and pO_read[c]) else ()),
                        tag=f"mmo{d}_{c}")
                evo = p.op("vector", lambda e, d=d, c=c: e.tensor_copy(
                    ofm.ap()[:, d * 1024 + c * 512:d * 1024 + (c + 1) * 512],
                    pO[c][:, 0:512]), deps=(mm,), tag="evo")
                pO_read[c] = evo
                ofm_done.append(evo)
            wo_last[bs] = mm
        stores = []
        xslot_free = [[], []]
        ptrB_free = None
        for t in range(8):
            sl = t % 2
            xt = xbuf.ap()[:, sl * D:(sl + 1) * D]
            dl = p.dma("sync", lambda e, t=t, xt=xt: e.dma_start(
                out=xt, in_=x_src[t * 128:(t + 1) * 128, :]),
                deps=tuple(xslot_free[sl]), tag=f"xl{t}")
            adds = []
            for half in range(2):
                trs = []
                for i in range(8):
                    d = half * 8 + i
                    war = (ptrB_free,) if (i == 0 and ptrB_free) else ()
                    _, tr = self._tr_only(ptrB[:, i * 128:(i + 1) * 128],
                                          ofm.ap()[:, d * 1024 + t * 128:
                                                   d * 1024 + (t + 1) * 128],
                                          deps=(ofm_done[2 * d + t // 4],) + war,
                                          bf16=True)
                    trs.append(tr)
                ev4 = p.op("vector", lambda e, half=half: e.tensor_copy(
                    ptrB_stage[:, half * 1024:(half + 1) * 1024], ptrB[:, 0:1024]),
                    deps=tuple(trs) + tuple(stage_war[half]), tag="oev")
                ptrB_free = ev4
                ad = p.op("vector", lambda e, xt=xt, half=half: e.tensor_add(
                    xt[:, half * 1024:(half + 1) * 1024],
                    xt[:, half * 1024:(half + 1) * 1024],
                    ptrB_stage[:, half * 1024:(half + 1) * 1024]),
                    deps=(ev4, dl), tag="oadd")
                stage_war[half] = [ad]
                adds.append(ad)
            st = p.dma("sync", lambda e, t=t, xt=xt: e.dma_start(
                out=x_dst[t * 128:(t + 1) * 128, :], in_=xt),
                deps=tuple(adds), tag=f"xs{t}")
            xslot_free[sl] = [st]
            stores.append(st)
        return stores

    def _gn_stats(self, g1c, base, dep_ops):
        p = self.p
        prevg = getattr(self, "_gn_prev", ())
        mu = p.op("vector", lambda e: e.tensor_scalar_mul(
            g1c[:, base + 8:base + 12], g1c[:, base:base + 4], 1.0 / 128),
            deps=tuple(dep_ops), tag="gn_mu")
        vv = p.op("vector", lambda e: e.tensor_scalar_mul(
            g1c[:, 32:36], g1c[:, base + 4:base + 8], 1.0 / 128),
            deps=tuple(dep_ops) + tuple(prevg), tag="gn_v")
        ms = p.op("vector", lambda e: e.tensor_mul(
            g1c[:, 36:40], g1c[:, base + 8:base + 12], g1c[:, base + 8:base + 12]),
            deps=(mu,) + tuple(prevg), tag="gn_ms")
        v2 = p.op("vector", lambda e: e.tensor_sub(
            g1c[:, 32:36], g1c[:, 32:36], g1c[:, 36:40]), deps=(vv, ms), tag="gn_v2")
        v3 = p.op("vector", lambda e: e.tensor_scalar(
            g1c[:, 32:36], g1c[:, 32:36], EPS, None, ALU.add), deps=(v2,), tag="gn_v3")
        ln_ = p.op("scalar", lambda e: e.activation(
            g1c[:, 36:40], g1c[:, 32:36], AF.Ln), deps=(v3, ms), tag="gn_ln")
        rstd = p.op("scalar", lambda e: e.activation(
            g1c[:, base + 12:base + 16], g1c[:, 36:40], AF.Exp, scale=-0.5),
            deps=(ln_,), tag="gn_rstd")
        fin = p.op("vector", lambda e: e.engine_nop(), deps=(mu, rstd), tag="gn_fin")
        self._gn_prev = (ln_, rstd)
        return fin

    def emit_l1ffn(self, x_src, x_dst, wa_pk, rg_pk, rw1_pk, rw2_pk,
                   rgb_sb, rgb_dep, rb1_sb, rb1_dep, rbc_sb, rbc_dep,
                   oh18, oh18_dep, coeff=0.5):
        p = self.p
        T = T_CORE
        n_t = 8
        xn_fm = self.sb("xnfm", [128, 16 * T], BF16)
        xnv = xn_fm.ap().rearrange("p (nd tt) -> p nd tt", nd=16)
        xbuf = self.sb("xbuf", [128, 2 * D], F32)
        xslot = [xbuf.ap()[:, 0:D], xbuf.ap()[:, D:2 * D]]
        stats = self.sb("lnstats", [128, 2 * n_t], F32)
        xn_ready, _, lnptr = self._ln_stream(x_src, xslot, xn_fm, n_t, stats)
        pS = self.ps("pS", [128, 512], F32)
        pC = pS
        ptrC = self.ps("ptrC", [128, 512], F32)
        aff = self.sb("aff", [16, T], F32)
        cfull = self.sb("cfull", [18, T], F32)
        tk = self.sb("tk", [128, 28], F32)
        ctm = self.sb("ctm", [128, 16], F32)
        wa_sb = self.sb("wa_sb", [128, 256], BF16)
        dwa = p.dma("gpsimd", lambda e: e.dma_start(out=wa_sb[:], in_=wa_pk[0]),
                    tag="wa")
        aff_evs = []
        for c in range(2):
            mm = None
            for k in range(16):
                mm = p.op("tensor", lambda e, k=k, c=c: e.matmul(
                    pS[0:16, 0:512], wa_sb[:, k * 16:(k + 1) * 16],
                    xnv[:, k, c * 512:(c + 1) * 512], start=(k == 0), stop=(k == 15)),
                    deps=(dwa,) + tuple(xn_ready) + tuple(aff_evs), tag="mm_wa")
            sg = p.op("scalar", lambda e, c=c: e.activation(
                aff[:, c * 512:(c + 1) * 512], pS[0:16, 0:512], AF.Sigmoid),
                deps=(mm,), tag="aff_sig")
            aff_evs.append(sg)
        cset = p.op("vector", lambda e: e.memset(cfull.ap()[0:18, :], 1.0),
                    tag="ones2")
        c_evs = [cset]
        trC_free = None
        ctm_war = []
        for tc in range(8):
            war = (trC_free,) if trC_free else ()
            _, tr = self._tr_only(ptrC[:, 0:16], aff.ap()[0:16, tc * 128:(tc + 1) * 128],
                                  deps=tuple(aff_evs) + war, bf16=False, npart=16)
            ev = p.op("vector", lambda e: e.tensor_copy(ctm[:], ptrC[:, 0:16]),
                      deps=(tr,) + tuple(ctm_war), tag="c_ev")
            mx1 = p.op("vector", lambda e: e.reduce_max(tk[:, 0:1], ctm[:], axis=AX.X),
                       deps=(ev,))
            ge1 = p.op("vector", lambda e: e.tensor_single_scalar(
                tk[:, 4:20], ctm[:], tk[:, 0:1], ALU.is_ge), deps=(mx1,))
            am = p.op("vector", lambda e: e.scalar_tensor_tensor(
                tk[:, 4:20], tk[:, 4:20], -1e30, ctm[:], ALU.mult, ALU.add),
                deps=(ge1,))
            mx2 = p.op("vector", lambda e: e.reduce_max(tk[:, 1:2], tk[:, 4:20],
                                                        axis=AX.X), deps=(am,))
            den = p.op("vector", lambda e: e.tensor_add(tk[:, 2:3], tk[:, 0:1],
                                                        tk[:, 1:2]), deps=(mx2,))
            rc = p.op("vector", lambda e: e.reciprocal(tk[:, 3:4], tk[:, 2:3]),
                      deps=(den,))
            ge2 = p.op("vector", lambda e: e.tensor_single_scalar(
                tk[:, 4:20], ctm[:], tk[:, 1:2], ALU.is_ge), deps=(mx2, am))
            cm = p.op("vector", lambda e: e.tensor_mul(tk[:, 4:20], tk[:, 4:20],
                                                       ctm[:]), deps=(ge2,))
            cs = p.op("vector", lambda e: e.tensor_scalar_mul(
                tk[:, 4:20], tk[:, 4:20], tk[:, 3:4]), deps=(cm, rc), tag="c_scale")
            ctm_war = [cs]
            war = (trC_free,) if trC_free else ()
            _, trb = self._tr_only(ptrC[0:16, 128:256], tk[:, 4:20], deps=(cs,) + war,
                                   bf16=False)
            evb = p.op("vector", lambda e, tc=tc: e.tensor_copy(
                cfull.ap()[0:16, tc * 128:(tc + 1) * 128], ptrC[0:16, 128:256]),
                deps=(trb, cset), tag="c_evb")
            trC_free = evb
            c_evs.append(evb)
        acc = self.sb("acc", [128, 16 * T], BF16)
        accv = acc.ap().rearrange("p (nd tt) -> p nd tt", nd=16)
        hs = self.sb("hs", [128, 8 * T], BF16)
        hsv = hs.ap().rearrange("p (nf tt) -> p nf tt", nf=8)
        cb = self.sb("cb", [128, T], BF16)
        gsc = self.sb("gsc", [128, 512], F32)
        wgbuf = [self.sb("wgb", [128, D], BF16) for _ in range(2)]
        w1buf = [self.sb("w1b", [128, D], BF16) for _ in range(2)]
        w2buf = [self.sb("w2b", [128, 1024], BF16) for _ in range(2)]
        pg = self.ps("pg", [128, 512], F32)
        pu = self.ps("pu", [128, 512], F32)
        pd = [self.ps("pdd", [128, 512], F32) for _ in range(2)]
        wg_last = [None, None]
        w1_last = [None, None]
        w2_last = [None, None]
        pg_rd = None
        pu_rd = None
        pd_rd = [None, None]
        cb_rd = []
        gsc_rd = None
        acc_done = []
        acc_last = {}
        hs_rd = []
        for e in range(18):
            routed = e < 16
            if routed:
                cbe = []
                for c in range(2):
                    mmc = p.op("tensor", lambda e_, c=c, e=e: e_.matmul(
                        pC[:, 0:512], oh18[:, e * 128:(e + 1) * 128],
                        cfull.ap()[0:18, c * 512:(c + 1) * 512], start=True, stop=True),
                        deps=tuple(c_evs) + tuple(aff_evs) + (oh18_dep,) + tuple(cb_rd),
                        tag="mm_cb")
                    evc = p.op("vector", lambda e_, c=c: e_.tensor_copy(
                        cb[:, c * 512:(c + 1) * 512], pC[:, 0:512]),
                        deps=(mmc,), tag="cb_ev")
                    cb_rd = [evc]
                    cbe.append(evc)
            hw = []
            for ft in range(8):
                bs = ft % 2
                war = (wg_last[bs],) if wg_last[bs] else ()
                dg = p.dma("gpsimd", lambda e_, e=e, ft=ft, bs=bs: e_.dma_start(
                    out=wgbuf[bs][:], in_=rg_pk[e, ft]), deps=war, tag=f"rg{e}_{ft}")
                war = (w1_last[bs],) if w1_last[bs] else ()
                d1 = p.dma("gpsimd", lambda e_, e=e, ft=ft, bs=bs: e_.dma_start(
                    out=w1buf[bs][:], in_=rw1_pk[e, ft]), deps=war, tag=f"rw1{e}_{ft}")
                for c in range(2):
                    xr = tuple(xn_ready[8 * c:8 * c + 8])
                    mmg = None
                    war = (pg_rd,) if pg_rd else ()
                    for k in range(16):
                        mmg = p.op("tensor", lambda e_, k=k, c=c, bs=bs: e_.matmul(
                            pg[:, 0:512], wgbuf[bs][:, k * 128:(k + 1) * 128],
                            xnv[:, k, c * 512:(c + 1) * 512],
                            start=(k == 0), stop=(k == 15)),
                            deps=(dg,) + xr + (war if k == 0 else ()),
                            tag="mm_g")
                    mmu = None
                    war = (pu_rd,) if pu_rd else ()
                    for k in range(16):
                        mmu = p.op("tensor", lambda e_, k=k, c=c, bs=bs: e_.matmul(
                            pu[:, 0:512], w1buf[bs][:, k * 128:(k + 1) * 128],
                            xnv[:, k, c * 512:(c + 1) * 512],
                            start=(k == 0), stop=(k == 15)),
                            deps=(d1,) + xr + (war if k == 0 else ()),
                            tag="mm_u")
                    wg_last[bs] = mmg
                    w1_last[bs] = mmu
                    gcol = rgb_sb.ap()[:, e * 8 + ft:e * 8 + ft + 1]
                    gl = p.op("scalar", lambda e_, c=c, gcol=gcol: e_.activation(
                        gsc[:], pg[:, 0:512], AF.Gelu, bias=gcol),
                        deps=(mmg, rgb_dep) + ((gsc_rd,) if gsc_rd else ()),
                        tag="gelu")
                    pg_rd = gl
                    ucol = rb1_sb.ap()[:, e * 8 + ft:e * 8 + ft + 1]
                    hm = p.op("vector", lambda e_, c=c, ft=ft, ucol=ucol:
                              e_.scalar_tensor_tensor(
                                  hsv[:, ft, c * 512:(c + 1) * 512], pu[:, 0:512],
                                  ucol, gsc[:], ALU.add, ALU.mult),
                              deps=(mmu, gl, rb1_dep) + tuple(hs_rd), tag="hmul")
                    gsc_rd = hm
                    pu_rd = hm
                    if routed:
                        hc = p.op("gpsimd", lambda e_, c=c, ft=ft: e_.tensor_mul(
                            hsv[:, ft, c * 512:(c + 1) * 512],
                            hsv[:, ft, c * 512:(c + 1) * 512],
                            cb[:, c * 512:(c + 1) * 512]),
                            deps=(hm, cbe[c]), tag="hcb")
                        hw.append(hc)
                    else:
                        hw.append(hm)
            hs_rd = []
            for d in range(16):
                bs = d % 2
                war = (w2_last[bs],) if w2_last[bs] else ()
                d2 = p.dma("gpsimd", lambda e_, e=e, d=d, bs=bs: e_.dma_start(
                    out=w2buf[bs][:], in_=rw2_pk[e, d]), deps=war, tag=f"rw2{e}_{d}")
                for c in range(2):
                    mmd = None
                    war = (pd_rd[c],) if pd_rd[c] else ()
                    for k in range(8):
                        mmd = p.op("tensor", lambda e_, k=k, c=c, bs=bs: e_.matmul(
                            pd[c][:, 0:512], w2buf[bs][:, k * 128:(k + 1) * 128],
                            hsv[:, k, c * 512:(c + 1) * 512],
                            start=(k == 0), stop=(k == 7)),
                            deps=(d2,) + tuple(hw) + (war if k == 0 else ()),
                            tag="mm_d")
                    w2_last[bs] = mmd
                    hs_rd.append(mmd)
                    prev = acc_last.get((d, c))
                    if e == 0:
                        ad = p.op("vector", lambda e_, d=d, c=c: e_.tensor_copy(
                            accv[:, d, c * 512:(c + 1) * 512], pd[c][:, 0:512]),
                            deps=(mmd,), tag="acc0")
                    else:
                        ad = p.op("vector", lambda e_, d=d, c=c: e_.tensor_add(
                            accv[:, d, c * 512:(c + 1) * 512],
                            accv[:, d, c * 512:(c + 1) * 512], pd[c][:, 0:512]),
                            deps=(mmd, prev), tag="accA")
                    pd_rd[c] = ad
                    acc_last[(d, c)] = ad
        for d in range(16):
            for c in range(2):
                war = (pd_rd[c],) if pd_rd[c] else ()
                mmb = p.op("tensor", lambda e_, d=d, c=c: e_.matmul(
                    pd[c][:, 0:512], rbc_sb.ap()[0:18, d * 128:(d + 1) * 128],
                    cfull.ap()[0:18, c * 512:(c + 1) * 512], start=True, stop=True),
                    deps=tuple(c_evs) + (rbc_dep,) + war, tag="mm_rbc")
                ad = p.op("vector", lambda e_, d=d, c=c: e_.tensor_add(
                    accv[:, d, c * 512:(c + 1) * 512],
                    accv[:, d, c * 512:(c + 1) * 512], pd[c][:, 0:512]),
                    deps=(mmb, acc_last[(d, c)]), tag="accB")
                pd_rd[c] = ad
                acc_done.append(ad)
        ptrB = lnptr
        stg = self.sb("stg", [128, 2048], BF16)
        xn_scr = self.sb("xnscr", [128, D], F32)
        stores = []
        xslot_free = [list(xn_ready[-2:]), list(xn_ready[-2:])]
        stage_war = [[], []]
        ptrB_free = None
        xn_scr_rd = None
        for t in range(8):
            sl = t % 2
            xt = xslot[sl]
            dl = p.dma("sync", lambda e, t=t, xt=xt: e.dma_start(
                out=xt, in_=x_src[t * 128:(t + 1) * 128, :]),
                deps=tuple(xslot_free[sl]) + tuple(xn_ready), tag=f"xl2_{t}")
            nrm = p.op("vector", lambda e, xt=xt, t=t: e.tensor_scalar(
                xn_scr[:], xt[:], stats[:, 2 * t:2 * t + 1],
                stats[:, 2 * t + 1:2 * t + 2], ALU.subtract, ALU.mult),
                deps=(dl,) + ((xn_scr_rd,) if xn_scr_rd else ()), tag="nrm2")
            ax = p.op("vector", lambda e, xt=xt: e.scalar_tensor_tensor(
                xt[:], xn_scr[:], coeff, xt[:], ALU.mult, ALU.add),
                deps=(nrm,), tag="xn_add")
            xn_scr_rd = ax
            adds = [ax]
            for half in range(2):
                trs = []
                for i in range(8):
                    d = half * 8 + i
                    war = (ptrB_free,) if (i == 0 and ptrB_free) else ()
                    _, tr = self._tr_only(ptrB[:, i * 128:(i + 1) * 128],
                                          accv[:, d, t * 128:(t + 1) * 128],
                                          deps=tuple(acc_done) + war, bf16=True)
                    trs.append(tr)
                ev4 = p.op("vector", lambda e, half=half: e.tensor_copy(
                    stg[:, half * 1024:(half + 1) * 1024], ptrB[:, 0:1024]),
                    deps=tuple(trs) + tuple(stage_war[half]), tag="oev2")
                ptrB_free = ev4
                ad = p.op("vector", lambda e, xt=xt, half=half: e.scalar_tensor_tensor(
                    xt[:, half * 1024:(half + 1) * 1024],
                    stg[:, half * 1024:(half + 1) * 1024], coeff,
                    xt[:, half * 1024:(half + 1) * 1024], ALU.mult, ALU.add),
                    deps=(ev4, ax), tag="oadd2")
                stage_war[half] = [ad]
                adds.append(ad)
            st = p.dma("sync", lambda e, t=t, xt=xt: e.dma_start(
                out=x_dst[t * 128:(t + 1) * 128, :], in_=xt),
                deps=tuple(adds), tag=f"xs2_{t}")
            xslot_free[sl] = [st]
            stores.append(st)
        return stores

    def _ln_stream(self, x_src, xslot, xn_fm, n_t, stats_out):
        p = self.p
        scratch = self.sb("lnscr", [128, D], F32)
        sums_all = self.sb("lnsum", [128, 4 * n_t], F32)
        xn_tm = self.sb("xntm", [128, D], BF16)
        ptr = self.ps("lntr", [128, 1024], BF16)
        xnv = xn_fm.ap().rearrange("p (nd tt) -> p nd tt", nd=16)
        ready = []
        prev_sq = None
        prev_evs = []
        prev_half_ev = None
        xslot_readers = [[], []]
        for t in range(n_t):
            xt = xslot[t % 2]
            dl = p.dma("sync", lambda e, t=t, xt=xt: e.dma_start(
                out=xt, in_=x_src[t * 128:(t + 1) * 128, :]),
                deps=tuple(xslot_readers[t % 2]), tag=f"lnx{t}")
            sums = sums_all.ap()[:, 4 * t:4 * t + 4]
            dsum = p.op("vector", lambda e, xt=xt, sums=sums: e.reduce_sum(
                sums[:, 0:1], xt[:], axis=AX.X), deps=(dl,))
            dsq = p.op("scalar", lambda e, xt=xt, sums=sums: e.activation(
                scratch[:], xt[:], AF.Square, accum_out=sums[:, 1:2]),
                deps=[dl] + ([prev_sq] if prev_sq else []))
            prev_sq = dsq
            mu = p.op("vector", lambda e, t=t, sums=sums: e.tensor_scalar_mul(
                stats_out[:, 2 * t:2 * t + 1], sums[:, 0:1], 1.0 / D), deps=(dsum,))
            musq = p.op("vector", lambda e, sums=sums, t=t: e.tensor_mul(
                sums[:, 3:4], stats_out[:, 2 * t:2 * t + 1],
                stats_out[:, 2 * t:2 * t + 1]), deps=(mu,))
            var = p.op("vector", lambda e, sums=sums: e.tensor_scalar(
                sums[:, 2:3], sums[:, 1:2], 1.0 / D, None, ALU.mult), deps=(dsq,))
            var2 = p.op("vector", lambda e, sums=sums: e.tensor_scalar(
                sums[:, 2:3], sums[:, 2:3], sums[:, 3:4], EPS, ALU.subtract, ALU.add),
                deps=(var, musq))
            ln_ = p.op("scalar", lambda e, sums=sums: e.activation(
                sums[:, 3:4], sums[:, 2:3], AF.Ln), deps=(var2, musq))
            rstd = p.op("scalar", lambda e, t=t, sums=sums: e.activation(
                stats_out[:, 2 * t + 1:2 * t + 2], sums[:, 3:4], AF.Exp, scale=-0.5),
                deps=(ln_,))
            nrm = p.op("vector", lambda e, xt=xt, t=t: e.tensor_scalar(
                xn_tm[:], xt[:], stats_out[:, 2 * t:2 * t + 1],
                stats_out[:, 2 * t + 1:2 * t + 2], ALU.subtract, ALU.mult),
                deps=[mu, rstd] + prev_evs, tag="ln_nrm")
            xslot_readers[t % 2] = [nrm, dsq, dsum]
            prev_evs = []
            for half in range(2):
                trs = []
                for i in range(8):
                    dti = half * 8 + i
                    dps = (nrm,) if (half == 0 and t == 0) else (nrm, prev_half_ev)
                    _, tr = self._tr_only(ptr[:, i * 128:(i + 1) * 128],
                                          xn_tm[:, dti * 128:(dti + 1) * 128],
                                          deps=[dd for dd in dps if dd is not None])
                    trs.append(tr)
                ev = p.op("vector", lambda e, half=half, t=t: e.tensor_copy(
                    xnv[:, half * 8:(half + 1) * 8, t * 128:(t + 1) * 128],
                    ptr[:, 0:1024]), deps=tuple(trs), tag="ln_trev")
                prev_half_ev = ev
                prev_evs.append(ev)
                ready.append(ev)
        return ready, stats_out, ptr

    def emit_l1attn_C(self, x_src, wkv_pk, wcmp_pk, lgam, lgam_dep, lbet,
                      lbet_dep, ch_out):
        p = self.p
        T = T_CORE
        xn_fm = self.sb("xnfm", [128, 16 * T], BF16)
        xnv = xn_fm.ap().rearrange("p (nd tt) -> p nd tt", nd=16)
        xbuf = self.sb("xbuf", [128, 2 * D], F32)
        xslot = [xbuf.ap()[:, 0:D], xbuf.ap()[:, D:2 * D]]
        stats = self.sb("lnstats", [128, 2 * 8], F32)
        xn_ready, _, lnptr = self._ln_stream(x_src, xslot, xn_fm, 8, stats)
        kvl = self.sb("kvl", [128, 4 * T], BF16)
        wkvbuf = [self.sb("wkvb", [128, D], BF16) for _ in range(2)]
        pA = self.ps("pA", [128, 512], F32)
        pB = self.ps("pB", [128, 512], F32)
        kvl_evs = []
        wkv_last = [None, None]
        pA_rd = None
        for o in range(4):
            bs = o % 2
            war = (wkv_last[bs],) if wkv_last[bs] else ()
            dw = p.dma("gpsimd", lambda e, o=o, bs=bs: e.dma_start(
                out=wkvbuf[bs][:], in_=wkv_pk[o]), deps=war, tag=f"wkv{o}")
            for c in range(2):
                mm = None
                war = (pA_rd,) if pA_rd else ()
                for k in range(16):
                    mm = p.op("tensor", lambda e, k=k, c=c, bs=bs: e.matmul(
                        pA[:, 0:512], wkvbuf[bs][:, k * 128:(k + 1) * 128],
                        xnv[:, k, c * 512:(c + 1) * 512],
                        start=(k == 0), stop=(k == 15)),
                        deps=(dw,) + tuple(xn_ready) + (war if k == 0 else ()),
                        tag="mm_kvl")
                wkv_last[bs] = mm
                ev = p.op("vector", lambda e, o=o, c=c: e.tensor_copy(
                    kvl.ap()[:, o * T + c * 512:o * T + (c + 1) * 512], pA[:, 0:512]),
                    deps=(mm,), tag="ev_kvl")
                pA_rd = ev
                kvl_evs.append(ev)
        cf = self.sb("cf", [128, 2 * T], F32)
        wcbuf = self.sb("wcb", [128, 2 * 512], BF16)
        dwc = p.dma("gpsimd", lambda e: e.dma_start(
            out=wcbuf.ap().rearrange("p (a b) -> p a b", a=2),
            in_=wcmp_pk[0:2].rearrange("a p b -> p a b")),
            tag="wcmp")
        pB_rd = None
        cf_evs = []
        for o in range(2):
            for c in range(2):
                mm = None
                war = (pB_rd,) if pB_rd else ()
                for k in range(4):
                    mm = p.op("tensor", lambda e, k=k, c=c, o=o: e.matmul(
                        pB[:, 0:512], wcbuf[:, o * 512 + k * 128:o * 512 + (k + 1) * 128],
                        kvl.ap()[:, k * T + c * 512:k * T + (c + 1) * 512],
                        start=(k == 0), stop=(k == 3)),
                        deps=(dwc,) + tuple(kvl_evs) + (war if k == 0 else ()),
                        tag="mm_cf")
                ev = p.op("vector", lambda e, o=o, c=c: e.tensor_copy(
                    cf.ap()[:, o * T + c * 512:o * T + (c + 1) * 512], pB[:, 0:512]),
                    deps=(mm,), tag="ev_cf")
                pB_rd = ev
                cf_evs.append(ev)
        ptrF = self.ps("ptrF", [128, 512], F32)
        ctm = self.sb("ctm", [128, 256], F32)
        lsum = self.sb("lsum", [128, 8], F32)
        sqs2 = self.sb("sqs2", [128, 256], F32)
        chv = ch_out.ap().rearrange("p (o tt) -> p o tt", o=2)
        ready = []
        ptr_free = None
        ctm_war = []
        sq_prev2 = None
        for tc in range(8):
            trs = []
            for o in range(2):
                war = (ptr_free,) if (o == 0 and ptr_free) else ()
                _, tr = self._tr_only(
                    ptrF[:, o * 128:(o + 1) * 128],
                    cf.ap()[:, o * T + tc * 128:o * T + (tc + 1) * 128],
                    deps=(cf_evs[o * 2 + tc // 4],) + war, bf16=False)
                trs.append(tr)
            ev = p.op("vector", lambda e: e.tensor_copy(ctm[:], ptrF[:, 0:256]),
                      deps=tuple(trs) + tuple(ctm_war), tag="cln_ev")
            ptr_free = ev
            dsum = p.op("vector", lambda e: e.reduce_sum(lsum[:, 0:1], ctm[:],
                                                         axis=AX.X), deps=(ev,))
            dsq = p.op("scalar", lambda e: e.activation(
                sqs2[:], ctm[:], AF.Square, accum_out=lsum[:, 1:2]),
                deps=(ev,) + ((sq_prev2,) if sq_prev2 else ()))
            sq_prev2 = dsq
            mu = p.op("vector", lambda e: e.tensor_scalar_mul(
                lsum[:, 2:3], lsum[:, 0:1], 1.0 / 256), deps=(dsum,))
            musq = p.op("vector", lambda e: e.tensor_mul(
                lsum[:, 3:4], lsum[:, 2:3], lsum[:, 2:3]), deps=(mu,))
            var = p.op("vector", lambda e: e.tensor_scalar(
                lsum[:, 4:5], lsum[:, 1:2], 1.0 / 256, None, ALU.mult), deps=(dsq,))
            var2 = p.op("vector", lambda e: e.tensor_scalar(
                lsum[:, 4:5], lsum[:, 4:5], lsum[:, 3:4], EPS, ALU.subtract, ALU.add),
                deps=(var, musq))
            ln_ = p.op("scalar", lambda e: e.activation(
                lsum[:, 5:6], lsum[:, 4:5], AF.Ln), deps=(var2, musq))
            rstd = p.op("scalar", lambda e: e.activation(
                lsum[:, 6:7], lsum[:, 5:6], AF.Exp, scale=-0.5), deps=(ln_,))
            nrm = p.op("vector", lambda e: e.tensor_scalar(
                ctm[:], ctm[:], lsum[:, 2:3], lsum[:, 6:7], ALU.subtract, ALU.mult),
                deps=(mu, rstd, dsq), tag="cln_nrm")
            af1 = p.op("vector", lambda e: e.tensor_mul(ctm[:], ctm[:], lgam[:]),
                       deps=(nrm, lgam_dep))
            af2 = p.op("vector", lambda e: e.tensor_add(ctm[:], ctm[:], lbet[:]),
                       deps=(af1, lbet_dep))
            trbs = []
            for o in range(2):
                war = (ptr_free,) if (o == 0) else ()
                _, trb = self._tr_only(ptrF[:, o * 128:(o + 1) * 128],
                                       ctm[:, o * 128:(o + 1) * 128],
                                       deps=(af2,) + war, bf16=False)
                trbs.append(trb)
            evb = p.op("vector", lambda e, tc=tc: e.tensor_copy(
                chv[:, :, tc * 128:(tc + 1) * 128],
                ptrF[:, 0:256].rearrange("p (o f) -> p o f", o=2)),
                deps=tuple(trbs), tag="cln_evb")
            ptr_free = evb
            ctm_war = [evb]
            ready.append(evb)
        return ready

    def emit_l1attn_D(self, x_src, x_dst, ch_own, ch_dep, chall_dram,
                      wq_pk, wkvc_pk, w2l_pk, wtk_pk, wor_pk,
                      cosq, cosq_dep, sinq, sinq_dep, cosk_dram, sink_dram,
                      acol, acol_dep, bcol, bcol_dep, ut, ut_dep,
                      bout, bout_dep):
        p = self.p
        T = T_CORE
        pQ = self.ps("pQ", [128, 512], F32)
        pS = self.ps("pS", [128, 512], F32)
        pPV = self.ps("pPV", [128, 512], F32)
        ptrV = self.ps("ptrV", [128, 1024], BF16)
        ptrP = self.ps("ptrP", [128, 512], BF16)
        ptrF = self.ps("ptrF", [128, 512], F32)
        qhat = self.sb("qhat", [128, 2 * T], BF16)
        qf = self.sb("qf", [128, 2 * T], F32)
        rsc = self.sb("rsc", [128, 1024], F32)
        kf = self.sb("kf", [128, 1024], F32)
        vf = self.sb("vf", [128, 1024], F32)
        kb = self.sb("kb", [128, 1024], BF16)
        vT = self.sb("vT", [128, 1024], BF16)
        chv = [self.sb("chv", [128, 1024], BF16) for _ in range(2)]
        ckb = self.sb("ckb", [128, 2048], F32)
        oacc = self.sb("oacc", [128, 8 * 256], F32)
        ml = self.sb("ml", [128, 8 * 8], F32)
        pb = self.sb("pb", [128, 512], BF16)
        wq_sb = self.sb("wq_sb", [128, 2 * 256], BF16)
        wkvc_sb = self.sb("wkvc", [128, 4 * 256], BF16)
        w2l_sb = self.sb("w2l", [128, 4 * 256], BF16)
        obuf = self.sb("obuf", [128, 16 * T], BF16)
        zb = self.sb("zb", [128, 4 * T], BF16)
        wtkbuf = [self.sb("wtkb", [128, 512], BF16) for _ in range(2)]
        worbuf = [self.sb("worb", [128, 512], BF16) for _ in range(2)]
        xbuf = self.sb("xbuf", [128, 2 * D], F32)
        stg = self.sb("stg", [128, 2048], BF16)
        ptrB = ptrV

        dwq = p.dma("gpsimd", lambda e: e.dma_start(
            out=wq_sb.ap().rearrange("p (a b) -> p a b", a=2),
            in_=wq_pk[0:2].rearrange("a p b -> p a b")),
            tag="wq")
        dkvc = p.dma("gpsimd", lambda e: e.dma_start(
            out=wkvc_sb.ap().rearrange("p (a b) -> p a b", a=4),
            in_=wkvc_pk[0:4].rearrange("a p b -> p a b")),
            tag="wkvc")
        d2l = p.dma("gpsimd", lambda e: e.dma_start(
            out=w2l_sb.ap().rearrange("p (a b) -> p a b", a=4),
            in_=w2l_pk[0:4].rearrange("a p b -> p a b")),
            tag="w2l")
        qevs = []
        pQ_rd = None
        for o in range(2):
            for c in range(2):
                mm = None
                war = (pQ_rd,) if pQ_rd else ()
                for k in range(2):
                    mm = p.op("tensor", lambda e, o=o, c=c, k=k: e.matmul(
                        pQ[:, 0:512], wq_sb[:, o * 256 + k * 128:o * 256 + (k + 1) * 128],
                        ch_own.ap()[:, k * T + c * 512:k * T + (c + 1) * 512],
                        start=(k == 0), stop=(k == 1)),
                        deps=(dwq, ch_dep) + (war if k == 0 else ()), tag="mm_q")
                ev = p.op("vector", lambda e, o=o, c=c: e.tensor_copy(
                    qf.ap()[:, o * T + c * 512:o * T + (c + 1) * 512], pQ[:, 0:512]),
                    deps=(mm,), tag="ev_q")
                pQ_rd = ev
                qevs.append(ev)
        rope_q = []
        rq_war = []
        for c in range(2):
            sl = slice(c * 512, (c + 1) * 512)
            q0 = qf.ap()[:, c * 512:(c + 1) * 512]
            q1 = qf.ap()[:, T + c * 512:T + (c + 1) * 512]
            c0 = cosq.ap()[:, c * 512:(c + 1) * 512]
            c1 = cosq.ap()[:, T + c * 512:T + (c + 1) * 512]
            s0 = sinq.ap()[:, c * 512:(c + 1) * 512]
            s1 = sinq.ap()[:, T + c * 512:T + (c + 1) * 512]
            t1 = p.op("vector", lambda e, q0=q0, c0=c0, c=c: e.tensor_mul(
                rsc[:, 0:512], q0, c0), deps=tuple(qevs) + (cosq_dep,) + tuple(rq_war))
            t2 = p.op("vector", lambda e, q1=q1, s0=s0: e.tensor_mul(
                rsc[:, 512:1024], q1, s0), deps=tuple(qevs) + (sinq_dep,) + tuple(rq_war))
            h0 = p.op("vector", lambda e, c=c: e.tensor_sub(
                qhat.ap()[:, c * 512:(c + 1) * 512], rsc[:, 0:512], rsc[:, 512:1024]),
                deps=(t1, t2), tag="ropeq0")
            t3 = p.op("vector", lambda e, q1=q1, c1=c1: e.tensor_mul(
                rsc[:, 0:512], q1, c1), deps=(h0, t1) + (cosq_dep,))
            t4 = p.op("vector", lambda e, q0=q0, s1=s1: e.tensor_mul(
                rsc[:, 512:1024], q0, s1), deps=(h0, t2) + (sinq_dep,))
            h1 = p.op("vector", lambda e, c=c: e.tensor_add(
                qhat.ap()[:, T + c * 512:T + (c + 1) * 512], rsc[:, 0:512],
                rsc[:, 512:1024]), deps=(t3, t4), tag="ropeq1")
            rq_war = [h1]
            rope_q += [h0, h1]
        import kernel as _K
        if getattr(_K, "L1D_DBG", None):
            dbg = _K.L1D_DBG
            p.dma("sync", lambda e: e.dma_start(out=dbg["qhat"], in_=qhat[:]),
                  deps=tuple(rope_q), tag="dbg_qh")
        iacc = p.op("vector", lambda e: e.memset(oacc[:], 0.0), tag="o0")
        im = p.op("vector", lambda e: e.memset(ml[:], 0.0), tag="ml0")
        im2 = p.op("vector", lambda e: e.tensor_scalar(
            ml.ap().rearrange("p (qt c) -> p qt c", qt=8)[:, :, 0:1],
            ml.ap().rearrange("p (qt c) -> p qt c", qt=8)[:, :, 0:1],
            -1e30, None, ALU.add), deps=(im,), tag="mneg")
        mlv = ml.ap().rearrange("p (qt c) -> p qt c", qt=8)
        chv_war = [[], []]
        kv_evs = []
        vT_evac_war = []
        vtr_free = None
        ptrP_free = None
        stg_war = []
        pS_rd = {}
        pPV_rd = {}
        ml_prev = {}
        o_prev = {}
        ck_war = []
        kb_war = []
        vT_war = []
        pb_war = []
        obuf_evs = []
        for c16 in range(16):
            j, lc = c16 // 2, c16 % 2
            cs = c16 % 2
            dch = p.dma("sync", lambda e, j=j, lc=lc, cs=cs: e.dma_start(
                out=chv[cs].ap().rearrange("p (o f) -> p o f", o=2),
                in_=chall_dram[j, :, :].rearrange("p (o tt) -> p o tt", o=2)
                [:, :, lc * 512:(lc + 1) * 512]),
                deps=tuple(chv_war[cs]), tag=f"chl{c16}")
            dck = p.dma("sync", lambda e, c16=c16: e.dma_start(
                out=ckb.ap().rearrange("p (a f) -> p a f", a=4)[:, 0:2, :],
                in_=cosk_dram.rearrange("p (a tt) -> p a tt", a=2)
                [:, :, c16 * 512:(c16 + 1) * 512]),
                deps=tuple(ck_war), tag=f"ck{c16}")
            dsk = p.dma("sync", lambda e, c16=c16: e.dma_start(
                out=ckb.ap().rearrange("p (a f) -> p a f", a=4)[:, 2:4, :],
                in_=sink_dram.rearrange("p (a tt) -> p a tt", a=2)
                [:, :, c16 * 512:(c16 + 1) * 512]),
                deps=tuple(ck_war), tag=f"sk{c16}")
            kvev = {}
            for o4 in range(4):
                mm = None
                dst = kf if o4 < 2 else vf
                do = o4 % 2
                war = tuple(kv_evs[-1:]) if kv_evs else ((pQ_rd,) if pQ_rd else ())
                for k in range(2):
                    mm = p.op("tensor", lambda e, o4=o4, k=k, cs=cs: e.matmul(
                        pQ[:, 0:512],
                        wkvc_sb[:, o4 * 256 + k * 128:o4 * 256 + (k + 1) * 128],
                        chv[cs].ap()[:, k * 512:(k + 1) * 512],
                        start=(k == 0), stop=(k == 1)),
                        deps=(dkvc, dch) + (war if k == 0 else ()), tag="mm_kv")
                ev = p.op("vector", lambda e, dst=dst, do=do: e.tensor_copy(
                    dst[:, do * 512:(do + 1) * 512], pQ[:, 0:512]),
                    deps=(mm,) + tuple(kb_war if o4 < 2 else vT_war), tag="ev_kv")
                kv_evs.append(ev)
                kvev[o4] = ev
            chv_war[cs] = [kvev[3]]
            rk = []
            for do in range(2):
                k0 = kf.ap()[:, 0:512]
                k1 = kf.ap()[:, 512:1024]
                cc = ckb.ap()[:, do * 512:(do + 1) * 512]
                ss = ckb.ap()[:, 1024 + do * 512:1024 + (do + 1) * 512]
                src1 = k0 if do == 0 else k1
                src2 = k1 if do == 0 else k0
                t1 = p.op("vector", lambda e, src1=src1, cc=cc: e.tensor_mul(
                    rsc[:, 0:512], src1, cc),
                    deps=(kvev[0], kvev[1], dck) + tuple(rq_war), tag="rk1")
                t2 = p.op("vector", lambda e, src2=src2, ss=ss: e.tensor_mul(
                    rsc[:, 512:1024], src2, ss),
                    deps=(kvev[0], kvev[1], dsk), tag="rk2")
                hh = p.op("vector", lambda e, do=do: (
                    e.tensor_sub(kb[:, 0:512], rsc[:, 0:512], rsc[:, 512:1024])
                    if do == 0 else
                    e.tensor_add(kb[:, 512:1024], rsc[:, 0:512], rsc[:, 512:1024])),
                    deps=(t1, t2), tag="ropek")
                rq_war = [hh]
                rk.append(hh)
            kb_war = rk
            ck_war = [rk[0], rk[1]]
            vb = p.op("vector", lambda e: e.tensor_copy(vT[:, 0:1024], vf[:]),
                      deps=(kvev[2], kvev[3]) + tuple(vT_war), tag="vcast")
            trs = []
            first = True
            for dtile in range(2):
                for ssub in range(4):
                    slot = ssub * 2 + dtile
                    war = ((vtr_free,) if (first and vtr_free) else ())
                    first = False
                    _, tr = self._tr_only(ptrV[:, slot * 128:(slot + 1) * 128],
                                          vT[:, dtile * 512 + ssub * 128:
                                             dtile * 512 + (ssub + 1) * 128],
                                          deps=(vb,) + war, bf16=True)
                    trs.append(tr)
            evV = p.op("vector", lambda e: e.tensor_copy(vT[:], ptrV[:, 0:1024]),
                       deps=tuple(trs), tag="evV")
            vT_war = [evV]
            vtr_free = evV
            for qt in range(8):
                mc = ml.ap()[:, qt * 8:qt * 8 + 8]
                acolv = acol.ap()[:, qt * 16 + c16:qt * 16 + c16 + 1]
                bcolv = bcol.ap()[:, qt * 16 + c16:qt * 16 + c16 + 1]
                mm = None
                war = (pS_rd.get(0),)
                for k in range(2):
                    mm = p.op("tensor", lambda e, qt=qt, k=k: e.matmul(
                        pS[:, 0:512],
                        qhat.ap()[:, k * T + qt * 128:k * T + (qt + 1) * 128],
                        kb[:, k * 512:(k + 1) * 512],
                        start=(k == 0), stop=(k == 1)),
                        deps=tuple(rope_q) + tuple(rk) +
                        ((war if war[0] else ()) if k == 0 else ()), tag="mm_s")
                msk = p.op("vector", lambda e, qt=qt, bcolv=bcolv: e.scalar_tensor_tensor(
                    pS[:, 0:512], ut.ap()[:, (qt % 4) * 512:(qt % 4 + 1) * 512],
                    bcolv, pS[:, 0:512], ALU.mult, ALU.add),
                    deps=(mm, ut_dep, bcol_dep), tag="mask")
                mx = p.op("vector", lambda e, mc=mc: e.reduce_max(
                    mc[:, 2:3], pS[:, 0:512], axis=AX.X), deps=(msk,))
                cnd = p.op("vector", lambda e, mc=mc, acolv=acolv: e.tensor_scalar(
                    mc[:, 2:3], mc[:, 2:3], 1.0 / 16, acolv, ALU.mult, ALU.add),
                    deps=(mx, acol_dep, im2) + tuple(ml_prev.get(qt, ())))
                mnw = p.op("vector", lambda e, mc=mc: e.tensor_max(
                    mc[:, 6:7], mc[:, 0:1], mc[:, 2:3]), deps=(cnd,))
                dif = p.op("vector", lambda e, mc=mc: e.tensor_sub(
                    mc[:, 4:5], mc[:, 0:1], mc[:, 6:7]), deps=(mnw,))
                rsce = p.op("scalar", lambda e, mc=mc: e.activation(
                    mc[:, 3:4], mc[:, 4:5], AF.Exp), deps=(dif,))
                pbias = p.op("vector", lambda e, mc=mc, acolv=acolv: e.tensor_scalar(
                    mc[:, 7:8], mc[:, 6:7], -1.0, acolv, ALU.mult, ALU.add),
                    deps=(mnw, acol_dep))
                pexp = p.op("scalar", lambda e, mc=mc: e.activation(
                    pb[:], pS[:, 0:512], AF.Exp, bias=mc[:, 7:8], scale=1.0 / 16,
                    accum_out=mc[:, 5:6]),
                    deps=(pbias, msk) + tuple(pb_war), tag="pexp")
                pS_rd[0] = pexp
                lup = p.op("vector", lambda e, mc=mc: e.scalar_tensor_tensor(
                    mc[:, 1:2], mc[:, 1:2], mc[:, 3:4], mc[:, 5:6],
                    ALU.mult, ALU.add), deps=(pexp, rsce), tag="lup")
                mup = p.op("vector", lambda e, mc=mc: e.tensor_copy(
                    mc[:, 0:1], mc[:, 6:7]), deps=(dif, pbias), tag="mup")
                ml_prev[qt] = (lup, mup)
                if c16 == 0 and qt == 0 and getattr(_K, "L1D_DBG", None):
                    _d0 = p.dma("sync", lambda e: e.dma_start(
                        out=_K.L1D_DBG["ml"][:, 0:8], in_=ml[:, 0:8]),
                        deps=(lup, mup, cnd, mx), tag="dbg_ml0")
                    ml_prev[qt] = (lup, mup, _d0)
                osc = p.op("vector", lambda e, qt=qt, mc=mc: e.tensor_scalar_mul(
                    oacc.ap()[:, qt * 256:(qt + 1) * 256],
                    oacc.ap()[:, qt * 256:(qt + 1) * 256], mc[:, 3:4]),
                    deps=(rsce, iacc) + tuple(o_prev.get(qt, ())), tag="osc")
                trsp = []
                for i in range(4):
                    war = (ptrP_free,) if (i == 0 and ptrP_free) else ()
                    _, tr = self._tr_only(ptrP[:, i * 128:(i + 1) * 128],
                                          pb[:, i * 128:(i + 1) * 128],
                                          deps=(pexp,) + war, bf16=True)
                    trsp.append(tr)
                evp = p.op("vector", lambda e: e.tensor_copy(stg[:, 0:512],
                           ptrP[:, 0:512]), deps=tuple(trsp) + tuple(stg_war),
                           tag="evp")
                ptrP_free = evp
                pb_war = [evp]
                mmv = None
                for ssub in range(4):
                    mmv = p.op("tensor", lambda e, ssub=ssub: e.matmul(
                        pPV[:, 0:256], stg[:, ssub * 128:(ssub + 1) * 128],
                        vT[:, ssub * 256:(ssub + 1) * 256],
                        start=(ssub == 0), stop=(ssub == 3)),
                        deps=(evp, evV) + ((pPV_rd.get(0),) if (ssub == 0 and pPV_rd.get(0)) else ()),
                        tag="mm_pv")
                oup = p.op("vector", lambda e, qt=qt: e.tensor_add(
                    oacc.ap()[:, qt * 256:(qt + 1) * 256],
                    oacc.ap()[:, qt * 256:(qt + 1) * 256], pPV[:, 0:256]),
                    deps=(mmv, osc), tag="oup")
                pPV_rd[0] = oup
                o_prev[qt] = (oup,)
                stg_war = [mmv]
        znorm = []
        for qt in range(8):
            mc = ml.ap()[:, qt * 8:qt * 8 + 8]
            rcp = p.op("vector", lambda e, mc=mc: e.reciprocal(
                    mc[:, 1:2], mc[:, 1:2]),
                    deps=tuple(ml_prev[qt]) + tuple(o_prev[qt]), tag="rcp")
            on = p.op("vector", lambda e, qt=qt, mc=mc: e.tensor_scalar_mul(
                oacc.ap()[:, qt * 256:(qt + 1) * 256],
                oacc.ap()[:, qt * 256:(qt + 1) * 256], mc[:, 1:2]),
                deps=(rcp,), tag="onorm")
            znorm.append(on)
        if getattr(_K, "L1D_DBG", None) and getattr(_K, "L1D_DBG_END", True):
            dbg = _K.L1D_DBG
            dago = p.dma("sync", lambda e: e.dma_start(out=dbg["oacc"], in_=oacc[:]),
                  deps=tuple(znorm), tag="dbg_oacc")
            dagm = p.dma("sync", lambda e: e.dma_start(out=dbg["ml"], in_=ml[:]),
                  deps=tuple(znorm), tag="dbg_ml")
            dagk = p.dma("sync", lambda e: e.dma_start(out=dbg["kb"], in_=kb[:]),
                  deps=tuple(rk), tag="dbg_kb")
            znorm = znorm + [dago, dagm, dagk]
        of = self.sb("ofb", [128, 2 * T_CORE], BF16)
        of_evs = []
        ptrF_free = None
        for qt in range(8):
            trs = []
            for dtile in range(2):
                war = (ptrF_free,) if (dtile == 0 and ptrF_free) else ()
                _, tr = self._tr_only(ptrF[:, dtile * 128:(dtile + 1) * 128],
                                      oacc.ap()[:, qt * 256 + dtile * 128:
                                                qt * 256 + (dtile + 1) * 128],
                                      deps=(znorm[qt],) + war, bf16=False)
                trs.append(tr)
            evo = p.op("vector", lambda e, qt=qt: e.tensor_copy(
                of.ap().rearrange("p (o tt) -> p o tt", o=2)[:, :, qt * 128:(qt + 1) * 128],
                ptrF[:, 0:256].rearrange("p (o f) -> p o f", o=2)),
                deps=tuple(trs) + tuple(qevs) + tuple(rope_q), tag="evof")
            ptrF_free = evo
            of_evs.append(evo)
        zevs = []
        pz_rd = kv_evs[-1]
        for zo in range(4):
            for c in range(2):
                mm = None
                war = (pz_rd,) if pz_rd else ()
                for k in range(2):
                    mm = p.op("tensor", lambda e, zo=zo, c=c, k=k: e.matmul(
                        pQ[:, 0:512],
                        w2l_sb[:, zo * 256 + k * 128:zo * 256 + (k + 1) * 128],
                        of.ap()[:, k * T + c * 512:k * T + (c + 1) * 512],
                        start=(k == 0), stop=(k == 1)),
                        deps=(d2l,) + tuple(of_evs) + (war if k == 0 else ()),
                        tag="mm_z")
                ev = p.op("vector", lambda e, zo=zo, c=c: e.tensor_copy(
                    zb.ap()[:, zo * T + c * 512:zo * T + (c + 1) * 512], pQ[:, 0:512]),
                    deps=(mm,), tag="ev_z")
                pz_rd = ev
                zevs.append(ev)
        wtk_last = [None, None]
        pS_last_read = pS_rd.get(0)
        for d in range(16):
            bs = d % 2
            war = tuple(x for x in (wtk_last[bs],) if x)
            dtk = p.dma("gpsimd", lambda e, d=d, bs=bs: e.dma_start(
                out=wtkbuf[bs][:], in_=wtk_pk[d]), deps=war, tag=f"wtk{d}")
            dor = p.dma("gpsimd", lambda e, d=d, bs=bs: e.dma_start(
                out=worbuf[bs][:], in_=wor_pk[d]), deps=war, tag=f"wor{d}")
            for c in range(2):
                mm = None
                war2 = (pS_last_read,) if pS_last_read else ()
                for k in range(4):
                    mm = p.op("tensor", lambda e, k=k, c=c, bs=bs: e.matmul(
                        pS[:, 0:512], wtkbuf[bs][:, k * 128:(k + 1) * 128],
                        zb.ap()[:, k * T + c * 512:k * T + (c + 1) * 512],
                        start=(k == 0), stop=False),
                        deps=(dtk,) + tuple(zevs) + (war2 if k == 0 else ()),
                        tag="mm_o1")
                for k in range(4):
                    mm = p.op("tensor", lambda e, k=k, c=c, bs=bs: e.matmul(
                        pS[:, 0:512], worbuf[bs][:, k * 128:(k + 1) * 128],
                        zb.ap()[:, k * T + c * 512:k * T + (c + 1) * 512],
                        start=False, stop=(k == 3)),
                        deps=(dor,), tag="mm_o2")
                wtk_last[bs] = mm
                evo = p.op("scalar", lambda e, d=d, c=c: e.activation(
                    obuf.ap()[:, d * T + c * 512:d * T + (c + 1) * 512],
                    pS[:, 0:512], AF.Identity, bias=bout.ap()[:, d:d + 1]),
                    deps=(mm, bout_dep), tag="ev_out")
                pS_last_read = evo
                obuf_evs.append(evo)
        stores = []
        xslot_free = [[], []]
        stage2_war = [[], []]
        ptrB_free = [vT_war[0]] if vT_war else []
        for t in range(8):
            sl = t % 2
            xt = xbuf.ap()[:, sl * D:(sl + 1) * D]
            dl = p.dma("sync", lambda e, t=t, xt=xt: e.dma_start(
                out=xt, in_=x_src[t * 128:(t + 1) * 128, :]),
                deps=tuple(xslot_free[sl]), tag=f"xl3_{t}")
            adds = []
            for half in range(2):
                trs = []
                for i in range(8):
                    d = half * 8 + i
                    war = tuple(ptrB_free) if i == 0 else ()
                    _, tr = self._tr_only(ptrB[:, i * 128:(i + 1) * 128],
                                          obuf.ap()[:, d * T + t * 128:
                                                    d * T + (t + 1) * 128],
                                          deps=(obuf_evs[2 * d + t // 4],) + war,
                                          bf16=True)
                    trs.append(tr)
                ev4 = p.op("vector", lambda e, half=half: e.tensor_copy(
                    stg[:, half * 1024:(half + 1) * 1024], ptrB[:, 0:1024]),
                    deps=tuple(trs) + tuple(stage2_war[half]), tag="oev3")
                ptrB_free = [ev4]
                ad = p.op("vector", lambda e, xt=xt, half=half: e.tensor_add(
                    xt[:, half * 1024:(half + 1) * 1024],
                    xt[:, half * 1024:(half + 1) * 1024],
                    stg[:, half * 1024:(half + 1) * 1024]),
                    deps=(ev4, dl), tag="oadd3")
                stage2_war[half] = [ad]
                adds.append(ad)
            st = p.dma("sync", lambda e, t=t, xt=xt: e.dma_start(
                out=x_dst[t * 128:(t + 1) * 128, :], in_=xt),
                deps=tuple(adds), tag=f"xs3_{t}")
            xslot_free[sl] = [st]
            stores.append(st)
        return stores



def _ident_np():
    import ml_dtypes
    i32 = np.eye(128, dtype=np.float32)
    ibf = i32.astype(ml_dtypes.bfloat16)
    return i32, ibf


def build_ffn0_nc(n_t=8):
    nc = bass.Bass("TRN2", target_bir_lowering=False)
    T = n_t * 128
    x_in = nc.dram_tensor("x", [T, D], F32, kind="ExternalInput").ap()
    wg = nc.dram_tensor("wg", [64, 128, D], F32, kind="ExternalInput").ap()
    wu = nc.dram_tensor("wu", [64, 128, D], F32, kind="ExternalInput").ap()
    wd = nc.dram_tensor("wd", [16, 128, 8192], F32, kind="ExternalInput").ap()
    id32 = nc.dram_tensor("id32", [128, 128], F32, kind="ExternalInput").ap()
    idbf = nc.dram_tensor("idbf", [128, 128], BF16, kind="ExternalInput").ap()
    x_out = nc.dram_tensor("x_out", [T, D], F32, kind="ExternalOutput").ap()

    with ExitStack() as ctx:
        p = Prog(nc)
        cb = CoreBuild(p, nc, ctx)
        cb.load_identity(id32, idbf)
        xres = cb.sb("xres", [128, n_t * D], F32)
        x_tiles = [xres.ap()[:, t * D:(t + 1) * D] for t in range(n_t)]
        x_deps = []
        for t in range(n_t):
            dl = p.dma("sync", lambda e, t=t: e.dma_start(
                out=x_tiles[t], in_=x_in[t * 128:(t + 1) * 128, :]), tag=f"xload{t}")
            x_deps.append([dl])
        xn_fm = cb.sb("xnfm", [128, 16 * T], BF16)
        xn_ready, _, _lnp = cb.emit_ln_to_fm(x_tiles, x_deps, xn_fm, n_t)
        add_ops = cb.emit_ffn0(x_tiles, xn_fm, xn_ready, wg, wu, wd, n_t, coeff=0.5)
        stores = []
        for t in range(n_t):
            st = p.dma("sync", lambda e, t=t: e.dma_start(
                out=x_out[t * 128:(t + 1) * 128, :], in_=x_tiles[t]),
                deps=tuple(add_ops[t]), tag=f"xstore{t}")
            stores.append(st)
        p.finalize_with_outputs(stores)

        _emit_prog(nc, p)
    return nc


def _emit_prog(nc, p):
    sems = {}
    dma_sems = {}
    with ExitStack() as semctx:
        for name in ("sync", "gpsimd", "tensor", "vector", "scalar"):
            sems[name] = semctx.enter_context(nc.semaphore(f"sem_{name}"))
            dma_sems[name] = [semctx.enter_context(nc.semaphore(f"dsem_{name}{i}"))
                              for i in range(Prog.N_DMA_SLOTS)]
        block = semctx.enter_context(nc.Block())
        p.emit(block, sems, dma_sems)


NCP = 10


def build_l0attnB_nc():
    nc = bass.Bass("TRN2", target_bir_lowering=False)
    T = T_CORE
    x_in = nc.dram_tensor("x", [T, D], F32, kind="ExternalInput").ap()
    mem_i = nc.dram_tensor("mem", [128, 16 * T], BF16, kind="ExternalInput").ap()
    qh_i = nc.dram_tensor("qhat", [128, 8 * D], BF16, kind="ExternalInput").ap()
    xc_i = nc.dram_tensor("xc", [128, 16 * T], BF16, kind="ExternalInput").ap()
    gm_i = nc.dram_tensor("gamma", [16, T], F32, kind="ExternalInput").ap()
    allP = nc.dram_tensor("allP", [8, 16], F32, kind="ExternalInput").ap()
    allM = nc.dram_tensor("allM", [8, D], F32, kind="ExternalInput").ap()
    mmask = nc.dram_tensor("mmask", [16, 8], F32, kind="ExternalInput").ap()
    ommask = nc.dram_tensor("ommask", [16, 8], F32, kind="ExternalInput").ap()
    oneh = nc.dram_tensor("onehot", [16, D], F32, kind="ExternalInput").ap()
    ones8 = nc.dram_tensor("ones8", [8, 1], F32, kind="ExternalInput").ap()
    cp = nc.dram_tensor("colpack", [D, NCP], F32, kind="ExternalInput").ap()
    mnw = nc.dram_tensor("mnw", [128, D], F32, kind="ExternalInput").ap()
    mnb = nc.dram_tensor("mnb", [128, D], F32, kind="ExternalInput").ap()
    wog = nc.dram_tensor("wog", [16, 128, D], F32, kind="ExternalInput").ap()
    wo = nc.dram_tensor("wo", [16, 128, D], F32, kind="ExternalInput").ap()
    id32 = nc.dram_tensor("id32", [128, 128], F32, kind="ExternalInput").ap()
    idbf = nc.dram_tensor("idbf", [128, 128], BF16, kind="ExternalInput").ap()
    x_out = nc.dram_tensor("x_out", [T, D], F32, kind="ExternalOutput").ap()

    with ExitStack() as ctx:
        p = Prog(nc)
        cb = CoreBuild(p, nc, ctx)
        cb.load_identity(id32, idbf)

        def ld(name, shape, dt, src):
            t = cb.sb(name, shape, dt)
            dep = p.dma("sync", lambda e, t=t, src=src: e.dma_start(out=t[:], in_=src),
                        tag=f"ld_{name}")
            return t, dep

        mem_sb, dmem = ld("mem", [128, 16 * T], BF16, mem_i)
        qh_sb, dqh = ld("qhat", [128, 8 * D], BF16, qh_i)
        xc_sb, dxc = ld("xc", [128, 16 * T], BF16, xc_i)
        gm_sb, dgm = ld("gamma", [16, T], F32, gm_i)
        ap_sb, dap = ld("allP", [8, 16], F32, allP)
        am_sb, dam = ld("allM", [8, D], F32, allM)
        mm_sb, dmm = ld("mmask", [16, 8], F32, mmask)
        om_sb, dom = ld("ommask", [16, 8], F32, ommask)
        oh_sb, doh = ld("oneh", [16, D], F32, oneh)
        o8_sb, do8 = ld("ones8", [8, 1], F32, ones8)
        mnw_sb, dmnw = ld("mnw", [128, D], F32, mnw)
        mnb_sb, dmnb = ld("mnb", [128, D], F32, mnb)
        cp_sb = cb.sb("cp", [128, 16 * NCP], F32)
        dcp = p.dma("sync", lambda e: e.dma_start(
            out=cp_sb.ap().rearrange("p (h c) -> p h c", h=16),
            in_=cp.rearrange("(h p) c -> p h c", p=128)), tag="cp")

        stores = cb.emit_l0attn_B(
            x_in, x_out, mem_sb, dmem, qh_sb, dqh, xc_sb, dxc, gm_sb, dgm,
            ap_sb.ap(), dap, am_sb.ap(), dam, mm_sb.ap(), dmm, om_sb.ap(), dom,
            oh_sb.ap(), doh, o8_sb.ap(), do8, cp_sb, dcp,
            mnw_sb, dmnw, mnb_sb, dmnb, wog, wo, NCP=NCP)
        p.finalize_with_outputs(stores)
        _emit_prog(nc, p)
    return nc


def run_l0attnB_launch(x_full, prep, a_outs):
    i32, ibf = _ident_np()
    if "l0b" not in _NC_CACHE:
        _NC_CACHE["l0b"] = build_l0attnB_nc()
    nc = _NC_CACHE["l0b"]
    allP = np.stack([a_outs[j]["tp_out"][:, 0] for j in range(N_CORES)])
    allM = np.stack([a_outs[j]["m_out"].reshape(D) for j in range(N_CORES)])
    in_maps = []
    for i in range(N_CORES):
        mmask = np.zeros((16, 8), np.float32)
        mmask[:, :i] = 1.0
        m = {"x": np.ascontiguousarray(x_full[i * T_CORE:(i + 1) * T_CORE]),
             "mem": a_outs[i]["mem_out"], "qhat": a_outs[i]["qhat_out"],
             "xc": a_outs[i]["xc_out"], "gamma": a_outs[i]["gamma_out"],
             "allP": allP, "allM": allM, "mmask": mmask,
             "ommask": np.ascontiguousarray(1.0 - mmask),
             "onehot": prep["onehot"], "ones8": np.ones((8, 1), np.float32),
             "colpack": prep["colpack"], "mnw": prep["mnw"], "mnb": prep["mnb"],
             "wog": prep["wog"], "wo": prep["wo"], "id32": i32, "idbf": ibf}
        in_maps.append(m)
    res = run_bass_kernel_spmd(nc, in_maps, core_ids=list(range(N_CORES)))
    return np.concatenate([res.results[i]["x_out"] for i in range(N_CORES)], axis=0)


def build_l0attnA_nc():
    nc = bass.Bass("TRN2", target_bir_lowering=False)
    T = T_CORE
    x_in = nc.dram_tensor("x", [T, D], F32, kind="ExternalInput").ap()
    xh3 = nc.dram_tensor("xh3", [3, D], F32, kind="ExternalInput").ap()
    cp = nc.dram_tensor("colpack", [D, NCP], F32, kind="ExternalInput").ap()
    wk = nc.dram_tensor("wk", [16, 128, D], F32, kind="ExternalInput").ap()
    wv = nc.dram_tensor("wv", [16, 128, D], F32, kind="ExternalInput").ap()
    wig = nc.dram_tensor("wig", [16, 128, D], F32, kind="ExternalInput").ap()
    wq = nc.dram_tensor("wq", [16, 128, D], F32, kind="ExternalInput").ap()
    wgp = nc.dram_tensor("wgp", [1, 128, 256], F32, kind="ExternalInput").ap()
    bg = nc.dram_tensor("bg", [16, 1], F32, kind="ExternalInput").ap()
    oneh = nc.dram_tensor("onehot", [16, D], F32, kind="ExternalInput").ap()
    id32 = nc.dram_tensor("id32", [128, 128], F32, kind="ExternalInput").ap()
    idbf = nc.dram_tensor("idbf", [128, 128], BF16, kind="ExternalInput").ap()
    mem_o = nc.dram_tensor("mem_out", [128, 16 * T], BF16, kind="ExternalOutput").ap()
    qh_o = nc.dram_tensor("qhat_out", [128, 8 * D], BF16, kind="ExternalOutput").ap()
    xc_o = nc.dram_tensor("xc_out", [128, 16 * T], BF16, kind="ExternalOutput").ap()
    gm_o = nc.dram_tensor("gamma_out", [16, T], F32, kind="ExternalOutput").ap()
    tp_o = nc.dram_tensor("tp_out", [16, 1], F32, kind="ExternalOutput").ap()
    m_o = nc.dram_tensor("m_out", [16, 128], F32, kind="ExternalOutput").ap()

    with ExitStack() as ctx:
        p = Prog(nc)
        cb = CoreBuild(p, nc, ctx)
        cb.load_identity(id32, idbf)
        xh_sb = cb.sb("xh", [3, D], F32)
        dxh = p.dma("sync", lambda e: e.dma_start(out=xh_sb[:], in_=xh3), tag="xh")
        cp_sb = cb.sb("cp", [128, 16 * NCP], F32)
        dcp = p.dma("sync", lambda e: e.dma_start(
            out=cp_sb.ap().rearrange("p (h c) -> p h c", h=16),
            in_=cp.rearrange("(h p) c -> p h c", p=128)), tag="cp")
        bg_sb = cb.sb("bg", [16, 1], F32)
        dbg = p.dma("sync", lambda e: e.dma_start(out=bg_sb[:], in_=bg), tag="bg")
        oh_sb = cb.sb("oneh", [16, D], F32)
        doh = p.dma("sync", lambda e: e.dma_start(out=oh_sb[:], in_=oneh), tag="oh")

        qhat_buf = cb.sb("qhat", [128, 8 * D], BF16)
        xc_buf = cb.sb("xc", [128, 16 * T_CORE], BF16)
        gamma_sb = cb.sb("gamma", [16, T_CORE], F32)
        tp_sb = cb.sb("tp", [16, 1], F32)
        mr_sb = cb.sb("mrow", [16, 128], F32)

        done = cb.emit_l0attn_A(x_in, xh_sb.ap(), dxh, cp_sb, dcp,
                                wk, wv, wig, wq, wgp, bg_sb.ap(), dbg, oh_sb.ap(),
                                doh, mem_o, qhat_buf, xc_buf, gamma_sb,
                                tp_sb, mr_sb, NCP=NCP)
        stores = [o for o in done["mem"] if o.is_dma]
        for (dst, src, dep) in ((qh_o, qhat_buf, done["qhat"]),
                                (xc_o, xc_buf, done["xc"]),
                                (gm_o, gamma_sb, done["gamma"]),
                                (tp_o, tp_sb, done["totalP"]),
                                (m_o, mr_sb, done["Mrow"])):
            st = p.dma("sync", lambda e, dst=dst, src=src: e.dma_start(
                out=dst, in_=src[:]), deps=tuple(dep), tag="store")
            stores.append(st)
        p.finalize_with_outputs(stores)
        _emit_prog(nc, p)
    return nc


def l0attn_host_prep(inp):
    d = {}
    d["wk"] = pack_lhsT(np.ascontiguousarray(inp["L0_attn_Wk"].T))
    d["wv"] = pack_lhsT(np.ascontiguousarray(inp["L0_attn_Wv"].T))
    d["wig"] = pack_lhsT(np.ascontiguousarray(inp["L0_attn_Wig"].T))
    d["wq"] = pack_lhsT(np.ascontiguousarray(inp["L0_attn_Wq"].T))
    d["wgp"] = pack_lhsT(np.ascontiguousarray(inp["L0_attn_Wg"].T), mw=16)
    d["bg"] = np.ascontiguousarray(inp["L0_attn_b_g"].reshape(16, 1))
    cp = np.zeros((D, NCP), np.float32)
    cp[:, 0:4] = inp["L0_attn_conv_w"]
    cp[:, 4] = inp["L0_attn_conv_b"]
    cp[:, 5] = inp["L0_attn_b_ig"]
    cp[:, 6] = inp["L0_attn_b_og"]
    cp[:, 8] = inp["L0_attn_gn_w"]
    cp[:, 9] = inp["L0_attn_gn_b"]
    d["colpack"] = cp
    oh = np.zeros((16, D), np.float32)
    for h in range(16):
        oh[h, h * 128:(h + 1) * 128] = 1.0
    d["onehot"] = oh
    d["wog"] = pack_lhsT(np.ascontiguousarray(inp["L0_attn_Wog"].T))
    d["wo"] = pack_lhsT(np.ascontiguousarray(inp["L0_attn_Wo"].T))
    d["mnw"] = np.ascontiguousarray(np.broadcast_to(
        inp["L0_attn_mn_w"].astype(np.float32), (128, D)))
    d["mnb"] = np.ascontiguousarray(np.broadcast_to(
        inp["L0_attn_mn_b"].astype(np.float32), (128, D)))
    return d


_NC_CACHE = {}


def run_l0attnA_launch(x_full, prep):
    i32, ibf = _ident_np()
    if "l0a" not in _NC_CACHE:
        _NC_CACHE["l0a"] = build_l0attnA_nc()
    nc = _NC_CACHE["l0a"]
    in_maps = []
    for i in range(N_CORES):
        xh = (np.zeros((3, D), np.float32) if i == 0
              else np.ascontiguousarray(x_full[i * T_CORE - 3:i * T_CORE]))
        m = {"x": np.ascontiguousarray(x_full[i * T_CORE:(i + 1) * T_CORE]),
             "xh3": xh, "id32": i32, "idbf": ibf}
        for k in ("wk", "wv", "wig", "wq", "wgp", "bg", "colpack", "onehot"):
            m[k] = prep[k]
        in_maps.append(m)
    res = run_bass_kernel_spmd(nc, in_maps, core_ids=list(range(N_CORES)))
    return [res.results[i] for i in range(N_CORES)]


def build_l1attnC_nc():
    nc = bass.Bass("TRN2", target_bir_lowering=False)
    T = T_CORE
    x_in = nc.dram_tensor("x", [T, D], F32, kind="ExternalInput").ap()
    wkv = nc.dram_tensor("wkv", [4, 128, D], F32, kind="ExternalInput").ap()
    wcmp = nc.dram_tensor("wcmp", [2, 128, 512], F32, kind="ExternalInput").ap()
    lgam = nc.dram_tensor("lgam", [128, 256], F32, kind="ExternalInput").ap()
    lbet = nc.dram_tensor("lbet", [128, 256], F32, kind="ExternalInput").ap()
    id32 = nc.dram_tensor("id32", [128, 128], F32, kind="ExternalInput").ap()
    idbf = nc.dram_tensor("idbf", [128, 128], BF16, kind="ExternalInput").ap()
    ch_o = nc.dram_tensor("ch_out", [128, 2 * T], BF16, kind="ExternalOutput").ap()
    with ExitStack() as ctx:
        p = Prog(nc)
        cb = CoreBuild(p, nc, ctx)
        cb.load_identity(id32, idbf)
        lg_sb = cb.sb("lg", [128, 256], F32)
        dlg = p.dma("sync", lambda e: e.dma_start(out=lg_sb[:], in_=lgam), tag="lg")
        lb_sb = cb.sb("lb", [128, 256], F32)
        dlb = p.dma("sync", lambda e: e.dma_start(out=lb_sb[:], in_=lbet), tag="lb")
        ch_sb = cb.sb("ch", [128, 2 * T], BF16)
        ready = cb.emit_l1attn_C(x_in, wkv, wcmp, lg_sb.ap(), dlg, lb_sb.ap(), dlb,
                                 ch_sb)
        st = p.dma("sync", lambda e: e.dma_start(out=ch_o, in_=ch_sb[:]),
                   deps=tuple(ready), tag="ch_st")
        p.finalize_with_outputs([st])
        _emit_prog(nc, p)
    return nc


def build_l1attnD_nc():
    nc = bass.Bass("TRN2", target_bir_lowering=False)
    T = T_CORE
    x_in = nc.dram_tensor("x", [T, D], F32, kind="ExternalInput").ap()
    ch_own = nc.dram_tensor("ch_own", [128, 2 * T], BF16, kind="ExternalInput").ap()
    chall = nc.dram_tensor("chall", [8, 128, 2 * T], BF16, kind="ExternalInput").ap()
    wq = nc.dram_tensor("wq", [2, 128, 256], F32, kind="ExternalInput").ap()
    wkvc = nc.dram_tensor("wkvc", [4, 128, 256], F32, kind="ExternalInput").ap()
    w2l = nc.dram_tensor("w2l", [4, 128, 256], F32, kind="ExternalInput").ap()
    wtk = nc.dram_tensor("wtk", [16, 128, 512], F32, kind="ExternalInput").ap()
    wor = nc.dram_tensor("wor", [16, 128, 512], F32, kind="ExternalInput").ap()
    cosq = nc.dram_tensor("cosq", [128, 2 * T], F32, kind="ExternalInput").ap()
    sinq = nc.dram_tensor("sinq", [128, 2 * T], F32, kind="ExternalInput").ap()
    cosk = nc.dram_tensor("cosk", [128, 2 * 8192], F32, kind="ExternalInput").ap()
    sink = nc.dram_tensor("sink", [128, 2 * 8192], F32, kind="ExternalInput").ap()
    acol = nc.dram_tensor("acol", [128, 128], F32, kind="ExternalInput").ap()
    bcol = nc.dram_tensor("bcol", [128, 128], F32, kind="ExternalInput").ap()
    ut = nc.dram_tensor("ut", [128, 4 * 512], F32, kind="ExternalInput").ap()
    bout = nc.dram_tensor("bout", [128, 16], F32, kind="ExternalInput").ap()
    id32 = nc.dram_tensor("id32", [128, 128], F32, kind="ExternalInput").ap()
    idbf = nc.dram_tensor("idbf", [128, 128], BF16, kind="ExternalInput").ap()
    x_out = nc.dram_tensor("x_out", [T, D], F32, kind="ExternalOutput").ap()
    import kernel as _K
    if getattr(_K, "L1D_WANT_DBG", False):
        _K.L1D_DBG = {
            "qhat": nc.dram_tensor("dbg_qhat", [128, 2 * T], BF16, kind="ExternalOutput").ap(),
            "oacc": nc.dram_tensor("dbg_oacc", [128, 8 * 256], F32, kind="ExternalOutput").ap(),
            "ml": nc.dram_tensor("dbg_ml", [128, 64], F32, kind="ExternalOutput").ap(),
            "kb": nc.dram_tensor("dbg_kb", [128, 1024], BF16, kind="ExternalOutput").ap(),
        }
    else:
        _K.L1D_DBG = None
    with ExitStack() as ctx:
        p = Prog(nc)
        cb = CoreBuild(p, nc, ctx)
        cb.load_identity(id32, idbf)

        def ld(name, shape, dt, src):
            t = cb.sb(name, shape, dt)
            dep = p.dma("sync", lambda e, t=t, src=src: e.dma_start(out=t[:], in_=src),
                        tag=f"ld_{name}")
            return t, dep

        chо_sb, dch = ld("chown", [128, 2 * T], BF16, ch_own)
        cq_sb, dcq = ld("cosq", [128, 2 * T], F32, cosq)
        sq_sb, dsq = ld("sinq", [128, 2 * T], F32, sinq)
        ac_sb, dac = ld("acol", [128, 128], F32, acol)
        bc_sb, dbc = ld("bcol", [128, 128], F32, bcol)
        ut_sb, dut = ld("ut", [128, 4 * 512], F32, ut)
        bo_sb, dbo = ld("bout", [128, 16], F32, bout)
        stores = cb.emit_l1attn_D(
            x_in, x_out, chо_sb, dch, chall, wq, wkvc, w2l, wtk, wor,
            cq_sb, dcq, sq_sb, dsq, cosk, sink, ac_sb, dac, bc_sb, dbc,
            ut_sb, dut, bo_sb, dbo)
        p.finalize_with_outputs(stores)
        _emit_prog(nc, p)
    return nc


def l1attn_host_prep(inp):
    d = {}
    d["wkv"] = pack_lhsT(np.ascontiguousarray(inp["L1_attn_W_to_kv"].T))
    d["wcmp"] = pack_lhsT(np.ascontiguousarray(inp["L1_attn_W_compress"].T))
    d["lgam"] = np.ascontiguousarray(np.broadcast_to(
        inp["L1_attn_ln_gamma"].astype(np.float32), (128, 256)))
    d["lbet"] = np.ascontiguousarray(np.broadcast_to(
        inp["L1_attn_ln_beta"].astype(np.float32), (128, 256)))
    d["wq"] = pack_lhsT(np.ascontiguousarray(inp["L1_attn_W_q_comp"].T))
    d["wkvc"] = pack_lhsT(np.ascontiguousarray(inp["L1_attn_W_kv_comp"].T))
    d["w2l"] = pack_lhsT(np.ascontiguousarray(inp["L1_attn_W_comp2lat"].T))
    d["wtk"] = pack_lhsT(np.ascontiguousarray(inp["L1_attn_W_to_kv"]))
    d["wor"] = pack_lhsT(np.ascontiguousarray(inp["L1_attn_W_out_res"].T))
    d["bout"] = np.ascontiguousarray(
        inp["L1_attn_b_out_res"].astype(np.float32).reshape(16, 128).T)
    inv = 1.0 / (10000.0 ** (np.arange(0, 256, 2, dtype=np.float64) / 256))
    freqs = np.arange(8192, dtype=np.float64)[:, None] * inv[None, :]
    cf = np.cos(freqs).T.astype(np.float32)
    sf = np.sin(freqs).T.astype(np.float32)
    d["cosk"] = np.ascontiguousarray(np.concatenate([cf, cf], 1))
    d["sink"] = np.ascontiguousarray(np.concatenate([sf, sf], 1))
    ut = np.zeros((128, 4 * 512), np.float32)
    for a in range(4):
        for tq in range(128):
            ut[tq, a * 512:(a + 1) * 512][np.arange(512) > tq + 128 * a] = -1e30
    d["ut"] = ut
    return d


def run_l1attn_launches(x_full, prep):
    i32, ibf = _ident_np()
    if "l1c" not in _NC_CACHE:
        _NC_CACHE["l1c"] = build_l1attnC_nc()
    if "l1d" not in _NC_CACHE:
        _NC_CACHE["l1d"] = build_l1attnD_nc()
    in_maps = []
    for i in range(N_CORES):
        m = {"x": np.ascontiguousarray(x_full[i * T_CORE:(i + 1) * T_CORE]),
             "id32": i32, "idbf": ibf}
        for k in ("wkv", "wcmp", "lgam", "lbet"):
            m[k] = prep[k]
        in_maps.append(m)
    res = run_bass_kernel_spmd(_NC_CACHE["l1c"], in_maps, core_ids=list(range(N_CORES)))
    chall = np.stack([res.results[i]["ch_out"] for i in range(N_CORES)])
    in_maps = []
    for i in range(N_CORES):
        acol = np.zeros((128, 128), np.float32)
        bcol = np.zeros((128, 128), np.float32)
        for qt in range(8):
            cstar = 2 * i + qt // 4
            for c in range(16):
                if c > cstar:
                    acol[:, qt * 16 + c] = -1e30
                elif c == cstar:
                    bcol[:, qt * 16 + c] = 1.0
        sl = slice(i * T_CORE, (i + 1) * T_CORE)
        cosq = np.ascontiguousarray(np.concatenate(
            [prep["cosk"][:, sl], prep["cosk"][:, sl]], 1))
        sinq = np.ascontiguousarray(np.concatenate(
            [prep["sink"][:, sl], prep["sink"][:, sl]], 1))
        m = {"x": np.ascontiguousarray(x_full[i * T_CORE:(i + 1) * T_CORE]),
             "ch_own": chall[i], "chall": chall, "cosq": cosq, "sinq": sinq,
             "acol": acol, "bcol": bcol, "id32": i32, "idbf": ibf}
        for k in ("wq", "wkvc", "w2l", "wtk", "wor", "cosk", "sink", "ut", "bout"):
            m[k] = prep[k]
        in_maps.append(m)
    res = run_bass_kernel_spmd(_NC_CACHE["l1d"], in_maps, core_ids=list(range(N_CORES)))
    return np.concatenate([res.results[i]["x_out"] for i in range(N_CORES)], axis=0)


def build_l1ffn_nc():
    nc = bass.Bass("TRN2", target_bir_lowering=False)
    T = T_CORE
    x_in = nc.dram_tensor("x", [T, D], F32, kind="ExternalInput").ap()
    wa = nc.dram_tensor("wa", [1, 128, 256], F32, kind="ExternalInput").ap()
    rg = nc.dram_tensor("rg", [18, 8, 128, D], F32, kind="ExternalInput").ap()
    rw1 = nc.dram_tensor("rw1", [18, 8, 128, D], F32, kind="ExternalInput").ap()
    rw2 = nc.dram_tensor("rw2", [18, 16, 128, 1024], F32, kind="ExternalInput").ap()
    rgbc = nc.dram_tensor("rgbc", [128, 18 * 8], F32, kind="ExternalInput").ap()
    rb1c = nc.dram_tensor("rb1c", [128, 18 * 8], F32, kind="ExternalInput").ap()
    rbc = nc.dram_tensor("rbc", [18, D], F32, kind="ExternalInput").ap()
    oh18 = nc.dram_tensor("oh18", [18, 18 * 128], F32, kind="ExternalInput").ap()
    id32 = nc.dram_tensor("id32", [128, 128], F32, kind="ExternalInput").ap()
    idbf = nc.dram_tensor("idbf", [128, 128], BF16, kind="ExternalInput").ap()
    x_out = nc.dram_tensor("x_out", [T, D], F32, kind="ExternalOutput").ap()

    with ExitStack() as ctx:
        p = Prog(nc)
        cb = CoreBuild(p, nc, ctx)
        cb.load_identity(id32, idbf)
        rgb_sb = cb.sb("rgb", [128, 18 * 8], F32)
        drgb = p.dma("sync", lambda e: e.dma_start(out=rgb_sb[:], in_=rgbc), tag="rgb")
        rb1_sb = cb.sb("rb1", [128, 18 * 8], F32)
        drb1 = p.dma("sync", lambda e: e.dma_start(out=rb1_sb[:], in_=rb1c), tag="rb1")
        rbc_sb = cb.sb("rbc", [18, D], F32)
        drbc = p.dma("sync", lambda e: e.dma_start(out=rbc_sb[:], in_=rbc), tag="rbc")
        oh_sb = cb.sb("oh18", [18, 18 * 128], F32)
        doh = p.dma("sync", lambda e: e.dma_start(out=oh_sb[:], in_=oh18), tag="oh18")
        stores = cb.emit_l1ffn(x_in, x_out, wa, rg, rw1, rw2,
                               rgb_sb, drgb, rb1_sb, drb1, rbc_sb, drbc,
                               oh_sb.ap(), doh, coeff=0.5)
        p.finalize_with_outputs(stores)
        _emit_prog(nc, p)
    return nc


def l1ffn_host_prep(inp):
    d = {}
    d["wa"] = pack_lhsT(np.ascontiguousarray(inp["L1_ffn_wa"]), mw=16)
    rg = np.concatenate([inp["L1_ffn_rg"], inp["L1_ffn_sg"]], 0)
    rw1 = np.concatenate([inp["L1_ffn_rw1"], inp["L1_ffn_sw1"]], 0)
    rw2 = np.concatenate([inp["L1_ffn_rw2"], inp["L1_ffn_sw2"]], 0)
    d["rg"] = np.stack([pack_lhsT(np.ascontiguousarray(rg[e])) for e in range(18)])
    d["rw1"] = np.stack([pack_lhsT(np.ascontiguousarray(rw1[e])) for e in range(18)])
    d["rw2"] = np.stack([pack_lhsT(np.ascontiguousarray(rw2[e])) for e in range(18)])
    _bias_cols = lambda a: np.ascontiguousarray(
        a.reshape(18, 8, 128).transpose(2, 0, 1).reshape(128, 144))
    d["rgbc"] = _bias_cols(np.concatenate([inp["L1_ffn_rgb"], inp["L1_ffn_sgb"]], 0))
    d["rb1c"] = _bias_cols(np.concatenate([inp["L1_ffn_rb1"], inp["L1_ffn_sb1"]], 0))
    d["rbc"] = np.concatenate([inp["L1_ffn_rb2"], inp["L1_ffn_sb2"]], 0)
    oh = np.zeros((18, 18 * 128), np.float32)
    for e in range(18):
        oh[e, e * 128:(e + 1) * 128] = 1.0
    d["oh18"] = oh
    return d


def run_l1ffn_launch(x_full, prep):
    i32, ibf = _ident_np()
    if "l1f" not in _NC_CACHE:
        _NC_CACHE["l1f"] = build_l1ffn_nc()
    nc = _NC_CACHE["l1f"]
    in_maps = []
    for i in range(N_CORES):
        m = {"x": np.ascontiguousarray(x_full[i * T_CORE:(i + 1) * T_CORE]),
             "id32": i32, "idbf": ibf}
        for k in ("wa", "rg", "rw1", "rw2", "rgbc", "rb1c", "rbc", "oh18"):
            m[k] = prep[k]
        in_maps.append(m)
    res = run_bass_kernel_spmd(nc, in_maps, core_ids=list(range(N_CORES)))
    return np.concatenate([res.results[i]["x_out"] for i in range(N_CORES)], axis=0)


_FFN0_CACHE = {}


def run_ffn0_launch(x_full, w_gate, w_up, w_down):
    i32, ibf = _ident_np()
    if "nc" not in _FFN0_CACHE:
        _FFN0_CACHE["nc"] = build_ffn0_nc()
    nc = _FFN0_CACHE["nc"]
    wg_pk = pack_lhsT(np.ascontiguousarray(w_gate.T))
    wu_pk = pack_lhsT(np.ascontiguousarray(w_up.T))
    wd_pk = pack_lhsT(np.ascontiguousarray(w_down.T))
    in_maps = []
    for i in range(N_CORES):
        in_maps.append({
            "x": np.ascontiguousarray(x_full[i * T_CORE:(i + 1) * T_CORE]),
            "wg": wg_pk, "wu": wu_pk, "wd": wd_pk, "id32": i32, "idbf": ibf,
        })
    res = run_bass_kernel_spmd(nc, in_maps, core_ids=list(range(N_CORES)))
    return np.concatenate([res.results[i]["x_out"] for i in range(N_CORES)], axis=0)


RG8 = [list(range(8))]


def build_fused_nc():
    nc = bass.Bass("TRN2", target_bir_lowering=False)
    T = T_CORE
    ei = lambda n, s, dt=F32: nc.dram_tensor(n, s, dt, kind="ExternalInput").ap()
    x_ext = ei("x", [T, D])
    wg_e = ei("wg", [64, 128, D]); wu_e = ei("wu", [64, 128, D])
    wd_e = ei("wd", [16, 128, 8192])
    wk_e = ei("wk", [16, 128, D]); wv_e = ei("wv", [16, 128, D])
    wig_e = ei("wig", [16, 128, D]); wq0_e = ei("wq0", [16, 128, D])
    wgp_e = ei("wgp", [1, 128, 256]); bg_e = ei("bg", [16, 1])
    oneh_e = ei("onehot", [16, D])
    cp_e = ei("colpack", [D, NCP])
    selh_e = ei("selh", [24, 3])
    mmask_e = ei("mmask", [16, 8]); ommask_e = ei("ommask", [16, 8])
    ones8_e = ei("ones8", [8, 1])
    mnw_e = ei("mnw", [128, D]); mnb_e = ei("mnb", [128, D])
    wog_e = ei("wog", [16, 128, D]); wo_e = ei("wo", [16, 128, D])
    wa_e = ei("wa", [1, 128, 256])
    rg_e = ei("rg", [18, 8, 128, D]); rw1_e = ei("rw1", [18, 8, 128, D])
    rw2_e = ei("rw2", [18, 16, 128, 1024])
    rgbc_e = ei("rgbc", [128, 144]); rb1c_e = ei("rb1c", [128, 144])
    rbc_e = ei("rbc", [18, D]); oh18_e = ei("oh18", [18, 18 * 128])
    wkv_e = ei("wkv", [4, 128, D]); wcmp_e = ei("wcmp", [2, 128, 512])
    lgam_e = ei("lgam", [128, 256]); lbet_e = ei("lbet", [128, 256])
    wq_e = ei("wqc", [2, 128, 256]); wkvc_e = ei("wkvc", [4, 128, 256])
    w2l_e = ei("w2l", [4, 128, 256]); wtk_e = ei("wtk", [16, 128, 512])
    wor_e = ei("wor", [16, 128, 512])
    cosq_e = ei("cosq", [128, 2 * T]); sinq_e = ei("sinq", [128, 2 * T])
    cosk_e = ei("cosk", [128, 2 * 8192]); sink_e = ei("sink", [128, 2 * 8192])
    acol_e = ei("acol", [128, 128]); bcol_e = ei("bcol", [128, 128])
    ut_e = ei("ut", [128, 4 * 512]); bout_e = ei("bout", [128, 16])
    id32_e = ei("id32", [128, 128])
    idbf_e = nc.dram_tensor("idbf", [128, 128], BF16, kind="ExternalInput").ap()
    x_out = nc.dram_tensor("x_out", [T, D], F32, kind="ExternalOutput").ap()
    xA = nc.dram_tensor("xA", [T, D], F32).ap()
    xB = nc.dram_tensor("xB", [T, D], F32).ap()
    memD = nc.dram_tensor("memD", [128, 16 * T], BF16).ap()
    qhD = nc.dram_tensor("qhD", [128, 8 * D], BF16).ap()
    xcD = nc.dram_tensor("xcD", [128, 16 * T], BF16).ap()
    gmD = nc.dram_tensor("gmD", [16, T], F32).ap()
    cch_in = nc.dram_tensor("cch_in", [3, D], F32)
    cch_out = nc.dram_tensor("cch_out", [24, D], F32)
    ccpm_in = nc.dram_tensor("ccpm_in", [17, 128], F32)
    ccpm_out = nc.dram_tensor("ccpm_out", [8 * 17, 128], F32)
    ccch_in = nc.dram_tensor("ccch_in", [128, 2 * T], BF16)
    ccch_out = nc.dram_tensor("ccch_out", [8 * 128, 2 * T], BF16)

    outer = ExitStack()
    with outer:
        p = Prog(nc)
        cb = CoreBuild(p, nc, outer)
        cb.load_identity(id32_e, idbf_e)

        def ffn0_section(x_src, x_dst):
            with ExitStack() as sctx:
                cb.ctx = sctx
                xres = cb.sb("xres", [128, 8 * D], F32)
                x_tiles = [xres.ap()[:, t * D:(t + 1) * D] for t in range(8)]
                x_deps = []
                for t in range(8):
                    dl = p.dma("sync", lambda e, t=t: e.dma_start(
                        out=x_tiles[t], in_=x_src[t * 128:(t + 1) * 128, :]),
                        tag=f"xload{t}")
                    x_deps.append([dl])
                xn_fm = cb.sb("xnfm", [128, 16 * 8 * 128], BF16)
                xn_ready, _, _ = cb.emit_ln_to_fm(x_tiles, x_deps, xn_fm, 8)
                add_ops = cb.emit_ffn0(x_tiles, xn_fm, xn_ready, wg_e, wu_e, wd_e,
                                       8, coeff=0.5)
                stores = []
                for t in range(8):
                    st = p.dma("sync", lambda e, t=t: e.dma_start(
                        out=x_dst[t * 128:(t + 1) * 128, :], in_=x_tiles[t]),
                        deps=tuple(add_ops[t]), tag=f"xstore{t}")
                    stores.append(st)
            _add_barrier(p)
            return stores

        s1 = ffn0_section(x_ext, xA)
        pub_h = p.dma("sync", lambda e: e.dma_start(
            out=cch_in[:, :], in_=xA[T - 3:T, :]), deps=tuple(s1), tag="pub_h")
        cc_h = p.op("gpsimd", lambda e: e.collective_compute(
            "AllGather", ALU.bypass, replica_groups=RG8,
            ins=[cch_in.ap().opt()], outs=[cch_out.ap().opt()]),
            deps=(pub_h,), tag="cc_h")
        with ExitStack() as sctx:
            cb.ctx = sctx
            xh_sb = cb.sb("xh", [3, D], F32)
            with ExitStack() as hctx:
                cb.ctx = hctx
                gh_sb = cb.sb("gh", [24, D], F32)
                dgh = p.dma("sync", lambda e: e.dma_start(
                    out=gh_sb[:], in_=cch_out[:, :]), deps=(cc_h,), tag="ld_gh")
                selh_sb = cb.sb("selh", [24, 3], F32)
                dselh = p.dma("sync", lambda e: e.dma_start(out=selh_sb[:], in_=selh_e),
                              tag="selh")
                pH = cb.ps("pH", [128, 512], F32)
                hsel_evs = []
                prev_ev = None
                for c4 in range(4):
                    mmh = p.op("tensor", lambda e, c4=c4: e.matmul(
                        pH[0:3, 0:512], selh_sb[:],
                        gh_sb.ap()[:, c4 * 512:(c4 + 1) * 512],
                        start=True, stop=True),
                        deps=(dgh, dselh) + ((prev_ev,) if prev_ev else ()), tag="mm_h")
                    evh = p.op("vector", lambda e, c4=c4: e.tensor_copy(
                        xh_sb.ap()[0:3, c4 * 512:(c4 + 1) * 512], pH[0:3, 0:512]),
                        deps=(mmh,), tag="ev_h")
                    prev_ev = evh
                    hsel_evs.append(evh)
            cb.ctx = sctx
            _add_barrier(p)
            cp_sb = cb.sb("cp", [128, 16 * NCP], F32)
            dcp = p.dma("sync", lambda e: e.dma_start(
                out=cp_sb.ap().rearrange("p (h c) -> p h c", h=16),
                in_=cp_e.rearrange("(h p) c -> p h c", p=128)), tag="cp")
            bg_sb = cb.sb("bg", [16, 1], F32)
            dbg = p.dma("sync", lambda e: e.dma_start(out=bg_sb[:], in_=bg_e), tag="bg")
            oh_sb = cb.sb("oneh", [16, D], F32)
            doh = p.dma("sync", lambda e: e.dma_start(out=oh_sb[:], in_=oneh_e),
                        tag="oh")
            qhat_buf = cb.sb("qhat", [128, 8 * D], BF16)
            xc_buf = cb.sb("xc", [128, 16 * T], BF16)
            gamma_sb = cb.sb("gamma", [16, T], F32)
            tp_sb = cb.sb("tp", [16, 1], F32)
            mr_sb = cb.sb("mrow", [16, 128], F32)
            done = cb.emit_l0attn_A(xA, xh_sb.ap(), hsel_evs[-1], cp_sb, dcp,
                                    wk_e, wv_e, wig_e, wq0_e, wgp_e, bg_sb.ap(),
                                    dbg, oh_sb.ap(), doh, memD, qhat_buf, xc_buf,
                                    gamma_sb, tp_sb, mr_sb, NCP=NCP)
            st_qh = p.dma("sync", lambda e: e.dma_start(out=qhD, in_=qhat_buf[:]),
                          deps=tuple(done["qhat"]), tag="st_qh")
            st_xc = p.dma("sync", lambda e: e.dma_start(out=xcD, in_=xc_buf[:]),
                          deps=tuple(done["xc"]), tag="st_xc")
            st_gm = p.dma("sync", lambda e: e.dma_start(out=gmD, in_=gamma_sb[:]),
                          deps=tuple(done["gamma"]), tag="st_gm")
            pub_tp = p.dma("sync", lambda e: e.dma_start(
                out=ccpm_in[0:1, 0:16].transpose([1, 0]), in_=tp_sb[:],
                allow_slow_non_contiguous=True),
                deps=tuple(done["totalP"]), tag="pub_tp")
            pub_m = p.dma("sync", lambda e: e.dma_start(
                out=ccpm_in[1:17, :], in_=mr_sb[:]),
                deps=tuple(done["Mrow"]), tag="pub_m")
        cc_pm = p.op("gpsimd", lambda e: e.collective_compute(
            "AllGather", ALU.bypass, replica_groups=RG8,
            ins=[ccpm_in.ap().opt()], outs=[ccpm_out.ap().opt()]),
            deps=(pub_tp, pub_m, st_qh, st_xc, st_gm), tag="cc_pm")
        _add_barrier(p)
        with ExitStack() as sctx:
            cb.ctx = sctx

            def ld(name, shape, dt, src):
                t = cb.sb(name, shape, dt)
                dep = p.dma("sync", lambda e, t=t, src=src: e.dma_start(
                    out=t[:], in_=src), tag=f"ld_{name}")
                return t, dep

            mem_sb, dmem = ld("mem", [128, 16 * T], BF16, memD)
            qh_sb, dqh = ld("qhat", [128, 8 * D], BF16, qhD)
            xc_sb, dxc = ld("xc", [128, 16 * T], BF16, xcD)
            gm_sb, dgm = ld("gamma", [16, T], F32, gmD)
            ap_sb = cb.sb("allP", [8, 16], F32)
            dap = p.dma("sync", lambda e: e.dma_start(
                out=ap_sb.ap().rearrange("r (o f) -> r o f", o=1),
                in_=ccpm_out.ap().rearrange("(r s) c -> r s c", s=17)[:, 0:1, 0:16]),
                deps=(cc_pm,), tag="ld_allP")
            am_sb = cb.sb("allM", [8, D], F32)
            dam = p.dma("sync", lambda e: e.dma_start(
                out=am_sb.ap().rearrange("r (s c) -> r s c", s=16),
                in_=ccpm_out.ap().rearrange("(r s) c -> r s c", s=17)[:, 1:17, :]),
                deps=(cc_pm,), tag="ld_allM")
            mm_sb, dmm = ld("mmask", [16, 8], F32, mmask_e)
            om_sb, dom = ld("ommask", [16, 8], F32, ommask_e)
            oh_sb2, doh2 = ld("oneh2", [16, D], F32, oneh_e)
            o8_sb, do8 = ld("ones8", [8, 1], F32, ones8_e)
            mnw_sb, dmnw = ld("mnw", [128, D], F32, mnw_e)
            mnb_sb, dmnb = ld("mnb", [128, D], F32, mnb_e)
            cp_sb2 = cb.sb("cp2", [128, 16 * NCP], F32)
            dcp2 = p.dma("sync", lambda e: e.dma_start(
                out=cp_sb2.ap().rearrange("p (h c) -> p h c", h=16),
                in_=cp_e.rearrange("(h p) c -> p h c", p=128)), tag="cp2")
            cb.emit_l0attn_B(
                xA, xB, mem_sb, dmem, qh_sb, dqh, xc_sb, dxc, gm_sb, dgm,
                ap_sb.ap(), dap, am_sb.ap(), dam, mm_sb.ap(), dmm, om_sb.ap(), dom,
                oh_sb2.ap(), doh2, o8_sb.ap(), do8, cp_sb2, dcp2,
                mnw_sb, dmnw, mnb_sb, dmnb, wog_e, wo_e, NCP=NCP)
        _add_barrier(p)
        ffn0_section(xB, xA)
        def l1ffn_section(x_src, x_dst):
            with ExitStack() as sctx:
                cb.ctx = sctx
                rgb_sb = cb.sb("rgb", [128, 144], F32)
                drgb = p.dma("sync", lambda e: e.dma_start(out=rgb_sb[:], in_=rgbc_e),
                             tag="rgb")
                rb1_sb = cb.sb("rb1", [128, 144], F32)
                drb1 = p.dma("sync", lambda e: e.dma_start(out=rb1_sb[:], in_=rb1c_e),
                             tag="rb1")
                rbc_sb = cb.sb("rbc", [18, D], F32)
                drbc = p.dma("sync", lambda e: e.dma_start(out=rbc_sb[:], in_=rbc_e),
                             tag="rbc")
                oh_sb3 = cb.sb("oh18", [18, 18 * 128], F32)
                doh3 = p.dma("sync", lambda e: e.dma_start(out=oh_sb3[:], in_=oh18_e),
                             tag="oh18")
                cb.emit_l1ffn(x_src, x_dst, wa_e, rg_e, rw1_e, rw2_e,
                              rgb_sb, drgb, rb1_sb, drb1, rbc_sb, drbc,
                              oh_sb3.ap(), doh3, coeff=0.5)
            _add_barrier(p)

        l1ffn_section(xA, xB)
        with ExitStack() as sctx:
            cb.ctx = sctx
            lg_sb = cb.sb("lg", [128, 256], F32)
            dlg = p.dma("sync", lambda e: e.dma_start(out=lg_sb[:], in_=lgam_e),
                        tag="lg")
            lb_sb = cb.sb("lb", [128, 256], F32)
            dlb = p.dma("sync", lambda e: e.dma_start(out=lb_sb[:], in_=lbet_e),
                        tag="lb")
            ch_sb = cb.sb("ch", [128, 2 * T], BF16)
            ready = cb.emit_l1attn_C(xB, wkv_e, wcmp_e, lg_sb.ap(), dlg,
                                     lb_sb.ap(), dlb, ch_sb)
            pub_ch = p.dma("sync", lambda e: e.dma_start(
                out=ccch_in[:, :], in_=ch_sb[:]), deps=tuple(ready), tag="pub_ch")
        cc_ch = p.op("gpsimd", lambda e: e.collective_compute(
            "AllGather", ALU.bypass, replica_groups=RG8,
            ins=[ccch_in.ap().opt()], outs=[ccch_out.ap().opt()]),
            deps=(pub_ch,), tag="cc_ch")
        _add_barrier(p)
        with ExitStack() as sctx:
            cb.ctx = sctx

            def ld2(name, shape, dt, src, deps=()):
                t = cb.sb(name, shape, dt)
                dep = p.dma("sync", lambda e, t=t, src=src: e.dma_start(
                    out=t[:], in_=src), deps=deps, tag=f"ld_{name}")
                return t, dep

            cho_sb, dcho = ld2("chown", [128, 2 * T], BF16, ccch_in.ap())
            cq_sb, dcq = ld2("cosq", [128, 2 * T], F32, cosq_e)
            sq_sb, dsq2 = ld2("sinq", [128, 2 * T], F32, sinq_e)
            ac_sb, dac = ld2("acol", [128, 128], F32, acol_e)
            bc_sb, dbc = ld2("bcol", [128, 128], F32, bcol_e)
            ut_sb, dut = ld2("ut", [128, 4 * 512], F32, ut_e)
            bo_sb, dbo = ld2("bout", [128, 16], F32, bout_e)
            chall_view = ccch_out.ap().rearrange("(r p) f -> r p f", p=128)
            cb.emit_l1attn_D(
                xB, xA, cho_sb, dcho, chall_view, wq_e, wkvc_e, w2l_e, wtk_e,
                wor_e, cq_sb, dcq, sq_sb, dsq2, cosk_e, sink_e, ac_sb, dac,
                bc_sb, dbc, ut_sb, dut, bo_sb, dbo)
        _add_barrier(p)
        l1ffn_section(xA, x_out)
        last_stores = [o for o in p.ops if o.is_dma and o.tag.startswith("xs2_")][-8:]
        p.finalize_with_outputs(last_stores)
        _emit_prog(nc, p)
    return nc


def _fused_in_map(inputs, core):
    inp = {k: np.asarray(v, np.float32) for k, v in inputs.items()}
    x = inp["L0_attn_x"].reshape(T_FULL, D)
    i32, ibf = _ident_np()
    prepA = l0attn_host_prep(inp)
    prep1 = l1ffn_host_prep(inp)
    prepL = l1attn_host_prep(inp)
    i = core
    m = {"x": np.ascontiguousarray(x[i * T_CORE:(i + 1) * T_CORE]),
         "id32": i32, "idbf": ibf}
    m["wg"] = pack_lhsT(np.ascontiguousarray(inp["L0_ffn_w_gate"].T))
    m["wu"] = pack_lhsT(np.ascontiguousarray(inp["L0_ffn_w_up"].T))
    m["wd"] = pack_lhsT(np.ascontiguousarray(inp["L0_ffn_w_down"].T))
    m["wk"] = prepA["wk"]; m["wv"] = prepA["wv"]; m["wig"] = prepA["wig"]
    m["wq0"] = prepA["wq"]; m["wgp"] = prepA["wgp"]; m["bg"] = prepA["bg"]
    m["onehot"] = prepA["onehot"]; m["colpack"] = prepA["colpack"]
    m["wog"] = prepA["wog"]; m["wo"] = prepA["wo"]
    m["mnw"] = prepA["mnw"]; m["mnb"] = prepA["mnb"]
    selh = np.zeros((24, 3), np.float32)
    if i > 0:
        for r in range(3):
            selh[3 * (i - 1) + r, r] = 1.0
    m["selh"] = selh
    mmask = np.zeros((16, 8), np.float32); mmask[:, :i] = 1.0
    m["mmask"] = mmask; m["ommask"] = np.ascontiguousarray(1.0 - mmask)
    m["ones8"] = np.ones((8, 1), np.float32)
    for k in ("wa", "rg", "rw1", "rw2", "rgbc", "rb1c", "rbc", "oh18"):
        m[k] = prep1[k]
    for k in ("wkv", "wcmp", "lgam", "lbet", "wkvc", "w2l", "wtk", "wor",
              "cosk", "sink", "ut", "bout"):
        m[k] = prepL[k]
    m["wqc"] = prepL["wq"]
    acol = np.zeros((128, 128), np.float32)
    bcol = np.zeros((128, 128), np.float32)
    for qt in range(8):
        cstar = 2 * i + qt // 4
        for c in range(16):
            if c > cstar:
                acol[:, qt * 16 + c] = -1e30
            elif c == cstar:
                bcol[:, qt * 16 + c] = 1.0
    m["acol"] = acol; m["bcol"] = bcol
    sl = slice(i * T_CORE, (i + 1) * T_CORE)
    m["cosq"] = np.ascontiguousarray(np.concatenate(
        [prepL["cosk"][:, sl], prepL["cosk"][:, sl]], 1))
    m["sinq"] = np.ascontiguousarray(np.concatenate(
        [prepL["sink"][:, sl], prepL["sink"][:, sl]], 1))
    return m


def kernel_fused(**inputs):
    if "fused" not in _NC_CACHE:
        _NC_CACHE["fused"] = build_fused_nc()
    nc = _NC_CACHE["fused"]
    in_maps = [_fused_in_map(inputs, i) for i in range(N_CORES)]
    res = run_bass_kernel_spmd(nc, in_maps, core_ids=list(range(N_CORES)))
    x = np.concatenate([res.results[i]["x_out"] for i in range(N_CORES)], axis=0)
    return np.ascontiguousarray(x).reshape(1, T_FULL, D)


def kernel(**inputs):
    return kernel_fused(**inputs)


def kernel_unfused(**inputs):
    x = np.ascontiguousarray(np.asarray(inputs["L0_attn_x"], dtype=np.float32)
                             .reshape(T_FULL, D))
    wg = np.asarray(inputs["L0_ffn_w_gate"], np.float32)
    wu = np.asarray(inputs["L0_ffn_w_up"], np.float32)
    wd = np.asarray(inputs["L0_ffn_w_down"], np.float32)
    inp = {k: np.asarray(v, np.float32) for k, v in inputs.items()}
    prepA = l0attn_host_prep(inp)
    prep1 = l1ffn_host_prep(inp)
    prepL = l1attn_host_prep(inp)
    x = run_ffn0_launch(x, wg, wu, wd)
    a_outs = run_l0attnA_launch(x, prepA)
    x = run_l0attnB_launch(x, prepA, a_outs)
    x = run_ffn0_launch(x, wg, wu, wd)
    x = run_l1ffn_launch(x, prep1)
    x = run_l1attn_launches(x, prepL)
    x = run_l1ffn_launch(x, prep1)
    return np.ascontiguousarray(x).reshape(1, T_FULL, D)


# version 05
# speedup vs baseline: 1.1186x; 1.1186x over previous
import numpy as np
from contextlib import ExitStack

import concourse.bass as bass
import concourse.mybir as mybir
from concourse.bass_utils import run_bass_kernel_spmd

F32 = mybir.dt.float32
BF16 = mybir.dt.bfloat16
AF = mybir.ActivationFunctionType
ALU = mybir.AluOpType
AX = mybir.AxisListType

N_CORES = 8
T_FULL = 8192
T_CORE = T_FULL // N_CORES
D = 2048
NH = 16
HD = 128
EPS = 1e-5

COMPUTE_ENGINES = ("tensor", "vector", "scalar", "gpsimd")


class Op:
    __slots__ = ("idx", "engine", "fn", "deps", "is_dma", "inc_amt", "marked",
                 "cum", "tag", "sem_key", "slot")

    def __init__(self, idx, engine, fn, deps, is_dma, tag):
        self.idx = idx
        self.engine = engine
        self.fn = fn
        self.deps = deps
        self.is_dma = is_dma
        self.inc_amt = 16 if is_dma else 1
        self.marked = is_dma
        self.cum = None
        self.tag = tag
        self.sem_key = engine
        self.slot = None


class Prog:

    def __init__(self, nc):
        self.nc = nc
        self.ops = []

    barrier_deps = ()

    def op(self, engine, fn, deps=(), dma=False, tag=""):
        deps = tuple(deps) + tuple(self.barrier_deps)
        for d in deps:
            assert isinstance(d, Op) and d.idx < len(self.ops)
        o = Op(len(self.ops), engine, fn, deps, dma, tag)
        for d in o.deps:
            d.marked = True
        self.ops.append(o)
        return o

    def dma(self, engine, fn, deps=(), tag=""):
        return self.op(engine, fn, deps, dma=True, tag=tag)

    def barrier_all(self, engines=COMPUTE_ENGINES + ("sync",)):
        last = {}
        for o in self.ops:
            last[o.engine] = o
        return [last[e] for e in engines if e in last]

    N_DMA_SLOTS = 6

    def emit(self, block, sems, dma_sems):
        counts = {}
        dma_counts = {}
        for o in self.ops:
            if o.is_dma:
                k = dma_counts.get(o.engine, 0)
                dma_counts[o.engine] = k + 1
                o.slot = k % self.N_DMA_SLOTS
                o.sem_key = (o.engine, o.slot)
            c = counts.get(o.sem_key, 0)
            if o.marked:
                c += o.inc_amt
            o.cum = c
            counts[o.sem_key] = c
        streams = {}
        for o in self.ops:
            streams.setdefault(o.engine, []).append(o)

        def get_sem(key):
            if isinstance(key, tuple):
                return dma_sems[key[0]][key[1]]
            return sems[key]

        def run_stream(eng_name, eng):
            hw = {}
            def wait(key, val):
                if hw.get(key, -1) >= val:
                    return
                eng.wait_ge(get_sem(key), val)
                hw[key] = val
            for o in streams.get(eng_name, ()):
                for d in o.deps:
                    assert d.marked, f"dep {d.idx} {d.tag} of {o.idx} {o.tag} unmarked"
                    wait(d.sem_key, d.cum)
                if o.is_dma and o.cum > 16:
                    wait(o.sem_key, o.cum - 16)
                inst = o.fn(eng)
                if o.marked:
                    assert inst is not None, f"op {o.idx} {o.tag} marked but returned no inst"
                    inst.then_inc(get_sem(o.sem_key), o.inc_amt)

        for name in ("sync", "gpsimd", "tensor", "vector", "scalar"):
            if name not in streams:
                continue
            getattr(block, name)(lambda eng, n=name: run_stream(n, eng))

    def finalize_with_outputs(self, out_dmas):
        for o in out_dmas:
            o.marked = True
        def waiter(eng):
            return None
        w = self.op("sync", waiter, deps=tuple(out_dmas), tag="final_wait")
        return w


def ceil_div(a, b):
    return (a + b - 1) // b


def pack_lhsT(Wt, mw=128):
    K_in, M_out = Wt.shape
    nk, nm = K_in // 128, M_out // mw
    return np.ascontiguousarray(
        Wt.reshape(nk, 128, nm, mw).transpose(2, 1, 0, 3).reshape(nm, 128, nk * mw)
    )


class ProgBarrier:
    pass


def _add_barrier(prog):
    last = {}
    for o in prog.ops:
        last[o.engine] = o
    prog.barrier_deps = tuple(last.values())


class CoreBuild:

    def __init__(self, prog, nc, ctx):
        self.p = prog
        self.nc = nc
        self.ctx = ctx
        self._uid = 0
        self.id_bf = None
        self.id_f32 = None
        self.id_ready = None

    def uid(self, base):
        self._uid += 1
        return f"{base}_{self._uid}"

    def sb(self, base, shape, dt=F32):
        return self.ctx.enter_context(self.nc.sbuf_tensor(self.uid(base), shape, dt))

    def ps(self, base, shape, dt=F32):
        return self.ctx.enter_context(self.nc.psum_tensor(self.uid(base), shape, dt))

    def load_identity(self, ident_f32_ap, ident_bf_ap):
        self.id_f32 = self.sb("idf", [128, 128], F32)
        self.id_bf = self.sb("idb", [128, 128], BF16)
        d1 = self.p.dma("sync", lambda e: e.dma_start(out=self.id_f32[:], in_=ident_f32_ap), tag="id32")
        d2 = self.p.dma("sync", lambda e: e.dma_start(out=self.id_bf[:], in_=ident_bf_ap), tag="idbf")
        self.id_ready = [d1, d2]

    def transpose(self, dst_ap, src_ap, pslot_ap, deps, bf16=True, evac_engine="vector",
                  war=()):
        ident = self.id_bf if bf16 else self.id_f32
        tr = self.p.op("tensor",
                       lambda e: e.transpose(pslot_ap, src_ap, ident[:]),
                       deps=tuple(deps) + tuple(war) + tuple(self.id_ready), tag="tr")
        ev = self.p.op(evac_engine,
                       lambda e: (e.tensor_copy(dst_ap, pslot_ap) if evac_engine == "vector"
                                  else e.copy(dst_ap, pslot_ap)),
                       deps=(tr,), tag="trev")
        return ev, tr

    def emit_ln_to_fm(self, x_tiles, x_deps, xn_fm, n_t, scratch=None,
                      stats_out=None):
        p = self.p
        nd = D // 128
        if scratch is None:
            scratch = self.sb("lnscr", [128, D], F32)
        sums_all = self.sb("lnsum", [128, 4 * n_t], F32)
        if stats_out is None:
            stats_out = self.sb("lnstats", [128, 2 * n_t], F32)
        xn_tm = self.sb("xntm", [128, D], BF16)
        ptr = self.ps("lntr", [128, 1024], BF16)
        ready = []
        prev_sq = None
        prev_evs = []
        prev_norm = None
        for t in range(n_t):
            xt = x_tiles[t]
            sums = sums_all.ap()[:, 4 * t:4 * t + 4]
            dsum = p.op("vector", lambda e, xt=xt, sums=sums: e.reduce_sum(
                sums[:, 0:1], xt[:], axis=AX.X), deps=x_deps[t], tag=f"ln_sum{t}")
            sq_deps = list(x_deps[t]) + ([prev_sq] if prev_sq else [])
            dsq = p.op("scalar", lambda e, xt=xt, sums=sums: e.activation(
                scratch[:], xt[:], AF.Square, accum_out=sums[:, 1:2]),
                deps=sq_deps, tag=f"ln_sq{t}")
            prev_sq = dsq
            mu = p.op("vector", lambda e, t=t, sums=sums: e.tensor_scalar_mul(
                stats_out[:, 2 * t:2 * t + 1], sums[:, 0:1], 1.0 / D),
                deps=(dsum,), tag="ln_mu")
            musq = p.op("vector", lambda e, t=t, sums=sums: e.tensor_mul(
                sums[:, 3:4], stats_out[:, 2 * t:2 * t + 1], stats_out[:, 2 * t:2 * t + 1]),
                deps=(mu,), tag="ln_musq")
            var = p.op("vector", lambda e, sums=sums: e.tensor_scalar(
                sums[:, 2:3], sums[:, 1:2], 1.0 / D, None, ALU.mult),
                deps=(dsq,), tag="ln_var0")
            var2 = p.op("vector", lambda e, sums=sums: e.tensor_scalar(
                sums[:, 2:3], sums[:, 2:3], sums[:, 3:4], EPS,
                ALU.subtract, ALU.add),
                deps=(var, musq), tag="ln_var")
            ln_ = p.op("scalar", lambda e, sums=sums: e.activation(
                sums[:, 3:4], sums[:, 2:3], AF.Ln),
                deps=(var2, dsq, musq), tag="ln_ln")
            rstd = p.op("scalar", lambda e, t=t, sums=sums: e.activation(
                stats_out[:, 2 * t + 1:2 * t + 2], sums[:, 3:4], AF.Exp, scale=-0.5),
                deps=(ln_,), tag="ln_rstd")
            nrm_deps = [mu, rstd] + prev_evs
            nrm = p.op("vector", lambda e, xt=xt, t=t: e.tensor_scalar(
                xn_tm[:], xt[:], stats_out[:, 2 * t:2 * t + 1],
                stats_out[:, 2 * t + 1:2 * t + 2], ALU.subtract, ALU.mult),
                deps=nrm_deps, tag=f"ln_nrm{t}")
            prev_norm = nrm
            xnv = xn_fm.ap().rearrange("p (nd tt) -> p nd tt", nd=nd)
            prev_evs = []
            for half in range(2):
                trs = []
                for i in range(8):
                    dti = half * 8 + i
                    _, tr = self._tr_only(ptr[:, i * 128:(i + 1) * 128],
                                          xn_tm[:, dti * 128:(dti + 1) * 128],
                                          deps=(nrm,) + ((prev_half_ev,) if half or t else ()))
                    trs.append(tr)
                ev = p.op("vector", lambda e, half=half, t=t: e.tensor_copy(
                    xnv[:, half * 8:(half + 1) * 8, t * 128:(t + 1) * 128], ptr[:, 0:1024]),
                    deps=tuple(trs), tag="ln_trev")
                prev_half_ev = ev
                prev_evs.append(ev)
                ready.append(ev)
        return ready, stats_out, ptr

    def _tr_only(self, pslot_ap, src_ap, deps, bf16=True, npart=128):
        ident = self.id_bf if bf16 else self.id_f32
        tr = self.p.op("tensor",
                       lambda e: e.transpose(pslot_ap, src_ap, ident[0:npart, 0:npart]),
                       deps=tuple(deps) + tuple(self.id_ready), tag="tr")
        return None, tr

    def emit_ffn0(self, x_tiles, xn_fm, xn_ready, wg_pk, wu_pk, wd_pk, n_t,
                  coeff=0.5, dma_engine="gpsimd"):
        p = self.p
        T = n_t * 128
        n_ch = T // 512
        xnv = xn_fm.ap().rearrange("p (nd tt) -> p nd tt", nd=D // 128)
        hs = self.sb("hs", [128, 16 * T], BF16)
        hsv = hs.ap().rearrange("p (nf tt) -> p nf tt", nf=16)
        wgbuf = [self.sb("wgbuf", [128, D], BF16) for _ in range(2)]
        wubuf = [self.sb("wubuf", [128, D], BF16) for _ in range(2)]
        wdbuf = [self.sb("wdbuf", [128, D], BF16) for _ in range(2)]
        silu_tmp = self.sb("silu", [128, 1024], F32)
        dtmp = self.sb("dtmp", [128, 1024], BF16)
        pg = [self.ps("pg", [128, 512], F32) for _ in range(2)]
        pu = [self.ps("pu", [128, 512], F32) for _ in range(2)]
        pd = [self.ps("pd", [128, 512], F32) for _ in range(2)]
        ptrD = self.ps("ptrD", [128, 1024], BF16)
        dstg = self.sb("dstg", [128, 1024], BF16)
        ptrD_free = None
        dstg_war = [[], []]

        add_ops = [[] for _ in range(n_t)]
        wg_last_mm = [None, None]
        wu_last_mm = [None, None]
        wd_last_mm = [None, None]
        silu_last_tt = [None, None]
        pg_last_read = [None, None]
        pu_last_read = [None, None]
        pd_last_read = [None, None]
        dtmp_last_read = [None, None]
        h_last_writes_prev_range = []

        for r in range(4):
            h_writes = []
            for ft in range(16):
                gft = r * 16 + ft
                b = ft % 2
                war_g = (wg_last_mm[b],) if wg_last_mm[b] else ()
                dg = p.dma(dma_engine, lambda e, gft=gft, b=b: e.dma_start(
                    out=wgbuf[b][:], in_=wg_pk[gft]), deps=war_g, tag=f"wg{gft}")
                war_u = (wu_last_mm[b],) if wu_last_mm[b] else ()
                du = p.dma(dma_engine, lambda e, gft=gft, b=b: e.dma_start(
                    out=wubuf[b][:], in_=wu_pk[gft]), deps=war_u, tag=f"wu{gft}")
                for c in range(2):
                    xr = tuple(xn_ready[8 * c:8 * c + 8])
                    mmg = None
                    war = (silu_last_tt[c],) if silu_last_tt[c] else ()
                    war += (pg_last_read[c],) if pg_last_read[c] else ()
                    for k in range(16):
                        mmg = p.op("tensor", lambda e, b=b, k=k, c=c: e.matmul(
                            pg[c][:, 0:512], wgbuf[b][:, k * 128:(k + 1) * 128],
                            xnv[:, k, c * 512:(c + 1) * 512],
                            start=(k == 0), stop=(k == 15)),
                            deps=(dg,) + xr + (war if k == 0 else ()),
                            tag=f"mmg{gft}_{c}")
                    mmu = None
                    war = (silu_last_tt[c],) if silu_last_tt[c] else ()
                    war += (pu_last_read[c],) if pu_last_read[c] else ()
                    for k in range(16):
                        mmu = p.op("tensor", lambda e, b=b, k=k, c=c: e.matmul(
                            pu[c][:, 0:512], wubuf[b][:, k * 128:(k + 1) * 128],
                            xnv[:, k, c * 512:(c + 1) * 512],
                            start=(k == 0), stop=(k == 15)),
                            deps=(du,) + xr + (war if k == 0 else ()),
                            tag=f"mmu{gft}_{c}")
                    wg_last_mm[b] = mmg
                    wu_last_mm[b] = mmu
                    sil = p.op("scalar", lambda e, c=c: e.activation(
                        silu_tmp[:, c * 512:(c + 1) * 512], pg[c][:, 0:512], AF.Silu),
                        deps=(mmg,) + ((silu_last_tt[c],) if silu_last_tt[c] else ()),
                        tag="silu")
                    pg_last_read[c] = sil
                    tt = p.op("vector", lambda e, c=c, ft=ft: e.tensor_mul(
                        hsv[:, ft, c * 512:(c + 1) * 512],
                        silu_tmp[:, c * 512:(c + 1) * 512], pu[c][:, 0:512]),
                        deps=(sil, mmu) + tuple(h_last_writes_prev_range), tag="hmul")
                    silu_last_tt[c] = tt
                    pu_last_read[c] = tt
                    h_writes.append(tt)
            new_h_guard = []
            for d in range(16):
                b = d % 2
                war_d = (wd_last_mm[b],) if wd_last_mm[b] else ()
                dd = p.dma(dma_engine, lambda e, d=d, r=r, b=b: e.dma_start(
                    out=wdbuf[b][:], in_=wd_pk[d][:, r * 2048:(r + 1) * 2048]),
                    deps=war_d, tag=f"wd{r}_{d}")
                for c in range(2):
                    mmd = None
                    war = (pd_last_read[c],) if pd_last_read[c] else ()
                    for k in range(16):
                        mmd = p.op("tensor", lambda e, b=b, k=k, c=c: e.matmul(
                            pd[c][:, 0:512], wdbuf[b][:, k * 128:(k + 1) * 128],
                            hsv[:, k, c * 512:(c + 1) * 512],
                            start=(k == 0), stop=(k == 15)),
                            deps=(dd,) + tuple(h_writes) + (war if k == 0 else ()),
                            tag=f"mmd{r}_{d}_{c}")
                    wd_last_mm[b] = mmd
                    ev = p.op("vector", lambda e, c=c: e.tensor_copy(
                        dtmp[:, c * 512:(c + 1) * 512], pd[c][:, 0:512]),
                        deps=(mmd,) + ((dtmp_last_read[c],) if dtmp_last_read[c] else ()),
                        tag="dev")
                    pd_last_read[c] = ev
                    tr_ops = []
                    for i in range(4):
                        slot = c * 4 + i
                        war_t = (ptrD_free,) if (i == 0 and ptrD_free) else ()
                        _, tr = self._tr_only(
                            ptrD[:, slot * 128:(slot + 1) * 128],
                            dtmp[:, c * 512 + i * 128: c * 512 + (i + 1) * 128],
                            deps=(ev,) + war_t)
                        tr_ops.append(tr)
                    ev4 = p.op("vector", lambda e, c=c: e.tensor_copy(
                        dstg[:, c * 512:(c + 1) * 512],
                        ptrD[:, c * 512:(c + 1) * 512]),
                        deps=tuple(tr_ops) + tuple(dstg_war[c]), tag="dev4")
                    ptrD_free = ev4
                    stts = []
                    for i in range(4):
                        t_idx = c * 4 + i
                        stt = p.op("vector", lambda e, i=i, c=c, t_idx=t_idx, d=d:
                                   e.scalar_tensor_tensor(
                                       x_tiles[t_idx][:, d * 128:(d + 1) * 128],
                                       dstg[:, c * 512 + i * 128:c * 512 + (i + 1) * 128],
                                       coeff,
                                       x_tiles[t_idx][:, d * 128:(d + 1) * 128],
                                       ALU.mult, ALU.add),
                                   deps=(ev4,), tag=f"xadd{r}_{d}_{t_idx}")
                        stts.append(stt)
                        add_ops[t_idx].append(stt)
                    dstg_war[c] = stts
                    dtmp_last_read[c] = tr_ops[-1]
            h_last_writes_prev_range = [wd_last_mm[0], wd_last_mm[1]]
        return add_ops

    def emit_l0attn_A(self, x_src, xh3, xh_dep, colpack, colpack_dep,
                      wk_pk, wv_pk, wig_pk, wq_pk, wg_pk, bg, bg_dep, onehot,
                      onehot_dep, mem_dram, qhat_buf, xc_buf, gamma_sb,
                      totalP_sb, Mrow_sb, NCP=10):
        p = self.p
        n_t = 8
        W = 1028
        xnx = self.sb("xnx", [128, 16 * W], BF16)
        xnv = xnx.ap().rearrange("p (nd w) -> p nd w", nd=16)
        nd = 16
        scratch = self.sb("lnscr", [128, D], F32)
        sums_all = self.sb("lnsum", [128, 4 * n_t], F32)
        stats = self.sb("lnstats", [128, 2 * n_t], F32)
        xn_tm = self.sb("xntm", [128, D], BF16)
        ptr = self.ps("lntr", [128, 1024], BF16)
        xn_ready = []
        prev_sq = None
        prev_evs = []
        prev_half_ev = None
        xbuf = self.sb("xbuf", [128, 2 * D], F32)
        ptrB_stage = self.sb("ptrBst", [128, 2048], BF16)
        stage_war = [[], []]
        xslot = [xbuf.ap()[:, 0:D], xbuf.ap()[:, D:2 * D]]
        xslot_readers = [[], []]
        for t in range(n_t):
            xt = xslot[t % 2]
            dl = p.dma("sync", lambda e, t=t, xt=xt: e.dma_start(
                out=xt, in_=x_src[t * 128:(t + 1) * 128, :]),
                deps=tuple(xslot_readers[t % 2]), tag=f"xload{t}")
            sums = sums_all.ap()[:, 4 * t:4 * t + 4]
            dsum = p.op("vector", lambda e, xt=xt, sums=sums: e.reduce_sum(
                sums[:, 0:1], xt[:], axis=AX.X), deps=(dl,), tag="ln_sum")
            dsq = p.op("scalar", lambda e, xt=xt, sums=sums: e.activation(
                scratch[:], xt[:], AF.Square, accum_out=sums[:, 1:2]),
                deps=[dl] + ([prev_sq] if prev_sq else []), tag="ln_sq")
            prev_sq = dsq
            mu = p.op("vector", lambda e, t=t, sums=sums: e.tensor_scalar_mul(
                stats[:, 2 * t:2 * t + 1], sums[:, 0:1], 1.0 / D), deps=(dsum,))
            musq = p.op("vector", lambda e, sums=sums, t=t: e.tensor_mul(
                sums[:, 3:4], stats[:, 2 * t:2 * t + 1], stats[:, 2 * t:2 * t + 1]),
                deps=(mu,))
            var = p.op("vector", lambda e, sums=sums: e.tensor_scalar(
                sums[:, 2:3], sums[:, 1:2], 1.0 / D, None, ALU.mult), deps=(dsq,))
            var2 = p.op("vector", lambda e, sums=sums: e.tensor_scalar(
                sums[:, 2:3], sums[:, 2:3], sums[:, 3:4], EPS, ALU.subtract, ALU.add),
                deps=(var, musq))
            ln_ = p.op("scalar", lambda e, sums=sums: e.activation(
                sums[:, 3:4], sums[:, 2:3], AF.Ln), deps=(var2, musq))
            rstd = p.op("scalar", lambda e, t=t, sums=sums: e.activation(
                stats[:, 2 * t + 1:2 * t + 2], sums[:, 3:4], AF.Exp, scale=-0.5),
                deps=(ln_,))
            nrm = p.op("vector", lambda e, xt=xt, t=t: e.tensor_scalar(
                xn_tm[:], xt[:], stats[:, 2 * t:2 * t + 1],
                stats[:, 2 * t + 1:2 * t + 2], ALU.subtract, ALU.mult),
                deps=[mu, rstd] + prev_evs, tag="ln_nrm")
            xslot_readers[t % 2] = [nrm, dsq, dsum]
            prev_evs = []
            for half in range(2):
                trs = []
                for i in range(8):
                    dti = half * 8 + i
                    dps = (nrm,) if (half == 0 and t == 0) else (nrm, prev_half_ev)
                    _, tr = self._tr_only(ptr[:, i * 128:(i + 1) * 128],
                                          xn_tm[:, dti * 128:(dti + 1) * 128],
                                          deps=[d for d in dps if d is not None])
                    trs.append(tr)
                ev = p.op("vector", lambda e, half=half, t=t: e.tensor_copy(
                    xnv[:, half * 8:(half + 1) * 8, 4 + t * 128:4 + (t + 1) * 128],
                    ptr[:, 0:1024]), deps=tuple(trs), tag="ln_trev")
                prev_half_ev = ev
                prev_evs.append(ev)
                xn_ready.append(ev)
        hsum = self.sb("hsum", [128, 4], F32)
        hstat = self.sb("hstat", [128, 2], F32)
        htm = xn_tm
        hd = p.op("vector", lambda e: e.reduce_sum(hsum[0:3, 0:1], xh3[:], axis=AX.X),
                  deps=(xh_dep,), tag="halo_sum")
        hq = p.op("scalar", lambda e: e.activation(scratch[0:3, :], xh3[:], AF.Square,
                                                   accum_out=hsum[0:3, 1:2]),
                  deps=(xh_dep, prev_sq), tag="halo_sq")
        hmu = p.op("vector", lambda e: e.tensor_scalar_mul(hstat[0:3, 0:1],
                   hsum[0:3, 0:1], 1.0 / D), deps=(hd,))
        hms = p.op("vector", lambda e: e.tensor_mul(hsum[0:3, 3:4], hstat[0:3, 0:1],
                   hstat[0:3, 0:1]), deps=(hmu,))
        hv = p.op("vector", lambda e: e.tensor_scalar(hsum[0:3, 2:3], hsum[0:3, 1:2],
                  1.0 / D, None, ALU.mult), deps=(hq,))
        hv2 = p.op("vector", lambda e: e.tensor_scalar(hsum[0:3, 2:3], hsum[0:3, 2:3],
                   hsum[0:3, 3:4], EPS, ALU.subtract, ALU.add), deps=(hv, hms))
        hln = p.op("scalar", lambda e: e.activation(hsum[0:3, 3:4], hsum[0:3, 2:3],
                   AF.Ln), deps=(hv2, hms))
        hrs = p.op("scalar", lambda e: e.activation(hstat[0:3, 1:2], hsum[0:3, 3:4],
                   AF.Exp, scale=-0.5), deps=(hln,))
        hnrm = p.op("vector", lambda e: e.tensor_scalar(
            htm.ap()[0:3, :], xh3[:], hstat[0:3, 0:1], hstat[0:3, 1:2],
            ALU.subtract, ALU.mult),
            deps=(hmu, hrs, prev_half_ev), tag="halo_nrm")
        halo_evs = []
        for dti in range(16):
            _, tr = self._tr_only(ptr[:, 0:3],
                                  htm.ap()[0:3, dti * 128:(dti + 1) * 128],
                                  deps=(hnrm, prev_half_ev), npart=3)
            ev = p.op("vector", lambda e, dti=dti: e.tensor_copy(
                xnv[:, dti, 1:4], ptr[:, 0:3]), deps=(tr,), tag="halo_ev")
            prev_half_ev = ev
            halo_evs.append(ev)
        xn_ready += halo_evs
        cscr = self.sb("cscr", [128, 1024], F32)
        cscr2 = self.sb("cscr2", [128, 1024], F32)
        xcv = xc_buf.ap().rearrange("p (nd w) -> p nd w", nd=16)
        xc_ready = []
        prev_c = []
        for dt in range(16):
            wc = [colpack.ap()[:, dt * NCP + j:dt * NCP + j + 1] for j in range(4)]
            cb = colpack.ap()[:, dt * NCP + 4:dt * NCP + 5]
            dps = tuple(xn_ready) + (colpack_dep,) + tuple(prev_c)
            c1 = p.op("vector", lambda e, dt=dt, wc=wc: e.tensor_scalar_mul(
                cscr[:], xnv[:, dt, 1:1025], wc[0]), deps=dps, tag="conv1")
            c2 = p.op("vector", lambda e, dt=dt, wc=wc: e.scalar_tensor_tensor(
                cscr2[:], xnv[:, dt, 2:1026], wc[1], cscr[:], ALU.mult, ALU.add),
                deps=(c1,) + tuple(prev_c), tag="conv2")
            c3 = p.op("vector", lambda e, dt=dt, wc=wc: e.scalar_tensor_tensor(
                cscr[:], xnv[:, dt, 3:1027], wc[2], cscr2[:], ALU.mult, ALU.add),
                deps=(c2,), tag="conv3")
            c4 = p.op("vector", lambda e, dt=dt, wc=wc: e.scalar_tensor_tensor(
                cscr2[:], xnv[:, dt, 4:1028], wc[3], cscr[:], ALU.mult, ALU.add),
                deps=(c3,), tag="conv4")
            sil = p.op("scalar", lambda e, dt=dt, cb=cb: e.activation(
                xcv[:, dt, 0:1024], cscr2[:], AF.Silu, bias=cb),
                deps=(c4,), tag="conv_silu")
            prev_c = [sil, c4]
            xc_ready.append(sil)
        wgslab = self.sb("wgslab", [128, 256], BF16)
        dwg = p.dma("gpsimd", lambda e: e.dma_start(out=wgslab[:], in_=wg_pk[0]),
                    tag="wg_dma")
        pwg = self.ps("pwg", [16, 512], F32)
        zer16 = scratch.ap()[0:16, 0:1024]
        cumP = scratch.ap()[0:16, 1024:2048]
        zmem = p.op("vector", lambda e: e.memset(zer16, 0.0),
                    deps=(hq,), tag="z16")
        gam_evs = []
        for c in range(2):
            mmg = None
            for k in range(16):
                mmg = p.op("tensor", lambda e, k=k, c=c: e.matmul(
                    pwg[:, 0:512], wgslab[:, k * 16:(k + 1) * 16],
                    xcv[:, k, c * 512:(c + 1) * 512],
                    start=(k == 0), stop=(k == 15)),
                    deps=(dwg,) + tuple(xc_ready) + tuple(gam_evs), tag="mm_wg")
            sg = p.op("scalar", lambda e, c=c: e.activation(
                gamma_sb[:, c * 512:(c + 1) * 512], pwg[0:16, 0:512], AF.Sigmoid,
                bias=bg[:]), deps=(mmg, bg_dep), tag="gamma_sig")
            gam_evs.append(sg)
        cpr = p.op("vector", lambda e: e.tensor_tensor_scan(
            cumP, gamma_sb.ap()[0:16, :], zer16, 1.0, ALU.mult, ALU.add),
            deps=tuple(gam_evs) + (zmem,), tag="cumP")
        tp = p.op("vector", lambda e: e.tensor_copy(totalP_sb[:], cumP[:, 1023:1024]),
                  deps=(cpr,), tag="totalP")
        pK = self.ps("pK", [128, 512], F32)
        pV = self.ps("pV", [128, 512], F32)
        pIG = self.ps("pIG", [128, 512], F32)
        pQ = self.ps("pQ", [128, 512], F32)
        ptrA = self.ps("ptrA", [128, 512], F32)
        pGB = self.ps("pGB", [128, 512], F32)
        k_ts = [self.sb("k_t", [128, 1024], F32),
                scratch.ap()[:, 0:1024]]
        v_ts = [self.sb("v_t", [128, 1024], BF16),
                xn_tm.ap()[:, 0:1024]]
        ig_ts = [self.sb("ig_t", [128, 1024], BF16),
                 xn_tm.ap()[:, 1024:2048]]
        qfs = [self.sb("qf", [128, 1024], F32),
               scratch.ap()[:, 1024:2048]]
        gb = self.sb("gb", [128, 1024], F32)
        memwork = self.sb("memw", [128, 2 * 1024], BF16)
        memwork_free = [[], []]
        stA = self.sb("stA", [128, 512], F32)
        stB = self.sb("stB", [128, 512], F32)
        sqsc = self.sb("sqsc", [128, 128], F32)
        l2c = self.sb("l2c", [128, 12], F32)
        Mcols = self.sb("Mcols", [128, 16], F32)
        slabs = {}
        for nm in ("k", "v"):
            slabs[nm] = [self.sb(f"{nm}slab", [128, 2048], BF16) for _ in range(2)]
        for nm in ("ig", "q"):
            b0 = self.sb(f"{nm}slab", [128, 2048], BF16)
            slabs[nm] = [b0, b0]
        pk_src = {"k": wk_pk, "v": wv_pk, "ig": wig_pk, "q": wq_pk}
        last_mm = {("k", 0): None, ("k", 1): None, ("v", 0): None, ("v", 1): None,
                   ("ig", 0): None, ("ig", 1): None, ("q", 0): None, ("q", 1): None}
        psum_of = {"k": pK, "v": pV, "ig": pIG, "q": pQ}
        ev_of_prev = {}
        slot_last = [None] * 4
        scan_prev = {0: None, 1: None}
        mem_evs = []
        ptrA_free = None
        stA_war = []
        stB_war = []
        sq_prev = None
        reader_war = {(n, par): [tp, xn_ready[-1]]
                      for n in ("k", "v", "ig", "q") for par in (0, 1)}
        for h in range(16):
            par = h % 2
            k_t, v_t, ig_t, qf = k_ts[par], v_ts[par], ig_ts[par], qfs[par]
            b_scr = qf
            dmas = {}
            bs_of = {nm: (h % 2 if nm in ("k", "v") else 0) for nm in ("k", "v", "ig", "q")}
            for nm in ("k", "v", "ig", "q"):
                bslot = bs_of[nm]
                war = (last_mm[(nm, bslot)],) if last_mm[(nm, bslot)] else ()
                dmas[nm] = p.dma("gpsimd", lambda e, nm=nm, h=h, bslot=bslot:
                                 e.dma_start(out=slabs[nm][bslot][:], in_=pk_src[nm][h]),
                                 deps=war, tag=f"slab_{nm}{h}")
            evs = {}
            for c in range(2):
                for nm in ("k", "v", "q", "ig"):
                    mm = None
                    bslot = bs_of[nm]
                    war = (ev_of_prev[nm],) if nm in ev_of_prev else ()
                    rhs_v, rhs_off = (xcv, 0) if nm == "ig" else (xnv, 4)
                    rdy = tuple(xc_ready) if nm == "ig" else tuple(xn_ready)
                    for k in range(16):
                        mm = p.op("tensor", lambda e, nm=nm, k=k, c=c, bslot=bslot,
                                  rhs_v=rhs_v, rhs_off=rhs_off:
                                  e.matmul(psum_of[nm][:, 0:512],
                                           slabs[nm][bslot][:, k * 128:(k + 1) * 128],
                                           rhs_v[:, k, rhs_off + c * 512:rhs_off + (c + 1) * 512],
                                           start=(k == 0), stop=(k == 15)),
                                  deps=(dmas[nm],) + rdy + (war if k == 0 else ()),
                                  tag=f"mm_{nm}{h}_{c}")
                    last_mm[(nm, bslot)] = mm
                    if nm == "ig":
                        bigcol = colpack.ap()[:, h * NCP + 5:h * NCP + 6]
                        ev = p.op("scalar", lambda e, c=c, bigcol=bigcol, ig_t=ig_t:
                            e.activation(
                            ig_t[:, c * 512:(c + 1) * 512], pIG[:, 0:512], AF.Sigmoid,
                            bias=bigcol),
                            deps=(mm, colpack_dep) + tuple(reader_war[(nm, par)]),
                            tag="ig_sig")
                    else:
                        dst = {"k": k_t, "v": v_t, "q": qf}[nm]
                        ev = p.op("vector", lambda e, c=c, dst=dst, nm=nm: e.tensor_copy(
                            dst[:, c * 512:(c + 1) * 512], psum_of[nm][:, 0:512]),
                            deps=(mm,) + tuple(reader_war[(nm, par)]), tag=f"ev_{nm}")
                    ev_of_prev[nm] = ev
                    evs[(nm, c)] = ev
            kq_done = []
            for nm in ("k", "q"):
                src = k_t if nm == "k" else qf
                for ch in range(2):
                    trs = []
                    for i in range(4):
                        t = ch * 4 + i
                        war = (ptrA_free,) if (i == 0 and ptrA_free) else ()
                        _, tr = self._tr_only(ptrA[:, i * 128:(i + 1) * 128],
                                              src[:, t * 128:(t + 1) * 128],
                                              deps=(evs[(nm, t // 4)],) + war,
                                              bf16=False)
                        trs.append(tr)
                    ev4 = p.op("vector", lambda e: e.tensor_copy(stA[:], ptrA[:, 0:512]),
                               deps=tuple(trs) + tuple(stA_war), tag="l2_ev4")
                    ptrA_free = ev4
                    sqs = []
                    prev = sq_prev
                    for i in range(4):
                        sq = p.op("scalar", lambda e, i=i: e.activation(
                            sqsc[:], stA[:, i * 128:(i + 1) * 128], AF.Square,
                            accum_out=l2c[:, i:i + 1]),
                            deps=(ev4,) + ((prev,) if prev else ()), tag="l2_sq")
                        prev = sq
                        sqs.append(sq)
                    sq_prev = prev
                    sqr = p.op("scalar", lambda e: e.activation(
                        l2c[:, 4:8], l2c[:, 0:4], AF.Sqrt), deps=tuple(sqs), tag="l2_sqrt")
                    mx = p.op("vector", lambda e: e.tensor_scalar_max(
                        l2c[:, 8:12], l2c[:, 4:8], 1e-12), deps=(sqr,))
                    rc = p.op("vector", lambda e: e.reciprocal(
                        l2c[:, 8:12], l2c[:, 8:12]), deps=(mx,))
                    if nm == "k":
                        nss = []
                        for i in range(4):
                            ns = p.op("vector", lambda e, i=i: e.tensor_scalar_mul(
                                stB[:, i * 128:(i + 1) * 128],
                                stA[:, i * 128:(i + 1) * 128], l2c[:, 8 + i:9 + i]),
                                deps=(rc,) + tuple(stB_war), tag="l2_mul")
                            nss.append(ns)
                        stA_war = nss
                        trbs = []
                        for i in range(4):
                            war = (ptrA_free,) if (i == 0) else ()
                            _, trb = self._tr_only(ptrA[:, i * 128:(i + 1) * 128],
                                                   stB[:, i * 128:(i + 1) * 128],
                                                   deps=(nss[i],) + war, bf16=False)
                            trbs.append(trb)
                        evb = p.op("vector", lambda e, ch=ch, k_t=k_t: e.tensor_copy(
                            k_t[:, ch * 512:(ch + 1) * 512], ptrA[:, 0:512]),
                            deps=tuple(trbs), tag="l2_evb")
                        ptrA_free = evb
                        stB_war = [evb]
                        kq_done.append(evb)
                    else:
                        nss = []
                        for i in range(4):
                            t = ch * 4 + i
                            ns = p.op("vector", lambda e, t=t, i=i, h=h: e.tensor_scalar_mul(
                                qhat_buf.ap()[:, t * D + h * 128:t * D + (h + 1) * 128],
                                stA[:, i * 128:(i + 1) * 128], l2c[:, 8 + i:9 + i]),
                                deps=(rc,), tag="qhat_mul")
                            nss.append(ns)
                        stA_war = nss
                        kq_done += nss
            b1 = p.op("vector", lambda e, b_scr=b_scr, ig_t=ig_t, k_t=k_t:
                      e.tensor_mul(b_scr[:], ig_t[:], k_t[:]),
                      deps=tuple(kq_done) + (evs[("ig", 0)], evs[("ig", 1)]) +
                      ((scan_prev[par],) if scan_prev[par] else ()), tag="b1")
            b2 = p.op("vector", lambda e, b_scr=b_scr, v_t=v_t:
                      e.tensor_mul(b_scr[:], b_scr[:], v_t[:]),
                      deps=(b1, evs[("v", 0)], evs[("v", 1)]), tag="b2")
            gbe = []
            for c in range(2):
                mmb = p.op("tensor", lambda e, c=c, h=h: e.matmul(
                    pGB[:, 0:512],
                    onehot[:, h * 128:(h + 1) * 128],
                    gamma_sb.ap()[0:16, c * 512:(c + 1) * 512], start=True, stop=True),
                    deps=tuple(gam_evs) + (onehot_dep,) +
                    ((gb_prev,) if h > 0 or c > 0 else ()), tag="mm_gb")
                ge = p.op("vector", lambda e, c=c: e.tensor_copy(
                    gb[:, c * 512:(c + 1) * 512], pGB[:, 0:512]), deps=(mmb,), tag="gb_ev")
                gb_prev = ge
                gbe.append(ge)
            dbg_ops = []
            ms = h % 2
            war_m = tuple(memwork_free[ms])
            sc = p.op("vector", lambda e, h=h, ms=ms, b_scr=b_scr: e.tensor_tensor_scan(
                memwork.ap()[:, ms * 1024:(ms + 1) * 1024], gb[:], b_scr[:], 0.0,
                ALU.mult, ALU.add), deps=(b2,) + tuple(gbe) + war_m, tag=f"scan{h}")
            scan_prev[par] = sc
            mc = p.op("vector", lambda e, h=h, ms=ms: e.tensor_copy(
                Mcols[:, h:h + 1],
                memwork.ap()[:, ms * 1024 + 1023:ms * 1024 + 1024]),
                deps=(sc,), tag="Mcol")
            dmem = p.dma("sync", lambda e, h=h, ms=ms: e.dma_start(
                out=mem_dram[:, h * 1024:(h + 1) * 1024],
                in_=memwork.ap()[:, ms * 1024:(ms + 1) * 1024]),
                deps=(sc,), tag=f"mem_st{h}")
            memwork_free[ms] = [mc, dmem]
            mem_evs.append(mc)
            mem_evs.append(dmem)
            reader_war[("k", par)] = [b1] + dbg_ops
            reader_war[("v", par)] = [b2] + dbg_ops
            reader_war[("ig", par)] = [b1] + dbg_ops
            reader_war[("q", par)] = [sc] + dbg_ops
        if not mem_evs:
            mem_evs = [tp]
        _, trM = self._tr_only(pwg[:, 0:128], Mcols[:], deps=tuple(mem_evs), bf16=False)
        evM = p.op("vector", lambda e: e.tensor_copy(Mrow_sb[:], pwg[0:16, 0:128]),
                   deps=(trM,), tag="Mrow")
        return {"mem": mem_evs, "qhat": mem_evs, "xc": xc_ready,
                "gamma": gam_evs, "totalP": [tp], "Mrow": [evM]}

    def emit_l0attn_B(self, x_src, x_dst, mem_buf, mem_dep, qhat_buf, qhat_dep,
                      xc_buf, xc_dep, gamma_sb, gamma_dep, allP, allP_dep,
                      allM, allM_dep, mmask, mmask_dep, ommask, ommask_dep,
                      onehot, onehot_dep, ones8, ones8_dep, colpack, colpack_dep,
                      mnw, mnw_dep, mnb, mnb_dep, wog_pk, wo_pk, NCP=10):
        p = self.p
        T = T_CORE
        ptrA = self.ps("ptrA", [128, 512], F32)
        pGB = self.ps("pGB", [128, 512], F32)
        pO = [self.ps("pO", [128, 512], F32) for _ in range(2)]
        pOG = self.ps("pOG", [128, 512], F32)
        pW = self.ps("pW", [128, 512], F32)
        ptrB = self.ps("ptrB", [128, 1024], BF16)
        P_T = self.sb("P_T", [16, 8], F32)
        S = self.sb("S", [16, 9], F32)
        W = self.sb("W", [8, D], F32)
        carry = self.sb("carry", [128, 16], F32)
        gb = self.sb("gb", [128, 1024], F32)
        z = self.sb("z", [128, 1024], F32)
        zeros = self.sb("zeros", [128, 1024], BF16)
        og2 = [self.sb("og", [128, 1024], BF16) for _ in range(2)]
        stA = self.sb("stA", [128, 512], F32)
        stB = self.sb("stB", [128, 512], F32)
        sqsc = self.sb("sqsc", [128, 128], F32)
        g1c = self.sb("g1c", [128, 40], F32)
        ofm = xc_buf
        wogbuf = [self.sb("wogb", [128, D], BF16) for _ in range(2)]
        wobuf = [self.sb("wob", [128, D], BF16) for _ in range(2)]
        xbuf = self.sb("xbuf", [128, 2 * D], F32)
        ptrB_stage = self.sb("ptrBst", [128, 2048], BF16)
        stage_war = [[], []]

        _, trP = self._tr_only(pW[0:16, 0:8], allP[0:8, 0:16], deps=(allP_dep,),
                               bf16=False, npart=8)
        evP = p.op("vector", lambda e: e.tensor_copy(P_T[:], pW[0:16, 0:8]),
                   deps=(trP,), tag="evP")
        q1 = p.op("vector", lambda e: e.tensor_mul(S[:, 0:8], P_T[:], mmask[:]),
                  deps=(evP, mmask_dep), tag="Q1")
        q2 = p.op("vector", lambda e: e.tensor_add(S[:, 0:8], S[:, 0:8], ommask[:]),
                  deps=(q1, ommask_dep), tag="Q2")
        sset = p.op("vector", lambda e: e.memset(S[:, 8:9], 1.0), tag="S1")
        prev = q2
        sufs = [sset, q2]
        for j in range(7, -1, -1):
            prev = p.op("vector", lambda e, j=j: e.tensor_mul(
                S[:, j:j + 1], S[:, j + 1:j + 2],
                S[:, j:j + 1]), deps=(prev, sset), tag=f"suf{j}")
            sufs.append(prev)
        selc = p.op("vector", lambda e: e.tensor_mul(P_T[:], S[:, 1:9], mmask[:]),
                    deps=(prev, mmask_dep), tag="selc")
        wmm = []
        prev_evW = None
        for c4 in range(4):
            mmw = p.op("tensor", lambda e, c4=c4: e.matmul(
                pW[0:8, 0:512], P_T[0:16, 0:8], onehot[:, c4 * 512:(c4 + 1) * 512],
                start=True, stop=True),
                deps=(selc, onehot_dep, trP) + ((prev_evW,) if prev_evW else ()),
                tag="mmW")
            evw = p.op("vector", lambda e, c4=c4: e.tensor_copy(
                W.ap()[:, c4 * 512:(c4 + 1) * 512], pW[0:8, 0:512]),
                deps=(mmw,), tag="evW")
            prev_evW = evw
            wmm.append(evw)
        wm = p.op("vector", lambda e: e.tensor_mul(W[:], W[:], allM[:]),
                  deps=tuple(wmm) + (allM_dep,), tag="WM")
        ccm = []
        for h in range(16):
            mmc = p.op("tensor", lambda e, h=h: e.matmul(
                pW[0:128, h:h + 1], W.ap()[0:8, h * 128:(h + 1) * 128], ones8[:],
                start=True, stop=True),
                deps=(wm, ones8_dep) + ((prev_evW,) if h == 0 else ()), tag="mmcc")
            ccm.append(mmc)
        evcc = p.op("vector", lambda e: e.tensor_copy(carry[:], pW[:, 0:16]),
                    deps=tuple(ccm), tag="evcc")
        zm = p.op("vector", lambda e: e.memset(zeros[:], 0.0), tag="zeros")

        wog_last = [None, None]
        wo_last = [None, None]
        gb_prev = None
        ptrA_free = None
        stA_war = []
        stB_war = []
        og_war = [[], []]
        pog_rd = []
        sq_prev = None
        prewo_done = []
        mem_free = {}
        z_readers = []
        for h in range(16):
            bs = h % 2
            war = (wog_last[bs],) if wog_last[bs] else ()
            dwog = p.dma("gpsimd", lambda e, h=h, bs=bs: e.dma_start(
                out=wogbuf[bs][:], in_=wog_pk[h]), deps=war, tag=f"wog{h}")
            og = og2[h % 2]
            oge = []
            og_prev_sig = og_war[h % 2]
            for c in range(2):
                mm = None
                for k in range(16):
                    mm = p.op("tensor", lambda e, k=k, c=c, bs=bs: e.matmul(
                        pOG[:, 0:512], wogbuf[bs][:, k * 128:(k + 1) * 128],
                        xc_buf.ap().rearrange("p (nd w) -> p nd w", nd=16)[
                            :, k, c * 512:(c + 1) * 512],
                        start=(k == 0), stop=(k == 15)),
                        deps=(dwog, xc_dep) +
                        ((tuple(og_prev_sig) + tuple(pog_rd)) if k == 0 else ()),
                        tag=f"mmog{h}_{c}")
                bogcol = colpack.ap()[:, h * NCP + 6:h * NCP + 7]
                sg = p.op("scalar", lambda e, c=c, bogcol=bogcol, og=og: e.activation(
                    og[:, c * 512:(c + 1) * 512], pOG[:, 0:512], AF.Sigmoid,
                    bias=bogcol), deps=(mm, colpack_dep), tag="og_sig")
                pog_rd = [sg]
                oge.append(sg)
            wog_last[bs] = oge[-1]
            gbe = []
            for c in range(2):
                mmb = p.op("tensor", lambda e, c=c, h=h: e.matmul(
                    pGB[:, 0:512], onehot[:, h * 128:(h + 1) * 128],
                    gamma_sb.ap()[0:16, c * 512:(c + 1) * 512], start=True, stop=True),
                    deps=(gamma_dep, onehot_dep) + ((gb_prev,) if gb_prev else ()),
                    tag="mm_gb")
                ge = p.op("vector", lambda e, c=c: e.tensor_copy(
                    gb[:, c * 512:(c + 1) * 512], pGB[:, 0:512]), deps=(mmb,),
                    tag="gb_ev")
                gb_prev = ge
                gbe.append(ge)
            zdeps = tuple(gbe) + (evcc, zm) + tuple(z_readers)
            zs = p.op("vector", lambda e, h=h: e.tensor_tensor_scan(
                z[:], gb[:], zeros[:], carry[:, h:h + 1], ALU.mult, ALU.add),
                deps=zdeps, tag=f"zscan{h}")
            mf = p.op("vector", lambda e, h=h: e.tensor_add(
                z[:], mem_buf.ap()[:, h * 1024:(h + 1) * 1024], z[:]),
                deps=(zs, mem_dep), tag="memf")
            uch = []
            for ch in range(2):
                trs = []
                for i in range(4):
                    t = ch * 4 + i
                    war = (ptrA_free,) if (i == 0 and ptrA_free) else ()
                    _, tr = self._tr_only(ptrA[:, i * 128:(i + 1) * 128],
                                          z[:, t * 128:(t + 1) * 128],
                                          deps=(mf,) + war, bf16=False)
                    trs.append(tr)
                ev4 = p.op("vector", lambda e: e.tensor_copy(stA[:], ptrA[:, 0:512]),
                           deps=tuple(trs) + tuple(stA_war), tag="gn_ev4")
                ptrA_free = ev4
                if ch == 0:
                    z_readers = []
                z_readers.append(ev4)
                sums = []
                prevq = sq_prev
                for i in range(4):
                    ds = p.op("vector", lambda e, i=i: e.reduce_sum(
                        g1c[:, i:i + 1], stA[:, i * 128:(i + 1) * 128], axis=AX.X),
                        deps=(ev4,), tag="gn1_sum")
                    dq = p.op("scalar", lambda e, i=i: e.activation(
                        sqsc[:], stA[:, i * 128:(i + 1) * 128], AF.Square,
                        accum_out=g1c[:, 4 + i:5 + i]),
                        deps=(ev4,) + ((prevq,) if prevq else ()), tag="gn1_sq")
                    prevq = dq
                    sums += [ds, dq]
                sq_prev = prevq
                st1 = self._gn_stats(g1c, 0, sums)
                ops_i = []
                for i in range(4):
                    t = ch * 4 + i
                    n1 = p.op("vector", lambda e, i=i: e.tensor_scalar(
                        stB[:, i * 128:(i + 1) * 128], stA[:, i * 128:(i + 1) * 128],
                        g1c[:, 8 + i:9 + i], g1c[:, 12 + i:13 + i],
                        ALU.subtract, ALU.mult),
                        deps=(st1,) + tuple(stB_war), tag="gn1_n")
                    a1 = p.op("vector", lambda e, i=i, h=h: e.tensor_mul(
                        stB[:, i * 128:(i + 1) * 128], stB[:, i * 128:(i + 1) * 128],
                        mnw.ap()[:, h * 128:(h + 1) * 128]), deps=(n1, mnw_dep))
                    a2 = p.op("vector", lambda e, i=i, h=h: e.tensor_add(
                        stB[:, i * 128:(i + 1) * 128], stB[:, i * 128:(i + 1) * 128],
                        mnb.ap()[:, h * 128:(h + 1) * 128]), deps=(a1, mnb_dep))
                    qm = p.op("vector", lambda e, i=i, t=t, h=h: e.tensor_mul(
                        stB[:, i * 128:(i + 1) * 128], stB[:, i * 128:(i + 1) * 128],
                        qhat_buf.ap()[:, t * D + h * 128:t * D + (h + 1) * 128]),
                        deps=(a2, qhat_dep), tag="qmul")
                    ops_i.append(qm)
                sums2 = []
                prevq = sq_prev
                for i in range(4):
                    ds = p.op("vector", lambda e, i=i: e.reduce_sum(
                        g1c[:, 16 + i:17 + i], stB[:, i * 128:(i + 1) * 128],
                        axis=AX.X), deps=(ops_i[i],), tag="gn2_sum")
                    dq = p.op("scalar", lambda e, i=i: e.activation(
                        sqsc[:], stB[:, i * 128:(i + 1) * 128], AF.Square,
                        accum_out=g1c[:, 20 + i:21 + i]),
                        deps=(ops_i[i],) + ((prevq,) if prevq else ()), tag="gn2_sq")
                    prevq = dq
                    sums2 += [ds, dq]
                sq_prev = prevq
                st2 = self._gn_stats(g1c, 16, sums2)
                trbs = []
                for i in range(4):
                    n2 = p.op("vector", lambda e, i=i: e.tensor_scalar(
                        stB[:, i * 128:(i + 1) * 128], stB[:, i * 128:(i + 1) * 128],
                        g1c[:, 24 + i:25 + i], g1c[:, 28 + i:29 + i],
                        ALU.subtract, ALU.mult), deps=(st2,), tag="gn2_n")
                    war = (ptrA_free,) if (i == 0) else ()
                    _, trb = self._tr_only(ptrA[:, i * 128:(i + 1) * 128],
                                           stB[:, i * 128:(i + 1) * 128],
                                           deps=(n2,) + war, bf16=False)
                    trbs.append(trb)
                gnw = colpack.ap()[:, h * NCP + 8:h * NCP + 9]
                gnb = colpack.ap()[:, h * NCP + 9:h * NCP + 10]
                evb = p.op("scalar", lambda e, ch=ch, gnw=gnw, gnb=gnb: e.activation(
                    stA[:, 0:512], ptrA[:, 0:512], AF.Identity,
                    bias=gnb, scale=gnw),
                    deps=tuple(trbs) + (colpack_dep,), tag="gn_aff")
                ptrA_free = evb
                stA_war = [evb]
                stB_war = [evb]
                pw = p.op("vector", lambda e, ch=ch, h=h, og=og: e.tensor_mul(
                    mem_buf.ap()[:, h * 1024 + ch * 512:h * 1024 + (ch + 1) * 512],
                    stA[:, 0:512], og[:, ch * 512:(ch + 1) * 512]),
                    deps=(evb, oge[ch], mf), tag="prewo")
                stA_war.append(pw)
                uch.append(pw)
            mem_free[h] = uch
            og_war[h % 2] = oge + uch
            prewo_done += uch
        ofm_done = []
        pO_read = [None, None]
        for d in range(16):
            bs = d % 2
            war = (wo_last[bs],) if wo_last[bs] else ()
            dwo = p.dma("gpsimd", lambda e, d=d, bs=bs: e.dma_start(
                out=wobuf[bs][:], in_=wo_pk[d]), deps=war, tag=f"wo{d}")
            for c in range(2):
                mm = None
                for k in range(16):
                    mm = p.op("tensor", lambda e, k=k, c=c, bs=bs: e.matmul(
                        pO[c][:, 0:512], wobuf[bs][:, k * 128:(k + 1) * 128],
                        mem_buf.ap()[:, k * 1024 + c * 512:k * 1024 + (c + 1) * 512],
                        start=(k == 0), stop=(k == 15)),
                        deps=(dwo,) + tuple(prewo_done) +
                        ((pO_read[c],) if (k == 0 and pO_read[c]) else ()),
                        tag=f"mmo{d}_{c}")
                evo = p.op("vector", lambda e, d=d, c=c: e.tensor_copy(
                    ofm.ap()[:, d * 1024 + c * 512:d * 1024 + (c + 1) * 512],
                    pO[c][:, 0:512]), deps=(mm,), tag="evo")
                pO_read[c] = evo
                ofm_done.append(evo)
            wo_last[bs] = mm
        stores = []
        xslot_free = [[], []]
        ptrB_free = None
        for t in range(8):
            sl = t % 2
            xt = xbuf.ap()[:, sl * D:(sl + 1) * D]
            dl = p.dma("sync", lambda e, t=t, xt=xt: e.dma_start(
                out=xt, in_=x_src[t * 128:(t + 1) * 128, :]),
                deps=tuple(xslot_free[sl]), tag=f"xl{t}")
            adds = []
            for half in range(2):
                trs = []
                for i in range(8):
                    d = half * 8 + i
                    war = (ptrB_free,) if (i == 0 and ptrB_free) else ()
                    _, tr = self._tr_only(ptrB[:, i * 128:(i + 1) * 128],
                                          ofm.ap()[:, d * 1024 + t * 128:
                                                   d * 1024 + (t + 1) * 128],
                                          deps=(ofm_done[2 * d + t // 4],) + war,
                                          bf16=True)
                    trs.append(tr)
                ev4 = p.op("vector", lambda e, half=half: e.tensor_copy(
                    ptrB_stage[:, half * 1024:(half + 1) * 1024], ptrB[:, 0:1024]),
                    deps=tuple(trs) + tuple(stage_war[half]), tag="oev")
                ptrB_free = ev4
                ad = p.op("vector", lambda e, xt=xt, half=half: e.tensor_add(
                    xt[:, half * 1024:(half + 1) * 1024],
                    xt[:, half * 1024:(half + 1) * 1024],
                    ptrB_stage[:, half * 1024:(half + 1) * 1024]),
                    deps=(ev4, dl), tag="oadd")
                stage_war[half] = [ad]
                adds.append(ad)
            st = p.dma("sync", lambda e, t=t, xt=xt: e.dma_start(
                out=x_dst[t * 128:(t + 1) * 128, :], in_=xt),
                deps=tuple(adds), tag=f"xs{t}")
            xslot_free[sl] = [st]
            stores.append(st)
        return stores

    def _gn_stats(self, g1c, base, dep_ops):
        p = self.p
        prevg = getattr(self, "_gn_prev", ())
        mu = p.op("vector", lambda e: e.tensor_scalar_mul(
            g1c[:, base + 8:base + 12], g1c[:, base:base + 4], 1.0 / 128),
            deps=tuple(dep_ops), tag="gn_mu")
        vv = p.op("vector", lambda e: e.tensor_scalar_mul(
            g1c[:, 32:36], g1c[:, base + 4:base + 8], 1.0 / 128),
            deps=tuple(dep_ops) + tuple(prevg), tag="gn_v")
        ms = p.op("vector", lambda e: e.tensor_mul(
            g1c[:, 36:40], g1c[:, base + 8:base + 12], g1c[:, base + 8:base + 12]),
            deps=(mu,) + tuple(prevg), tag="gn_ms")
        v2 = p.op("vector", lambda e: e.tensor_sub(
            g1c[:, 32:36], g1c[:, 32:36], g1c[:, 36:40]), deps=(vv, ms), tag="gn_v2")
        v3 = p.op("vector", lambda e: e.tensor_scalar(
            g1c[:, 32:36], g1c[:, 32:36], EPS, None, ALU.add), deps=(v2,), tag="gn_v3")
        ln_ = p.op("scalar", lambda e: e.activation(
            g1c[:, 36:40], g1c[:, 32:36], AF.Ln), deps=(v3, ms), tag="gn_ln")
        rstd = p.op("scalar", lambda e: e.activation(
            g1c[:, base + 12:base + 16], g1c[:, 36:40], AF.Exp, scale=-0.5),
            deps=(ln_,), tag="gn_rstd")
        fin = p.op("vector", lambda e: e.engine_nop(), deps=(mu, rstd), tag="gn_fin")
        self._gn_prev = (ln_, rstd)
        return fin

    def emit_l1ffn(self, x_src, x_dst, wa_pk, rg_pk, rw1_pk, rw2_pk,
                   rgb_sb, rgb_dep, rb1_sb, rb1_dep, rbc_sb, rbc_dep,
                   oh18, oh18_dep, coeff=0.5):
        p = self.p
        T = T_CORE
        n_t = 8
        xn_fm = self.sb("xnfm", [128, 16 * T], BF16)
        xnv = xn_fm.ap().rearrange("p (nd tt) -> p nd tt", nd=16)
        xbuf = self.sb("xbuf", [128, 2 * D], F32)
        xslot = [xbuf.ap()[:, 0:D], xbuf.ap()[:, D:2 * D]]
        stats = self.sb("lnstats", [128, 2 * n_t], F32)
        xn_ready, _, lnptr = self._ln_stream(x_src, xslot, xn_fm, n_t, stats)
        pS = self.ps("pS", [128, 512], F32)
        pC = pS
        ptrC = self.ps("ptrC", [128, 512], F32)
        aff = self.sb("aff", [16, T], F32)
        cfull = self.sb("cfull", [18, T], F32)
        tk = self.sb("tk", [128, 28], F32)
        ctm = self.sb("ctm", [128, 16], F32)
        wa_sb = self.sb("wa_sb", [128, 256], BF16)
        dwa = p.dma("gpsimd", lambda e: e.dma_start(out=wa_sb[:], in_=wa_pk[0]),
                    tag="wa")
        aff_evs = []
        for c in range(2):
            mm = None
            for k in range(16):
                mm = p.op("tensor", lambda e, k=k, c=c: e.matmul(
                    pS[0:16, 0:512], wa_sb[:, k * 16:(k + 1) * 16],
                    xnv[:, k, c * 512:(c + 1) * 512], start=(k == 0), stop=(k == 15)),
                    deps=(dwa,) + tuple(xn_ready) + tuple(aff_evs), tag="mm_wa")
            sg = p.op("scalar", lambda e, c=c: e.activation(
                aff[:, c * 512:(c + 1) * 512], pS[0:16, 0:512], AF.Sigmoid),
                deps=(mm,), tag="aff_sig")
            aff_evs.append(sg)
        cset = p.op("vector", lambda e: e.memset(cfull.ap()[0:18, :], 1.0),
                    tag="ones2")
        c_evs = [cset]
        trC_free = None
        ctm_war = []
        for tc in range(8):
            war = (trC_free,) if trC_free else ()
            _, tr = self._tr_only(ptrC[:, 0:16], aff.ap()[0:16, tc * 128:(tc + 1) * 128],
                                  deps=tuple(aff_evs) + war, bf16=False, npart=16)
            ev = p.op("vector", lambda e: e.tensor_copy(ctm[:], ptrC[:, 0:16]),
                      deps=(tr,) + tuple(ctm_war), tag="c_ev")
            mx1 = p.op("vector", lambda e: e.reduce_max(tk[:, 0:1], ctm[:], axis=AX.X),
                       deps=(ev,))
            ge1 = p.op("vector", lambda e: e.tensor_single_scalar(
                tk[:, 4:20], ctm[:], tk[:, 0:1], ALU.is_ge), deps=(mx1,))
            am = p.op("vector", lambda e: e.scalar_tensor_tensor(
                tk[:, 4:20], tk[:, 4:20], -1e30, ctm[:], ALU.mult, ALU.add),
                deps=(ge1,))
            mx2 = p.op("vector", lambda e: e.reduce_max(tk[:, 1:2], tk[:, 4:20],
                                                        axis=AX.X), deps=(am,))
            den = p.op("vector", lambda e: e.tensor_add(tk[:, 2:3], tk[:, 0:1],
                                                        tk[:, 1:2]), deps=(mx2,))
            rc = p.op("vector", lambda e: e.reciprocal(tk[:, 3:4], tk[:, 2:3]),
                      deps=(den,))
            ge2 = p.op("vector", lambda e: e.tensor_single_scalar(
                tk[:, 4:20], ctm[:], tk[:, 1:2], ALU.is_ge), deps=(mx2, am))
            cm = p.op("vector", lambda e: e.tensor_mul(tk[:, 4:20], tk[:, 4:20],
                                                       ctm[:]), deps=(ge2,))
            cs = p.op("vector", lambda e: e.tensor_scalar_mul(
                tk[:, 4:20], tk[:, 4:20], tk[:, 3:4]), deps=(cm, rc), tag="c_scale")
            ctm_war = [cs]
            war = (trC_free,) if trC_free else ()
            _, trb = self._tr_only(ptrC[0:16, 128:256], tk[:, 4:20], deps=(cs,) + war,
                                   bf16=False)
            evb = p.op("vector", lambda e, tc=tc: e.tensor_copy(
                cfull.ap()[0:16, tc * 128:(tc + 1) * 128], ptrC[0:16, 128:256]),
                deps=(trb, cset), tag="c_evb")
            trC_free = evb
            c_evs.append(evb)
        acc = self.sb("acc", [128, 16 * T], BF16)
        accv = acc.ap().rearrange("p (nd tt) -> p nd tt", nd=16)
        hs = self.sb("hs", [128, 8 * T], BF16)
        hsv = hs.ap().rearrange("p (nf tt) -> p nf tt", nf=8)
        cb = self.sb("cb", [128, T], BF16)
        gsc = self.sb("gsc", [128, 512], F32)
        wgbuf = [self.sb("wgb", [128, D], BF16) for _ in range(2)]
        w1buf = [self.sb("w1b", [128, D], BF16) for _ in range(2)]
        w2buf = [self.sb("w2b", [128, 1024], BF16) for _ in range(2)]
        pg = self.ps("pg", [128, 512], F32)
        pu = self.ps("pu", [128, 512], F32)
        pd = [self.ps("pdd", [128, 512], F32) for _ in range(2)]
        wg_last = [None, None]
        w1_last = [None, None]
        w2_last = [None, None]
        pg_rd = None
        pu_rd = None
        pd_rd = [None, None]
        cb_rd = []
        gsc_rd = None
        acc_done = []
        acc_last = {}
        hs_rd = []
        for e in range(18):
            routed = e < 16
            if routed:
                cbe = []
                for c in range(2):
                    mmc = p.op("tensor", lambda e_, c=c, e=e: e_.matmul(
                        pC[:, 0:512], oh18[:, e * 128:(e + 1) * 128],
                        cfull.ap()[0:18, c * 512:(c + 1) * 512], start=True, stop=True),
                        deps=tuple(c_evs) + tuple(aff_evs) + (oh18_dep,) + tuple(cb_rd),
                        tag="mm_cb")
                    evc = p.op("vector", lambda e_, c=c: e_.tensor_copy(
                        cb[:, c * 512:(c + 1) * 512], pC[:, 0:512]),
                        deps=(mmc,), tag="cb_ev")
                    cb_rd = [evc]
                    cbe.append(evc)
            hw = []
            for ft in range(8):
                bs = ft % 2
                war = (wg_last[bs],) if wg_last[bs] else ()
                dg = p.dma("gpsimd", lambda e_, e=e, ft=ft, bs=bs: e_.dma_start(
                    out=wgbuf[bs][:], in_=rg_pk[e, ft]), deps=war, tag=f"rg{e}_{ft}")
                war = (w1_last[bs],) if w1_last[bs] else ()
                d1 = p.dma("gpsimd", lambda e_, e=e, ft=ft, bs=bs: e_.dma_start(
                    out=w1buf[bs][:], in_=rw1_pk[e, ft]), deps=war, tag=f"rw1{e}_{ft}")
                for c in range(2):
                    xr = tuple(xn_ready[8 * c:8 * c + 8])
                    mmg = None
                    war = (pg_rd,) if pg_rd else ()
                    for k in range(16):
                        mmg = p.op("tensor", lambda e_, k=k, c=c, bs=bs: e_.matmul(
                            pg[:, 0:512], wgbuf[bs][:, k * 128:(k + 1) * 128],
                            xnv[:, k, c * 512:(c + 1) * 512],
                            start=(k == 0), stop=(k == 15)),
                            deps=(dg,) + xr + (war if k == 0 else ()),
                            tag="mm_g")
                    mmu = None
                    war = (pu_rd,) if pu_rd else ()
                    for k in range(16):
                        mmu = p.op("tensor", lambda e_, k=k, c=c, bs=bs: e_.matmul(
                            pu[:, 0:512], w1buf[bs][:, k * 128:(k + 1) * 128],
                            xnv[:, k, c * 512:(c + 1) * 512],
                            start=(k == 0), stop=(k == 15)),
                            deps=(d1,) + xr + (war if k == 0 else ()),
                            tag="mm_u")
                    wg_last[bs] = mmg
                    w1_last[bs] = mmu
                    gcol = rgb_sb.ap()[:, e * 8 + ft:e * 8 + ft + 1]
                    gl = p.op("scalar", lambda e_, c=c, gcol=gcol: e_.activation(
                        gsc[:], pg[:, 0:512], AF.Gelu, bias=gcol),
                        deps=(mmg, rgb_dep) + ((gsc_rd,) if gsc_rd else ()),
                        tag="gelu")
                    pg_rd = gl
                    ucol = rb1_sb.ap()[:, e * 8 + ft:e * 8 + ft + 1]
                    hm = p.op("vector", lambda e_, c=c, ft=ft, ucol=ucol:
                              e_.scalar_tensor_tensor(
                                  hsv[:, ft, c * 512:(c + 1) * 512], pu[:, 0:512],
                                  ucol, gsc[:], ALU.add, ALU.mult),
                              deps=(mmu, gl, rb1_dep) + tuple(hs_rd), tag="hmul")
                    gsc_rd = hm
                    pu_rd = hm
                    if routed:
                        hc = p.op("vector", lambda e_, c=c, ft=ft: e_.tensor_mul(
                            hsv[:, ft, c * 512:(c + 1) * 512],
                            hsv[:, ft, c * 512:(c + 1) * 512],
                            cb[:, c * 512:(c + 1) * 512]),
                            deps=(hm, cbe[c]), tag="hcb")
                        hw.append(hc)
                    else:
                        hw.append(hm)
            hs_rd = []
            for d in range(16):
                bs = d % 2
                war = (w2_last[bs],) if w2_last[bs] else ()
                d2 = p.dma("gpsimd", lambda e_, e=e, d=d, bs=bs: e_.dma_start(
                    out=w2buf[bs][:], in_=rw2_pk[e, d]), deps=war, tag=f"rw2{e}_{d}")
                for c in range(2):
                    mmd = None
                    war = (pd_rd[c],) if pd_rd[c] else ()
                    for k in range(8):
                        mmd = p.op("tensor", lambda e_, k=k, c=c, bs=bs: e_.matmul(
                            pd[c][:, 0:512], w2buf[bs][:, k * 128:(k + 1) * 128],
                            hsv[:, k, c * 512:(c + 1) * 512],
                            start=(k == 0), stop=(k == 7)),
                            deps=(d2,) + tuple(hw) + (war if k == 0 else ()),
                            tag="mm_d")
                    w2_last[bs] = mmd
                    hs_rd.append(mmd)
                    prev = acc_last.get((d, c))
                    if e == 0:
                        ad = p.op("vector", lambda e_, d=d, c=c: e_.tensor_copy(
                            accv[:, d, c * 512:(c + 1) * 512], pd[c][:, 0:512]),
                            deps=(mmd,), tag="acc0")
                    else:
                        ad = p.op("vector", lambda e_, d=d, c=c: e_.tensor_add(
                            accv[:, d, c * 512:(c + 1) * 512],
                            accv[:, d, c * 512:(c + 1) * 512], pd[c][:, 0:512]),
                            deps=(mmd, prev), tag="accA")
                    pd_rd[c] = ad
                    acc_last[(d, c)] = ad
        for d in range(16):
            for c in range(2):
                war = (pd_rd[c],) if pd_rd[c] else ()
                mmb = p.op("tensor", lambda e_, d=d, c=c: e_.matmul(
                    pd[c][:, 0:512], rbc_sb.ap()[0:18, d * 128:(d + 1) * 128],
                    cfull.ap()[0:18, c * 512:(c + 1) * 512], start=True, stop=True),
                    deps=tuple(c_evs) + (rbc_dep,) + war, tag="mm_rbc")
                ad = p.op("vector", lambda e_, d=d, c=c: e_.tensor_add(
                    accv[:, d, c * 512:(c + 1) * 512],
                    accv[:, d, c * 512:(c + 1) * 512], pd[c][:, 0:512]),
                    deps=(mmb, acc_last[(d, c)]), tag="accB")
                pd_rd[c] = ad
                acc_done.append(ad)
        ptrB = lnptr
        stg = self.sb("stg", [128, 2048], BF16)
        xn_scr = self.sb("xnscr", [128, D], F32)
        stores = []
        xslot_free = [list(xn_ready[-2:]), list(xn_ready[-2:])]
        stage_war = [[], []]
        ptrB_free = None
        xn_scr_rd = None
        for t in range(8):
            sl = t % 2
            xt = xslot[sl]
            dl = p.dma("sync", lambda e, t=t, xt=xt: e.dma_start(
                out=xt, in_=x_src[t * 128:(t + 1) * 128, :]),
                deps=tuple(xslot_free[sl]) + tuple(xn_ready), tag=f"xl2_{t}")
            nrm = p.op("vector", lambda e, xt=xt, t=t: e.tensor_scalar(
                xn_scr[:], xt[:], stats[:, 2 * t:2 * t + 1],
                stats[:, 2 * t + 1:2 * t + 2], ALU.subtract, ALU.mult),
                deps=(dl,) + ((xn_scr_rd,) if xn_scr_rd else ()), tag="nrm2")
            ax = p.op("vector", lambda e, xt=xt: e.scalar_tensor_tensor(
                xt[:], xn_scr[:], coeff, xt[:], ALU.mult, ALU.add),
                deps=(nrm,), tag="xn_add")
            xn_scr_rd = ax
            adds = [ax]
            for half in range(2):
                trs = []
                for i in range(8):
                    d = half * 8 + i
                    war = (ptrB_free,) if (i == 0 and ptrB_free) else ()
                    _, tr = self._tr_only(ptrB[:, i * 128:(i + 1) * 128],
                                          accv[:, d, t * 128:(t + 1) * 128],
                                          deps=tuple(acc_done) + war, bf16=True)
                    trs.append(tr)
                ev4 = p.op("vector", lambda e, half=half: e.tensor_copy(
                    stg[:, half * 1024:(half + 1) * 1024], ptrB[:, 0:1024]),
                    deps=tuple(trs) + tuple(stage_war[half]), tag="oev2")
                ptrB_free = ev4
                ad = p.op("vector", lambda e, xt=xt, half=half: e.scalar_tensor_tensor(
                    xt[:, half * 1024:(half + 1) * 1024],
                    stg[:, half * 1024:(half + 1) * 1024], coeff,
                    xt[:, half * 1024:(half + 1) * 1024], ALU.mult, ALU.add),
                    deps=(ev4, ax), tag="oadd2")
                stage_war[half] = [ad]
                adds.append(ad)
            st = p.dma("sync", lambda e, t=t, xt=xt: e.dma_start(
                out=x_dst[t * 128:(t + 1) * 128, :], in_=xt),
                deps=tuple(adds), tag=f"xs2_{t}")
            xslot_free[sl] = [st]
            stores.append(st)
        return stores

    def _ln_stream(self, x_src, xslot, xn_fm, n_t, stats_out):
        p = self.p
        scratch = self.sb("lnscr", [128, D], F32)
        sums_all = self.sb("lnsum", [128, 4 * n_t], F32)
        xn_tm = self.sb("xntm", [128, D], BF16)
        ptr = self.ps("lntr", [128, 1024], BF16)
        xnv = xn_fm.ap().rearrange("p (nd tt) -> p nd tt", nd=16)
        ready = []
        prev_sq = None
        prev_evs = []
        prev_half_ev = None
        xslot_readers = [[], []]
        for t in range(n_t):
            xt = xslot[t % 2]
            dl = p.dma("sync", lambda e, t=t, xt=xt: e.dma_start(
                out=xt, in_=x_src[t * 128:(t + 1) * 128, :]),
                deps=tuple(xslot_readers[t % 2]), tag=f"lnx{t}")
            sums = sums_all.ap()[:, 4 * t:4 * t + 4]
            dsum = p.op("vector", lambda e, xt=xt, sums=sums: e.reduce_sum(
                sums[:, 0:1], xt[:], axis=AX.X), deps=(dl,))
            dsq = p.op("scalar", lambda e, xt=xt, sums=sums: e.activation(
                scratch[:], xt[:], AF.Square, accum_out=sums[:, 1:2]),
                deps=[dl] + ([prev_sq] if prev_sq else []))
            prev_sq = dsq
            mu = p.op("vector", lambda e, t=t, sums=sums: e.tensor_scalar_mul(
                stats_out[:, 2 * t:2 * t + 1], sums[:, 0:1], 1.0 / D), deps=(dsum,))
            musq = p.op("vector", lambda e, sums=sums, t=t: e.tensor_mul(
                sums[:, 3:4], stats_out[:, 2 * t:2 * t + 1],
                stats_out[:, 2 * t:2 * t + 1]), deps=(mu,))
            var = p.op("vector", lambda e, sums=sums: e.tensor_scalar(
                sums[:, 2:3], sums[:, 1:2], 1.0 / D, None, ALU.mult), deps=(dsq,))
            var2 = p.op("vector", lambda e, sums=sums: e.tensor_scalar(
                sums[:, 2:3], sums[:, 2:3], sums[:, 3:4], EPS, ALU.subtract, ALU.add),
                deps=(var, musq))
            ln_ = p.op("scalar", lambda e, sums=sums: e.activation(
                sums[:, 3:4], sums[:, 2:3], AF.Ln), deps=(var2, musq))
            rstd = p.op("scalar", lambda e, t=t, sums=sums: e.activation(
                stats_out[:, 2 * t + 1:2 * t + 2], sums[:, 3:4], AF.Exp, scale=-0.5),
                deps=(ln_,))
            nrm = p.op("vector", lambda e, xt=xt, t=t: e.tensor_scalar(
                xn_tm[:], xt[:], stats_out[:, 2 * t:2 * t + 1],
                stats_out[:, 2 * t + 1:2 * t + 2], ALU.subtract, ALU.mult),
                deps=[mu, rstd] + prev_evs, tag="ln_nrm")
            xslot_readers[t % 2] = [nrm, dsq, dsum]
            prev_evs = []
            for half in range(2):
                trs = []
                for i in range(8):
                    dti = half * 8 + i
                    dps = (nrm,) if (half == 0 and t == 0) else (nrm, prev_half_ev)
                    _, tr = self._tr_only(ptr[:, i * 128:(i + 1) * 128],
                                          xn_tm[:, dti * 128:(dti + 1) * 128],
                                          deps=[dd for dd in dps if dd is not None])
                    trs.append(tr)
                ev = p.op("vector", lambda e, half=half, t=t: e.tensor_copy(
                    xnv[:, half * 8:(half + 1) * 8, t * 128:(t + 1) * 128],
                    ptr[:, 0:1024]), deps=tuple(trs), tag="ln_trev")
                prev_half_ev = ev
                prev_evs.append(ev)
                ready.append(ev)
        return ready, stats_out, ptr

    def emit_l1attn_C(self, x_src, wkv_pk, wcmp_pk, lgam, lgam_dep, lbet,
                      lbet_dep, ch_out):
        p = self.p
        T = T_CORE
        xn_fm = self.sb("xnfm", [128, 16 * T], BF16)
        xnv = xn_fm.ap().rearrange("p (nd tt) -> p nd tt", nd=16)
        xbuf = self.sb("xbuf", [128, 2 * D], F32)
        xslot = [xbuf.ap()[:, 0:D], xbuf.ap()[:, D:2 * D]]
        stats = self.sb("lnstats", [128, 2 * 8], F32)
        xn_ready, _, lnptr = self._ln_stream(x_src, xslot, xn_fm, 8, stats)
        kvl = self.sb("kvl", [128, 4 * T], BF16)
        wkvbuf = [self.sb("wkvb", [128, D], BF16) for _ in range(2)]
        pA = self.ps("pA", [128, 512], F32)
        pB = self.ps("pB", [128, 512], F32)
        kvl_evs = []
        wkv_last = [None, None]
        pA_rd = None
        for o in range(4):
            bs = o % 2
            war = (wkv_last[bs],) if wkv_last[bs] else ()
            dw = p.dma("gpsimd", lambda e, o=o, bs=bs: e.dma_start(
                out=wkvbuf[bs][:], in_=wkv_pk[o]), deps=war, tag=f"wkv{o}")
            for c in range(2):
                mm = None
                war = (pA_rd,) if pA_rd else ()
                for k in range(16):
                    mm = p.op("tensor", lambda e, k=k, c=c, bs=bs: e.matmul(
                        pA[:, 0:512], wkvbuf[bs][:, k * 128:(k + 1) * 128],
                        xnv[:, k, c * 512:(c + 1) * 512],
                        start=(k == 0), stop=(k == 15)),
                        deps=(dw,) + tuple(xn_ready) + (war if k == 0 else ()),
                        tag="mm_kvl")
                wkv_last[bs] = mm
                ev = p.op("vector", lambda e, o=o, c=c: e.tensor_copy(
                    kvl.ap()[:, o * T + c * 512:o * T + (c + 1) * 512], pA[:, 0:512]),
                    deps=(mm,), tag="ev_kvl")
                pA_rd = ev
                kvl_evs.append(ev)
        cf = self.sb("cf", [128, 2 * T], F32)
        wcbuf = self.sb("wcb", [128, 2 * 512], BF16)
        dwc = p.dma("gpsimd", lambda e: e.dma_start(
            out=wcbuf.ap().rearrange("p (a b) -> p a b", a=2),
            in_=wcmp_pk[0:2].rearrange("a p b -> p a b")),
            tag="wcmp")
        pB_rd = None
        cf_evs = []
        for o in range(2):
            for c in range(2):
                mm = None
                war = (pB_rd,) if pB_rd else ()
                for k in range(4):
                    mm = p.op("tensor", lambda e, k=k, c=c, o=o: e.matmul(
                        pB[:, 0:512], wcbuf[:, o * 512 + k * 128:o * 512 + (k + 1) * 128],
                        kvl.ap()[:, k * T + c * 512:k * T + (c + 1) * 512],
                        start=(k == 0), stop=(k == 3)),
                        deps=(dwc,) + tuple(kvl_evs) + (war if k == 0 else ()),
                        tag="mm_cf")
                ev = p.op("vector", lambda e, o=o, c=c: e.tensor_copy(
                    cf.ap()[:, o * T + c * 512:o * T + (c + 1) * 512], pB[:, 0:512]),
                    deps=(mm,), tag="ev_cf")
                pB_rd = ev
                cf_evs.append(ev)
        ptrF = self.ps("ptrF", [128, 512], F32)
        ctm = self.sb("ctm", [128, 256], F32)
        lsum = self.sb("lsum", [128, 8], F32)
        sqs2 = self.sb("sqs2", [128, 256], F32)
        chv = ch_out.ap().rearrange("p (o tt) -> p o tt", o=2)
        ready = []
        ptr_free = None
        ctm_war = []
        sq_prev2 = None
        for tc in range(8):
            trs = []
            for o in range(2):
                war = (ptr_free,) if (o == 0 and ptr_free) else ()
                _, tr = self._tr_only(
                    ptrF[:, o * 128:(o + 1) * 128],
                    cf.ap()[:, o * T + tc * 128:o * T + (tc + 1) * 128],
                    deps=(cf_evs[o * 2 + tc // 4],) + war, bf16=False)
                trs.append(tr)
            ev = p.op("vector", lambda e: e.tensor_copy(ctm[:], ptrF[:, 0:256]),
                      deps=tuple(trs) + tuple(ctm_war), tag="cln_ev")
            ptr_free = ev
            dsum = p.op("vector", lambda e: e.reduce_sum(lsum[:, 0:1], ctm[:],
                                                         axis=AX.X), deps=(ev,))
            dsq = p.op("scalar", lambda e: e.activation(
                sqs2[:], ctm[:], AF.Square, accum_out=lsum[:, 1:2]),
                deps=(ev,) + ((sq_prev2,) if sq_prev2 else ()))
            sq_prev2 = dsq
            mu = p.op("vector", lambda e: e.tensor_scalar_mul(
                lsum[:, 2:3], lsum[:, 0:1], 1.0 / 256), deps=(dsum,))
            musq = p.op("vector", lambda e: e.tensor_mul(
                lsum[:, 3:4], lsum[:, 2:3], lsum[:, 2:3]), deps=(mu,))
            var = p.op("vector", lambda e: e.tensor_scalar(
                lsum[:, 4:5], lsum[:, 1:2], 1.0 / 256, None, ALU.mult), deps=(dsq,))
            var2 = p.op("vector", lambda e: e.tensor_scalar(
                lsum[:, 4:5], lsum[:, 4:5], lsum[:, 3:4], EPS, ALU.subtract, ALU.add),
                deps=(var, musq))
            ln_ = p.op("scalar", lambda e: e.activation(
                lsum[:, 5:6], lsum[:, 4:5], AF.Ln), deps=(var2, musq))
            rstd = p.op("scalar", lambda e: e.activation(
                lsum[:, 6:7], lsum[:, 5:6], AF.Exp, scale=-0.5), deps=(ln_,))
            nrm = p.op("vector", lambda e: e.tensor_scalar(
                ctm[:], ctm[:], lsum[:, 2:3], lsum[:, 6:7], ALU.subtract, ALU.mult),
                deps=(mu, rstd, dsq), tag="cln_nrm")
            af1 = p.op("vector", lambda e: e.tensor_mul(ctm[:], ctm[:], lgam[:]),
                       deps=(nrm, lgam_dep))
            af2 = p.op("vector", lambda e: e.tensor_add(ctm[:], ctm[:], lbet[:]),
                       deps=(af1, lbet_dep))
            trbs = []
            for o in range(2):
                war = (ptr_free,) if (o == 0) else ()
                _, trb = self._tr_only(ptrF[:, o * 128:(o + 1) * 128],
                                       ctm[:, o * 128:(o + 1) * 128],
                                       deps=(af2,) + war, bf16=False)
                trbs.append(trb)
            evb = p.op("vector", lambda e, tc=tc: e.tensor_copy(
                chv[:, :, tc * 128:(tc + 1) * 128],
                ptrF[:, 0:256].rearrange("p (o f) -> p o f", o=2)),
                deps=tuple(trbs), tag="cln_evb")
            ptr_free = evb
            ctm_war = [evb]
            ready.append(evb)
        return ready

    def emit_l1attn_D(self, x_src, x_dst, ch_own, ch_dep, chall_dram,
                      wq_pk, wkvc_pk, w2l_pk, wtk_pk, wor_pk,
                      cosq, cosq_dep, sinq, sinq_dep, cosk_dram, sink_dram,
                      acol, acol_dep, bcol, bcol_dep, ut, ut_dep,
                      bout, bout_dep):
        p = self.p
        T = T_CORE
        pQ = self.ps("pQ", [128, 512], F32)
        pS = self.ps("pS", [128, 512], F32)
        pPV = self.ps("pPV", [128, 512], F32)
        ptrV = self.ps("ptrV", [128, 1024], BF16)
        ptrP = self.ps("ptrP", [128, 512], BF16)
        ptrF = self.ps("ptrF", [128, 512], F32)
        qhat = self.sb("qhat", [128, 2 * T], BF16)
        qf = self.sb("qf", [128, 2 * T], F32)
        rsc = self.sb("rsc", [128, 1024], F32)
        kf = self.sb("kf", [128, 1024], F32)
        vf = self.sb("vf", [128, 1024], F32)
        kb = self.sb("kb", [128, 1024], BF16)
        vT = self.sb("vT", [128, 1024], BF16)
        chv = [self.sb("chv", [128, 1024], BF16) for _ in range(2)]
        ckb = self.sb("ckb", [128, 2048], F32)
        oacc = self.sb("oacc", [128, 8 * 256], F32)
        ml = self.sb("ml", [128, 8 * 8], F32)
        pb = self.sb("pb", [128, 512], BF16)
        wq_sb = self.sb("wq_sb", [128, 2 * 256], BF16)
        wkvc_sb = self.sb("wkvc", [128, 4 * 256], BF16)
        w2l_sb = self.sb("w2l", [128, 4 * 256], BF16)
        obuf = self.sb("obuf", [128, 16 * T], BF16)
        zb = self.sb("zb", [128, 4 * T], BF16)
        wtkbuf = [self.sb("wtkb", [128, 512], BF16) for _ in range(2)]
        worbuf = [self.sb("worb", [128, 512], BF16) for _ in range(2)]
        xbuf = self.sb("xbuf", [128, 2 * D], F32)
        stg = self.sb("stg", [128, 2048], BF16)
        ptrB = ptrV

        dwq = p.dma("gpsimd", lambda e: e.dma_start(
            out=wq_sb.ap().rearrange("p (a b) -> p a b", a=2),
            in_=wq_pk[0:2].rearrange("a p b -> p a b")),
            tag="wq")
        dkvc = p.dma("gpsimd", lambda e: e.dma_start(
            out=wkvc_sb.ap().rearrange("p (a b) -> p a b", a=4),
            in_=wkvc_pk[0:4].rearrange("a p b -> p a b")),
            tag="wkvc")
        d2l = p.dma("gpsimd", lambda e: e.dma_start(
            out=w2l_sb.ap().rearrange("p (a b) -> p a b", a=4),
            in_=w2l_pk[0:4].rearrange("a p b -> p a b")),
            tag="w2l")
        qevs = []
        pQ_rd = None
        for o in range(2):
            for c in range(2):
                mm = None
                war = (pQ_rd,) if pQ_rd else ()
                for k in range(2):
                    mm = p.op("tensor", lambda e, o=o, c=c, k=k: e.matmul(
                        pQ[:, 0:512], wq_sb[:, o * 256 + k * 128:o * 256 + (k + 1) * 128],
                        ch_own.ap()[:, k * T + c * 512:k * T + (c + 1) * 512],
                        start=(k == 0), stop=(k == 1)),
                        deps=(dwq, ch_dep) + (war if k == 0 else ()), tag="mm_q")
                ev = p.op("vector", lambda e, o=o, c=c: e.tensor_copy(
                    qf.ap()[:, o * T + c * 512:o * T + (c + 1) * 512], pQ[:, 0:512]),
                    deps=(mm,), tag="ev_q")
                pQ_rd = ev
                qevs.append(ev)
        rope_q = []
        rq_war = []
        for c in range(2):
            sl = slice(c * 512, (c + 1) * 512)
            q0 = qf.ap()[:, c * 512:(c + 1) * 512]
            q1 = qf.ap()[:, T + c * 512:T + (c + 1) * 512]
            c0 = cosq.ap()[:, c * 512:(c + 1) * 512]
            c1 = cosq.ap()[:, T + c * 512:T + (c + 1) * 512]
            s0 = sinq.ap()[:, c * 512:(c + 1) * 512]
            s1 = sinq.ap()[:, T + c * 512:T + (c + 1) * 512]
            t1 = p.op("vector", lambda e, q0=q0, c0=c0, c=c: e.tensor_mul(
                rsc[:, 0:512], q0, c0), deps=tuple(qevs) + (cosq_dep,) + tuple(rq_war))
            t2 = p.op("vector", lambda e, q1=q1, s0=s0: e.tensor_mul(
                rsc[:, 512:1024], q1, s0), deps=tuple(qevs) + (sinq_dep,) + tuple(rq_war))
            h0 = p.op("vector", lambda e, c=c: e.tensor_sub(
                qhat.ap()[:, c * 512:(c + 1) * 512], rsc[:, 0:512], rsc[:, 512:1024]),
                deps=(t1, t2), tag="ropeq0")
            t3 = p.op("vector", lambda e, q1=q1, c1=c1: e.tensor_mul(
                rsc[:, 0:512], q1, c1), deps=(h0, t1) + (cosq_dep,))
            t4 = p.op("vector", lambda e, q0=q0, s1=s1: e.tensor_mul(
                rsc[:, 512:1024], q0, s1), deps=(h0, t2) + (sinq_dep,))
            h1 = p.op("vector", lambda e, c=c: e.tensor_add(
                qhat.ap()[:, T + c * 512:T + (c + 1) * 512], rsc[:, 0:512],
                rsc[:, 512:1024]), deps=(t3, t4), tag="ropeq1")
            rq_war = [h1]
            rope_q += [h0, h1]
        import kernel as _K
        if getattr(_K, "L1D_DBG", None):
            dbg = _K.L1D_DBG
            p.dma("sync", lambda e: e.dma_start(out=dbg["qhat"], in_=qhat[:]),
                  deps=tuple(rope_q), tag="dbg_qh")
        iacc = p.op("vector", lambda e: e.memset(oacc[:], 0.0), tag="o0")
        im = p.op("vector", lambda e: e.memset(ml[:], 0.0), tag="ml0")
        im2 = p.op("vector", lambda e: e.tensor_scalar(
            ml.ap().rearrange("p (qt c) -> p qt c", qt=8)[:, :, 0:1],
            ml.ap().rearrange("p (qt c) -> p qt c", qt=8)[:, :, 0:1],
            -1e30, None, ALU.add), deps=(im,), tag="mneg")
        mlv = ml.ap().rearrange("p (qt c) -> p qt c", qt=8)
        chv_war = [[], []]
        kv_evs = []
        vT_evac_war = []
        vtr_free = None
        ptrP_free = None
        stg_war = []
        pS_rd = {}
        pPV_rd = {}
        ml_prev = {}
        o_prev = {}
        ck_war = []
        kb_war = []
        vT_war = []
        pb_war = []
        obuf_evs = []
        for c16 in range(16):
            j, lc = c16 // 2, c16 % 2
            cs = c16 % 2
            dch = p.dma("sync", lambda e, j=j, lc=lc, cs=cs: e.dma_start(
                out=chv[cs].ap().rearrange("p (o f) -> p o f", o=2),
                in_=chall_dram[j, :, :].rearrange("p (o tt) -> p o tt", o=2)
                [:, :, lc * 512:(lc + 1) * 512]),
                deps=tuple(chv_war[cs]), tag=f"chl{c16}")
            dck = p.dma("sync", lambda e, c16=c16: e.dma_start(
                out=ckb.ap().rearrange("p (a f) -> p a f", a=4)[:, 0:2, :],
                in_=cosk_dram.rearrange("p (a tt) -> p a tt", a=2)
                [:, :, c16 * 512:(c16 + 1) * 512]),
                deps=tuple(ck_war), tag=f"ck{c16}")
            dsk = p.dma("sync", lambda e, c16=c16: e.dma_start(
                out=ckb.ap().rearrange("p (a f) -> p a f", a=4)[:, 2:4, :],
                in_=sink_dram.rearrange("p (a tt) -> p a tt", a=2)
                [:, :, c16 * 512:(c16 + 1) * 512]),
                deps=tuple(ck_war), tag=f"sk{c16}")
            kvev = {}
            for o4 in range(4):
                mm = None
                dst = kf if o4 < 2 else vf
                do = o4 % 2
                war = tuple(kv_evs[-1:]) if kv_evs else ((pQ_rd,) if pQ_rd else ())
                for k in range(2):
                    mm = p.op("tensor", lambda e, o4=o4, k=k, cs=cs: e.matmul(
                        pQ[:, 0:512],
                        wkvc_sb[:, o4 * 256 + k * 128:o4 * 256 + (k + 1) * 128],
                        chv[cs].ap()[:, k * 512:(k + 1) * 512],
                        start=(k == 0), stop=(k == 1)),
                        deps=(dkvc, dch) + (war if k == 0 else ()), tag="mm_kv")
                ev = p.op("vector", lambda e, dst=dst, do=do: e.tensor_copy(
                    dst[:, do * 512:(do + 1) * 512], pQ[:, 0:512]),
                    deps=(mm,) + tuple(kb_war if o4 < 2 else vT_war), tag="ev_kv")
                kv_evs.append(ev)
                kvev[o4] = ev
            chv_war[cs] = [kvev[3]]
            rk = []
            for do in range(2):
                k0 = kf.ap()[:, 0:512]
                k1 = kf.ap()[:, 512:1024]
                cc = ckb.ap()[:, do * 512:(do + 1) * 512]
                ss = ckb.ap()[:, 1024 + do * 512:1024 + (do + 1) * 512]
                src1 = k0 if do == 0 else k1
                src2 = k1 if do == 0 else k0
                t1 = p.op("vector", lambda e, src1=src1, cc=cc: e.tensor_mul(
                    rsc[:, 0:512], src1, cc),
                    deps=(kvev[0], kvev[1], dck) + tuple(rq_war), tag="rk1")
                t2 = p.op("vector", lambda e, src2=src2, ss=ss: e.tensor_mul(
                    rsc[:, 512:1024], src2, ss),
                    deps=(kvev[0], kvev[1], dsk), tag="rk2")
                hh = p.op("vector", lambda e, do=do: (
                    e.tensor_sub(kb[:, 0:512], rsc[:, 0:512], rsc[:, 512:1024])
                    if do == 0 else
                    e.tensor_add(kb[:, 512:1024], rsc[:, 0:512], rsc[:, 512:1024])),
                    deps=(t1, t2), tag="ropek")
                rq_war = [hh]
                rk.append(hh)
            kb_war = rk
            ck_war = [rk[0], rk[1]]
            vb = p.op("vector", lambda e: e.tensor_copy(vT[:, 0:1024], vf[:]),
                      deps=(kvev[2], kvev[3]) + tuple(vT_war), tag="vcast")
            trs = []
            first = True
            for dtile in range(2):
                for ssub in range(4):
                    slot = ssub * 2 + dtile
                    war = ((vtr_free,) if (first and vtr_free) else ())
                    first = False
                    _, tr = self._tr_only(ptrV[:, slot * 128:(slot + 1) * 128],
                                          vT[:, dtile * 512 + ssub * 128:
                                             dtile * 512 + (ssub + 1) * 128],
                                          deps=(vb,) + war, bf16=True)
                    trs.append(tr)
            evV = p.op("vector", lambda e: e.tensor_copy(vT[:], ptrV[:, 0:1024]),
                       deps=tuple(trs), tag="evV")
            vT_war = [evV]
            vtr_free = evV
            for qt in range(8):
                mc = ml.ap()[:, qt * 8:qt * 8 + 8]
                acolv = acol.ap()[:, qt * 16 + c16:qt * 16 + c16 + 1]
                bcolv = bcol.ap()[:, qt * 16 + c16:qt * 16 + c16 + 1]
                mm = None
                war = (pS_rd.get(0),)
                for k in range(2):
                    mm = p.op("tensor", lambda e, qt=qt, k=k: e.matmul(
                        pS[:, 0:512],
                        qhat.ap()[:, k * T + qt * 128:k * T + (qt + 1) * 128],
                        kb[:, k * 512:(k + 1) * 512],
                        start=(k == 0), stop=(k == 1)),
                        deps=tuple(rope_q) + tuple(rk) +
                        ((war if war[0] else ()) if k == 0 else ()), tag="mm_s")
                msk = p.op("vector", lambda e, qt=qt, bcolv=bcolv: e.scalar_tensor_tensor(
                    pS[:, 0:512], ut.ap()[:, (qt % 4) * 512:(qt % 4 + 1) * 512],
                    bcolv, pS[:, 0:512], ALU.mult, ALU.add),
                    deps=(mm, ut_dep, bcol_dep), tag="mask")
                mx = p.op("vector", lambda e, mc=mc: e.reduce_max(
                    mc[:, 2:3], pS[:, 0:512], axis=AX.X), deps=(msk,))
                cnd = p.op("vector", lambda e, mc=mc, acolv=acolv: e.tensor_scalar(
                    mc[:, 2:3], mc[:, 2:3], 1.0 / 16, acolv, ALU.mult, ALU.add),
                    deps=(mx, acol_dep, im2) + tuple(ml_prev.get(qt, ())))
                mnw = p.op("vector", lambda e, mc=mc: e.tensor_max(
                    mc[:, 6:7], mc[:, 0:1], mc[:, 2:3]), deps=(cnd,))
                dif = p.op("vector", lambda e, mc=mc: e.tensor_sub(
                    mc[:, 4:5], mc[:, 0:1], mc[:, 6:7]), deps=(mnw,))
                rsce = p.op("scalar", lambda e, mc=mc: e.activation(
                    mc[:, 3:4], mc[:, 4:5], AF.Exp), deps=(dif,))
                pbias = p.op("vector", lambda e, mc=mc, acolv=acolv: e.tensor_scalar(
                    mc[:, 7:8], mc[:, 6:7], -1.0, acolv, ALU.mult, ALU.add),
                    deps=(mnw, acol_dep))
                pexp = p.op("scalar", lambda e, mc=mc: e.activation(
                    pb[:], pS[:, 0:512], AF.Exp, bias=mc[:, 7:8], scale=1.0 / 16,
                    accum_out=mc[:, 5:6]),
                    deps=(pbias, msk) + tuple(pb_war), tag="pexp")
                pS_rd[0] = pexp
                lup = p.op("vector", lambda e, mc=mc: e.scalar_tensor_tensor(
                    mc[:, 1:2], mc[:, 1:2], mc[:, 3:4], mc[:, 5:6],
                    ALU.mult, ALU.add), deps=(pexp, rsce), tag="lup")
                mup = p.op("vector", lambda e, mc=mc: e.tensor_copy(
                    mc[:, 0:1], mc[:, 6:7]), deps=(dif, pbias), tag="mup")
                ml_prev[qt] = (lup, mup)
                if c16 == 0 and qt == 0 and getattr(_K, "L1D_DBG", None):
                    _d0 = p.dma("sync", lambda e: e.dma_start(
                        out=_K.L1D_DBG["ml"][:, 0:8], in_=ml[:, 0:8]),
                        deps=(lup, mup, cnd, mx), tag="dbg_ml0")
                    ml_prev[qt] = (lup, mup, _d0)
                osc = p.op("vector", lambda e, qt=qt, mc=mc: e.tensor_scalar_mul(
                    oacc.ap()[:, qt * 256:(qt + 1) * 256],
                    oacc.ap()[:, qt * 256:(qt + 1) * 256], mc[:, 3:4]),
                    deps=(rsce, iacc) + tuple(o_prev.get(qt, ())), tag="osc")
                trsp = []
                for i in range(4):
                    war = (ptrP_free,) if (i == 0 and ptrP_free) else ()
                    _, tr = self._tr_only(ptrP[:, i * 128:(i + 1) * 128],
                                          pb[:, i * 128:(i + 1) * 128],
                                          deps=(pexp,) + war, bf16=True)
                    trsp.append(tr)
                evp = p.op("vector", lambda e: e.tensor_copy(stg[:, 0:512],
                           ptrP[:, 0:512]), deps=tuple(trsp) + tuple(stg_war),
                           tag="evp")
                ptrP_free = evp
                pb_war = [evp]
                mmv = None
                for ssub in range(4):
                    mmv = p.op("tensor", lambda e, ssub=ssub: e.matmul(
                        pPV[:, 0:256], stg[:, ssub * 128:(ssub + 1) * 128],
                        vT[:, ssub * 256:(ssub + 1) * 256],
                        start=(ssub == 0), stop=(ssub == 3)),
                        deps=(evp, evV) + ((pPV_rd.get(0),) if (ssub == 0 and pPV_rd.get(0)) else ()),
                        tag="mm_pv")
                oup = p.op("vector", lambda e, qt=qt: e.tensor_add(
                    oacc.ap()[:, qt * 256:(qt + 1) * 256],
                    oacc.ap()[:, qt * 256:(qt + 1) * 256], pPV[:, 0:256]),
                    deps=(mmv, osc), tag="oup")
                pPV_rd[0] = oup
                o_prev[qt] = (oup,)
                stg_war = [mmv]
        znorm = []
        for qt in range(8):
            mc = ml.ap()[:, qt * 8:qt * 8 + 8]
            rcp = p.op("vector", lambda e, mc=mc: e.reciprocal(
                    mc[:, 1:2], mc[:, 1:2]),
                    deps=tuple(ml_prev[qt]) + tuple(o_prev[qt]), tag="rcp")
            on = p.op("vector", lambda e, qt=qt, mc=mc: e.tensor_scalar_mul(
                oacc.ap()[:, qt * 256:(qt + 1) * 256],
                oacc.ap()[:, qt * 256:(qt + 1) * 256], mc[:, 1:2]),
                deps=(rcp,), tag="onorm")
            znorm.append(on)
        if getattr(_K, "L1D_DBG", None) and getattr(_K, "L1D_DBG_END", True):
            dbg = _K.L1D_DBG
            dago = p.dma("sync", lambda e: e.dma_start(out=dbg["oacc"], in_=oacc[:]),
                  deps=tuple(znorm), tag="dbg_oacc")
            dagm = p.dma("sync", lambda e: e.dma_start(out=dbg["ml"], in_=ml[:]),
                  deps=tuple(znorm), tag="dbg_ml")
            dagk = p.dma("sync", lambda e: e.dma_start(out=dbg["kb"], in_=kb[:]),
                  deps=tuple(rk), tag="dbg_kb")
            znorm = znorm + [dago, dagm, dagk]
        of = self.sb("ofb", [128, 2 * T_CORE], BF16)
        of_evs = []
        ptrF_free = None
        for qt in range(8):
            trs = []
            for dtile in range(2):
                war = (ptrF_free,) if (dtile == 0 and ptrF_free) else ()
                _, tr = self._tr_only(ptrF[:, dtile * 128:(dtile + 1) * 128],
                                      oacc.ap()[:, qt * 256 + dtile * 128:
                                                qt * 256 + (dtile + 1) * 128],
                                      deps=(znorm[qt],) + war, bf16=False)
                trs.append(tr)
            evo = p.op("vector", lambda e, qt=qt: e.tensor_copy(
                of.ap().rearrange("p (o tt) -> p o tt", o=2)[:, :, qt * 128:(qt + 1) * 128],
                ptrF[:, 0:256].rearrange("p (o f) -> p o f", o=2)),
                deps=tuple(trs) + tuple(qevs) + tuple(rope_q), tag="evof")
            ptrF_free = evo
            of_evs.append(evo)
        zevs = []
        pz_rd = kv_evs[-1]
        for zo in range(4):
            for c in range(2):
                mm = None
                war = (pz_rd,) if pz_rd else ()
                for k in range(2):
                    mm = p.op("tensor", lambda e, zo=zo, c=c, k=k: e.matmul(
                        pQ[:, 0:512],
                        w2l_sb[:, zo * 256 + k * 128:zo * 256 + (k + 1) * 128],
                        of.ap()[:, k * T + c * 512:k * T + (c + 1) * 512],
                        start=(k == 0), stop=(k == 1)),
                        deps=(d2l,) + tuple(of_evs) + (war if k == 0 else ()),
                        tag="mm_z")
                ev = p.op("vector", lambda e, zo=zo, c=c: e.tensor_copy(
                    zb.ap()[:, zo * T + c * 512:zo * T + (c + 1) * 512], pQ[:, 0:512]),
                    deps=(mm,), tag="ev_z")
                pz_rd = ev
                zevs.append(ev)
        wtk_last = [None, None]
        pS_last_read = pS_rd.get(0)
        for d in range(16):
            bs = d % 2
            war = tuple(x for x in (wtk_last[bs],) if x)
            dtk = p.dma("gpsimd", lambda e, d=d, bs=bs: e.dma_start(
                out=wtkbuf[bs][:], in_=wtk_pk[d]), deps=war, tag=f"wtk{d}")
            dor = p.dma("gpsimd", lambda e, d=d, bs=bs: e.dma_start(
                out=worbuf[bs][:], in_=wor_pk[d]), deps=war, tag=f"wor{d}")
            for c in range(2):
                mm = None
                war2 = (pS_last_read,) if pS_last_read else ()
                for k in range(4):
                    mm = p.op("tensor", lambda e, k=k, c=c, bs=bs: e.matmul(
                        pS[:, 0:512], wtkbuf[bs][:, k * 128:(k + 1) * 128],
                        zb.ap()[:, k * T + c * 512:k * T + (c + 1) * 512],
                        start=(k == 0), stop=False),
                        deps=(dtk,) + tuple(zevs) + (war2 if k == 0 else ()),
                        tag="mm_o1")
                for k in range(4):
                    mm = p.op("tensor", lambda e, k=k, c=c, bs=bs: e.matmul(
                        pS[:, 0:512], worbuf[bs][:, k * 128:(k + 1) * 128],
                        zb.ap()[:, k * T + c * 512:k * T + (c + 1) * 512],
                        start=False, stop=(k == 3)),
                        deps=(dor,), tag="mm_o2")
                wtk_last[bs] = mm
                evo = p.op("scalar", lambda e, d=d, c=c: e.activation(
                    obuf.ap()[:, d * T + c * 512:d * T + (c + 1) * 512],
                    pS[:, 0:512], AF.Identity, bias=bout.ap()[:, d:d + 1]),
                    deps=(mm, bout_dep), tag="ev_out")
                pS_last_read = evo
                obuf_evs.append(evo)
        stores = []
        xslot_free = [[], []]
        stage2_war = [[], []]
        ptrB_free = [vT_war[0]] if vT_war else []
        for t in range(8):
            sl = t % 2
            xt = xbuf.ap()[:, sl * D:(sl + 1) * D]
            dl = p.dma("sync", lambda e, t=t, xt=xt: e.dma_start(
                out=xt, in_=x_src[t * 128:(t + 1) * 128, :]),
                deps=tuple(xslot_free[sl]), tag=f"xl3_{t}")
            adds = []
            for half in range(2):
                trs = []
                for i in range(8):
                    d = half * 8 + i
                    war = tuple(ptrB_free) if i == 0 else ()
                    _, tr = self._tr_only(ptrB[:, i * 128:(i + 1) * 128],
                                          obuf.ap()[:, d * T + t * 128:
                                                    d * T + (t + 1) * 128],
                                          deps=(obuf_evs[2 * d + t // 4],) + war,
                                          bf16=True)
                    trs.append(tr)
                ev4 = p.op("vector", lambda e, half=half: e.tensor_copy(
                    stg[:, half * 1024:(half + 1) * 1024], ptrB[:, 0:1024]),
                    deps=tuple(trs) + tuple(stage2_war[half]), tag="oev3")
                ptrB_free = [ev4]
                ad = p.op("vector", lambda e, xt=xt, half=half: e.tensor_add(
                    xt[:, half * 1024:(half + 1) * 1024],
                    xt[:, half * 1024:(half + 1) * 1024],
                    stg[:, half * 1024:(half + 1) * 1024]),
                    deps=(ev4, dl), tag="oadd3")
                stage2_war[half] = [ad]
                adds.append(ad)
            st = p.dma("sync", lambda e, t=t, xt=xt: e.dma_start(
                out=x_dst[t * 128:(t + 1) * 128, :], in_=xt),
                deps=tuple(adds), tag=f"xs3_{t}")
            xslot_free[sl] = [st]
            stores.append(st)
        return stores



def _ident_np():
    import ml_dtypes
    i32 = np.eye(128, dtype=np.float32)
    ibf = i32.astype(ml_dtypes.bfloat16)
    return i32, ibf


def build_ffn0_nc(n_t=8):
    nc = bass.Bass("TRN2", target_bir_lowering=False)
    T = n_t * 128
    x_in = nc.dram_tensor("x", [T, D], F32, kind="ExternalInput").ap()
    wg = nc.dram_tensor("wg", [64, 128, D], F32, kind="ExternalInput").ap()
    wu = nc.dram_tensor("wu", [64, 128, D], F32, kind="ExternalInput").ap()
    wd = nc.dram_tensor("wd", [16, 128, 8192], F32, kind="ExternalInput").ap()
    id32 = nc.dram_tensor("id32", [128, 128], F32, kind="ExternalInput").ap()
    idbf = nc.dram_tensor("idbf", [128, 128], BF16, kind="ExternalInput").ap()
    x_out = nc.dram_tensor("x_out", [T, D], F32, kind="ExternalOutput").ap()

    with ExitStack() as ctx:
        p = Prog(nc)
        cb = CoreBuild(p, nc, ctx)
        cb.load_identity(id32, idbf)
        xres = cb.sb("xres", [128, n_t * D], F32)
        x_tiles = [xres.ap()[:, t * D:(t + 1) * D] for t in range(n_t)]
        x_deps = []
        for t in range(n_t):
            dl = p.dma("sync", lambda e, t=t: e.dma_start(
                out=x_tiles[t], in_=x_in[t * 128:(t + 1) * 128, :]), tag=f"xload{t}")
            x_deps.append([dl])
        xn_fm = cb.sb("xnfm", [128, 16 * T], BF16)
        xn_ready, _, _lnp = cb.emit_ln_to_fm(x_tiles, x_deps, xn_fm, n_t)
        add_ops = cb.emit_ffn0(x_tiles, xn_fm, xn_ready, wg, wu, wd, n_t, coeff=0.5)
        stores = []
        for t in range(n_t):
            st = p.dma("sync", lambda e, t=t: e.dma_start(
                out=x_out[t * 128:(t + 1) * 128, :], in_=x_tiles[t]),
                deps=tuple(add_ops[t]), tag=f"xstore{t}")
            stores.append(st)
        p.finalize_with_outputs(stores)

        _emit_prog(nc, p)
    return nc


def _emit_prog(nc, p):
    sems = {}
    dma_sems = {}
    with ExitStack() as semctx:
        for name in ("sync", "gpsimd", "tensor", "vector", "scalar"):
            sems[name] = semctx.enter_context(nc.semaphore(f"sem_{name}"))
            dma_sems[name] = [semctx.enter_context(nc.semaphore(f"dsem_{name}{i}"))
                              for i in range(Prog.N_DMA_SLOTS)]
        block = semctx.enter_context(nc.Block())
        p.emit(block, sems, dma_sems)


NCP = 10


def build_l0attnB_nc():
    nc = bass.Bass("TRN2", target_bir_lowering=False)
    T = T_CORE
    x_in = nc.dram_tensor("x", [T, D], F32, kind="ExternalInput").ap()
    mem_i = nc.dram_tensor("mem", [128, 16 * T], BF16, kind="ExternalInput").ap()
    qh_i = nc.dram_tensor("qhat", [128, 8 * D], BF16, kind="ExternalInput").ap()
    xc_i = nc.dram_tensor("xc", [128, 16 * T], BF16, kind="ExternalInput").ap()
    gm_i = nc.dram_tensor("gamma", [16, T], F32, kind="ExternalInput").ap()
    allP = nc.dram_tensor("allP", [8, 16], F32, kind="ExternalInput").ap()
    allM = nc.dram_tensor("allM", [8, D], F32, kind="ExternalInput").ap()
    mmask = nc.dram_tensor("mmask", [16, 8], F32, kind="ExternalInput").ap()
    ommask = nc.dram_tensor("ommask", [16, 8], F32, kind="ExternalInput").ap()
    oneh = nc.dram_tensor("onehot", [16, D], F32, kind="ExternalInput").ap()
    ones8 = nc.dram_tensor("ones8", [8, 1], F32, kind="ExternalInput").ap()
    cp = nc.dram_tensor("colpack", [D, NCP], F32, kind="ExternalInput").ap()
    mnw = nc.dram_tensor("mnw", [128, D], F32, kind="ExternalInput").ap()
    mnb = nc.dram_tensor("mnb", [128, D], F32, kind="ExternalInput").ap()
    wog = nc.dram_tensor("wog", [16, 128, D], F32, kind="ExternalInput").ap()
    wo = nc.dram_tensor("wo", [16, 128, D], F32, kind="ExternalInput").ap()
    id32 = nc.dram_tensor("id32", [128, 128], F32, kind="ExternalInput").ap()
    idbf = nc.dram_tensor("idbf", [128, 128], BF16, kind="ExternalInput").ap()
    x_out = nc.dram_tensor("x_out", [T, D], F32, kind="ExternalOutput").ap()

    with ExitStack() as ctx:
        p = Prog(nc)
        cb = CoreBuild(p, nc, ctx)
        cb.load_identity(id32, idbf)

        def ld(name, shape, dt, src):
            t = cb.sb(name, shape, dt)
            dep = p.dma("sync", lambda e, t=t, src=src: e.dma_start(out=t[:], in_=src),
                        tag=f"ld_{name}")
            return t, dep

        mem_sb, dmem = ld("mem", [128, 16 * T], BF16, mem_i)
        qh_sb, dqh = ld("qhat", [128, 8 * D], BF16, qh_i)
        xc_sb, dxc = ld("xc", [128, 16 * T], BF16, xc_i)
        gm_sb, dgm = ld("gamma", [16, T], F32, gm_i)
        ap_sb, dap = ld("allP", [8, 16], F32, allP)
        am_sb, dam = ld("allM", [8, D], F32, allM)
        mm_sb, dmm = ld("mmask", [16, 8], F32, mmask)
        om_sb, dom = ld("ommask", [16, 8], F32, ommask)
        oh_sb, doh = ld("oneh", [16, D], F32, oneh)
        o8_sb, do8 = ld("ones8", [8, 1], F32, ones8)
        mnw_sb, dmnw = ld("mnw", [128, D], F32, mnw)
        mnb_sb, dmnb = ld("mnb", [128, D], F32, mnb)
        cp_sb = cb.sb("cp", [128, 16 * NCP], F32)
        dcp = p.dma("sync", lambda e: e.dma_start(
            out=cp_sb.ap().rearrange("p (h c) -> p h c", h=16),
            in_=cp.rearrange("(h p) c -> p h c", p=128)), tag="cp")

        stores = cb.emit_l0attn_B(
            x_in, x_out, mem_sb, dmem, qh_sb, dqh, xc_sb, dxc, gm_sb, dgm,
            ap_sb.ap(), dap, am_sb.ap(), dam, mm_sb.ap(), dmm, om_sb.ap(), dom,
            oh_sb.ap(), doh, o8_sb.ap(), do8, cp_sb, dcp,
            mnw_sb, dmnw, mnb_sb, dmnb, wog, wo, NCP=NCP)
        p.finalize_with_outputs(stores)
        _emit_prog(nc, p)
    return nc


def run_l0attnB_launch(x_full, prep, a_outs):
    i32, ibf = _ident_np()
    if "l0b" not in _NC_CACHE:
        _NC_CACHE["l0b"] = build_l0attnB_nc()
    nc = _NC_CACHE["l0b"]
    allP = np.stack([a_outs[j]["tp_out"][:, 0] for j in range(N_CORES)])
    allM = np.stack([a_outs[j]["m_out"].reshape(D) for j in range(N_CORES)])
    in_maps = []
    for i in range(N_CORES):
        mmask = np.zeros((16, 8), np.float32)
        mmask[:, :i] = 1.0
        m = {"x": np.ascontiguousarray(x_full[i * T_CORE:(i + 1) * T_CORE]),
             "mem": a_outs[i]["mem_out"], "qhat": a_outs[i]["qhat_out"],
             "xc": a_outs[i]["xc_out"], "gamma": a_outs[i]["gamma_out"],
             "allP": allP, "allM": allM, "mmask": mmask,
             "ommask": np.ascontiguousarray(1.0 - mmask),
             "onehot": prep["onehot"], "ones8": np.ones((8, 1), np.float32),
             "colpack": prep["colpack"], "mnw": prep["mnw"], "mnb": prep["mnb"],
             "wog": prep["wog"], "wo": prep["wo"], "id32": i32, "idbf": ibf}
        in_maps.append(m)
    res = run_bass_kernel_spmd(nc, in_maps, core_ids=list(range(N_CORES)))
    return np.concatenate([res.results[i]["x_out"] for i in range(N_CORES)], axis=0)


def build_l0attnA_nc():
    nc = bass.Bass("TRN2", target_bir_lowering=False)
    T = T_CORE
    x_in = nc.dram_tensor("x", [T, D], F32, kind="ExternalInput").ap()
    xh3 = nc.dram_tensor("xh3", [3, D], F32, kind="ExternalInput").ap()
    cp = nc.dram_tensor("colpack", [D, NCP], F32, kind="ExternalInput").ap()
    wk = nc.dram_tensor("wk", [16, 128, D], F32, kind="ExternalInput").ap()
    wv = nc.dram_tensor("wv", [16, 128, D], F32, kind="ExternalInput").ap()
    wig = nc.dram_tensor("wig", [16, 128, D], F32, kind="ExternalInput").ap()
    wq = nc.dram_tensor("wq", [16, 128, D], F32, kind="ExternalInput").ap()
    wgp = nc.dram_tensor("wgp", [1, 128, 256], F32, kind="ExternalInput").ap()
    bg = nc.dram_tensor("bg", [16, 1], F32, kind="ExternalInput").ap()
    oneh = nc.dram_tensor("onehot", [16, D], F32, kind="ExternalInput").ap()
    id32 = nc.dram_tensor("id32", [128, 128], F32, kind="ExternalInput").ap()
    idbf = nc.dram_tensor("idbf", [128, 128], BF16, kind="ExternalInput").ap()
    mem_o = nc.dram_tensor("mem_out", [128, 16 * T], BF16, kind="ExternalOutput").ap()
    qh_o = nc.dram_tensor("qhat_out", [128, 8 * D], BF16, kind="ExternalOutput").ap()
    xc_o = nc.dram_tensor("xc_out", [128, 16 * T], BF16, kind="ExternalOutput").ap()
    gm_o = nc.dram_tensor("gamma_out", [16, T], F32, kind="ExternalOutput").ap()
    tp_o = nc.dram_tensor("tp_out", [16, 1], F32, kind="ExternalOutput").ap()
    m_o = nc.dram_tensor("m_out", [16, 128], F32, kind="ExternalOutput").ap()

    with ExitStack() as ctx:
        p = Prog(nc)
        cb = CoreBuild(p, nc, ctx)
        cb.load_identity(id32, idbf)
        xh_sb = cb.sb("xh", [3, D], F32)
        dxh = p.dma("sync", lambda e: e.dma_start(out=xh_sb[:], in_=xh3), tag="xh")
        cp_sb = cb.sb("cp", [128, 16 * NCP], F32)
        dcp = p.dma("sync", lambda e: e.dma_start(
            out=cp_sb.ap().rearrange("p (h c) -> p h c", h=16),
            in_=cp.rearrange("(h p) c -> p h c", p=128)), tag="cp")
        bg_sb = cb.sb("bg", [16, 1], F32)
        dbg = p.dma("sync", lambda e: e.dma_start(out=bg_sb[:], in_=bg), tag="bg")
        oh_sb = cb.sb("oneh", [16, D], F32)
        doh = p.dma("sync", lambda e: e.dma_start(out=oh_sb[:], in_=oneh), tag="oh")

        qhat_buf = cb.sb("qhat", [128, 8 * D], BF16)
        xc_buf = cb.sb("xc", [128, 16 * T_CORE], BF16)
        gamma_sb = cb.sb("gamma", [16, T_CORE], F32)
        tp_sb = cb.sb("tp", [16, 1], F32)
        mr_sb = cb.sb("mrow", [16, 128], F32)

        done = cb.emit_l0attn_A(x_in, xh_sb.ap(), dxh, cp_sb, dcp,
                                wk, wv, wig, wq, wgp, bg_sb.ap(), dbg, oh_sb.ap(),
                                doh, mem_o, qhat_buf, xc_buf, gamma_sb,
                                tp_sb, mr_sb, NCP=NCP)
        stores = [o for o in done["mem"] if o.is_dma]
        for (dst, src, dep) in ((qh_o, qhat_buf, done["qhat"]),
                                (xc_o, xc_buf, done["xc"]),
                                (gm_o, gamma_sb, done["gamma"]),
                                (tp_o, tp_sb, done["totalP"]),
                                (m_o, mr_sb, done["Mrow"])):
            st = p.dma("sync", lambda e, dst=dst, src=src: e.dma_start(
                out=dst, in_=src[:]), deps=tuple(dep), tag="store")
            stores.append(st)
        p.finalize_with_outputs(stores)
        _emit_prog(nc, p)
    return nc


def l0attn_host_prep(inp):
    d = {}
    d["wk"] = pack_lhsT(np.ascontiguousarray(inp["L0_attn_Wk"].T))
    d["wv"] = pack_lhsT(np.ascontiguousarray(inp["L0_attn_Wv"].T))
    d["wig"] = pack_lhsT(np.ascontiguousarray(inp["L0_attn_Wig"].T))
    d["wq"] = pack_lhsT(np.ascontiguousarray(inp["L0_attn_Wq"].T))
    d["wgp"] = pack_lhsT(np.ascontiguousarray(inp["L0_attn_Wg"].T), mw=16)
    d["bg"] = np.ascontiguousarray(inp["L0_attn_b_g"].reshape(16, 1))
    cp = np.zeros((D, NCP), np.float32)
    cp[:, 0:4] = inp["L0_attn_conv_w"]
    cp[:, 4] = inp["L0_attn_conv_b"]
    cp[:, 5] = inp["L0_attn_b_ig"]
    cp[:, 6] = inp["L0_attn_b_og"]
    cp[:, 8] = inp["L0_attn_gn_w"]
    cp[:, 9] = inp["L0_attn_gn_b"]
    d["colpack"] = cp
    oh = np.zeros((16, D), np.float32)
    for h in range(16):
        oh[h, h * 128:(h + 1) * 128] = 1.0
    d["onehot"] = oh
    d["wog"] = pack_lhsT(np.ascontiguousarray(inp["L0_attn_Wog"].T))
    d["wo"] = pack_lhsT(np.ascontiguousarray(inp["L0_attn_Wo"].T))
    d["mnw"] = np.ascontiguousarray(np.broadcast_to(
        inp["L0_attn_mn_w"].astype(np.float32), (128, D)))
    d["mnb"] = np.ascontiguousarray(np.broadcast_to(
        inp["L0_attn_mn_b"].astype(np.float32), (128, D)))
    return d


_NC_CACHE = {}


def run_l0attnA_launch(x_full, prep):
    i32, ibf = _ident_np()
    if "l0a" not in _NC_CACHE:
        _NC_CACHE["l0a"] = build_l0attnA_nc()
    nc = _NC_CACHE["l0a"]
    in_maps = []
    for i in range(N_CORES):
        xh = (np.zeros((3, D), np.float32) if i == 0
              else np.ascontiguousarray(x_full[i * T_CORE - 3:i * T_CORE]))
        m = {"x": np.ascontiguousarray(x_full[i * T_CORE:(i + 1) * T_CORE]),
             "xh3": xh, "id32": i32, "idbf": ibf}
        for k in ("wk", "wv", "wig", "wq", "wgp", "bg", "colpack", "onehot"):
            m[k] = prep[k]
        in_maps.append(m)
    res = run_bass_kernel_spmd(nc, in_maps, core_ids=list(range(N_CORES)))
    return [res.results[i] for i in range(N_CORES)]


def build_l1attnC_nc():
    nc = bass.Bass("TRN2", target_bir_lowering=False)
    T = T_CORE
    x_in = nc.dram_tensor("x", [T, D], F32, kind="ExternalInput").ap()
    wkv = nc.dram_tensor("wkv", [4, 128, D], F32, kind="ExternalInput").ap()
    wcmp = nc.dram_tensor("wcmp", [2, 128, 512], F32, kind="ExternalInput").ap()
    lgam = nc.dram_tensor("lgam", [128, 256], F32, kind="ExternalInput").ap()
    lbet = nc.dram_tensor("lbet", [128, 256], F32, kind="ExternalInput").ap()
    id32 = nc.dram_tensor("id32", [128, 128], F32, kind="ExternalInput").ap()
    idbf = nc.dram_tensor("idbf", [128, 128], BF16, kind="ExternalInput").ap()
    ch_o = nc.dram_tensor("ch_out", [128, 2 * T], BF16, kind="ExternalOutput").ap()
    with ExitStack() as ctx:
        p = Prog(nc)
        cb = CoreBuild(p, nc, ctx)
        cb.load_identity(id32, idbf)
        lg_sb = cb.sb("lg", [128, 256], F32)
        dlg = p.dma("sync", lambda e: e.dma_start(out=lg_sb[:], in_=lgam), tag="lg")
        lb_sb = cb.sb("lb", [128, 256], F32)
        dlb = p.dma("sync", lambda e: e.dma_start(out=lb_sb[:], in_=lbet), tag="lb")
        ch_sb = cb.sb("ch", [128, 2 * T], BF16)
        ready = cb.emit_l1attn_C(x_in, wkv, wcmp, lg_sb.ap(), dlg, lb_sb.ap(), dlb,
                                 ch_sb)
        st = p.dma("sync", lambda e: e.dma_start(out=ch_o, in_=ch_sb[:]),
                   deps=tuple(ready), tag="ch_st")
        p.finalize_with_outputs([st])
        _emit_prog(nc, p)
    return nc


def build_l1attnD_nc():
    nc = bass.Bass("TRN2", target_bir_lowering=False)
    T = T_CORE
    x_in = nc.dram_tensor("x", [T, D], F32, kind="ExternalInput").ap()
    ch_own = nc.dram_tensor("ch_own", [128, 2 * T], BF16, kind="ExternalInput").ap()
    chall = nc.dram_tensor("chall", [8, 128, 2 * T], BF16, kind="ExternalInput").ap()
    wq = nc.dram_tensor("wq", [2, 128, 256], F32, kind="ExternalInput").ap()
    wkvc = nc.dram_tensor("wkvc", [4, 128, 256], F32, kind="ExternalInput").ap()
    w2l = nc.dram_tensor("w2l", [4, 128, 256], F32, kind="ExternalInput").ap()
    wtk = nc.dram_tensor("wtk", [16, 128, 512], F32, kind="ExternalInput").ap()
    wor = nc.dram_tensor("wor", [16, 128, 512], F32, kind="ExternalInput").ap()
    cosq = nc.dram_tensor("cosq", [128, 2 * T], F32, kind="ExternalInput").ap()
    sinq = nc.dram_tensor("sinq", [128, 2 * T], F32, kind="ExternalInput").ap()
    cosk = nc.dram_tensor("cosk", [128, 2 * 8192], F32, kind="ExternalInput").ap()
    sink = nc.dram_tensor("sink", [128, 2 * 8192], F32, kind="ExternalInput").ap()
    acol = nc.dram_tensor("acol", [128, 128], F32, kind="ExternalInput").ap()
    bcol = nc.dram_tensor("bcol", [128, 128], F32, kind="ExternalInput").ap()
    ut = nc.dram_tensor("ut", [128, 4 * 512], F32, kind="ExternalInput").ap()
    bout = nc.dram_tensor("bout", [128, 16], F32, kind="ExternalInput").ap()
    id32 = nc.dram_tensor("id32", [128, 128], F32, kind="ExternalInput").ap()
    idbf = nc.dram_tensor("idbf", [128, 128], BF16, kind="ExternalInput").ap()
    x_out = nc.dram_tensor("x_out", [T, D], F32, kind="ExternalOutput").ap()
    import kernel as _K
    if getattr(_K, "L1D_WANT_DBG", False):
        _K.L1D_DBG = {
            "qhat": nc.dram_tensor("dbg_qhat", [128, 2 * T], BF16, kind="ExternalOutput").ap(),
            "oacc": nc.dram_tensor("dbg_oacc", [128, 8 * 256], F32, kind="ExternalOutput").ap(),
            "ml": nc.dram_tensor("dbg_ml", [128, 64], F32, kind="ExternalOutput").ap(),
            "kb": nc.dram_tensor("dbg_kb", [128, 1024], BF16, kind="ExternalOutput").ap(),
        }
    else:
        _K.L1D_DBG = None
    with ExitStack() as ctx:
        p = Prog(nc)
        cb = CoreBuild(p, nc, ctx)
        cb.load_identity(id32, idbf)

        def ld(name, shape, dt, src):
            t = cb.sb(name, shape, dt)
            dep = p.dma("sync", lambda e, t=t, src=src: e.dma_start(out=t[:], in_=src),
                        tag=f"ld_{name}")
            return t, dep

        chо_sb, dch = ld("chown", [128, 2 * T], BF16, ch_own)
        cq_sb, dcq = ld("cosq", [128, 2 * T], F32, cosq)
        sq_sb, dsq = ld("sinq", [128, 2 * T], F32, sinq)
        ac_sb, dac = ld("acol", [128, 128], F32, acol)
        bc_sb, dbc = ld("bcol", [128, 128], F32, bcol)
        ut_sb, dut = ld("ut", [128, 4 * 512], F32, ut)
        bo_sb, dbo = ld("bout", [128, 16], F32, bout)
        stores = cb.emit_l1attn_D(
            x_in, x_out, chо_sb, dch, chall, wq, wkvc, w2l, wtk, wor,
            cq_sb, dcq, sq_sb, dsq, cosk, sink, ac_sb, dac, bc_sb, dbc,
            ut_sb, dut, bo_sb, dbo)
        p.finalize_with_outputs(stores)
        _emit_prog(nc, p)
    return nc


def l1attn_host_prep(inp):
    d = {}
    d["wkv"] = pack_lhsT(np.ascontiguousarray(inp["L1_attn_W_to_kv"].T))
    d["wcmp"] = pack_lhsT(np.ascontiguousarray(inp["L1_attn_W_compress"].T))
    d["lgam"] = np.ascontiguousarray(np.broadcast_to(
        inp["L1_attn_ln_gamma"].astype(np.float32), (128, 256)))
    d["lbet"] = np.ascontiguousarray(np.broadcast_to(
        inp["L1_attn_ln_beta"].astype(np.float32), (128, 256)))
    d["wq"] = pack_lhsT(np.ascontiguousarray(inp["L1_attn_W_q_comp"].T))
    d["wkvc"] = pack_lhsT(np.ascontiguousarray(inp["L1_attn_W_kv_comp"].T))
    d["w2l"] = pack_lhsT(np.ascontiguousarray(inp["L1_attn_W_comp2lat"].T))
    d["wtk"] = pack_lhsT(np.ascontiguousarray(inp["L1_attn_W_to_kv"]))
    d["wor"] = pack_lhsT(np.ascontiguousarray(inp["L1_attn_W_out_res"].T))
    d["bout"] = np.ascontiguousarray(
        inp["L1_attn_b_out_res"].astype(np.float32).reshape(16, 128).T)
    inv = 1.0 / (10000.0 ** (np.arange(0, 256, 2, dtype=np.float64) / 256))
    freqs = np.arange(8192, dtype=np.float64)[:, None] * inv[None, :]
    cf = np.cos(freqs).T.astype(np.float32)
    sf = np.sin(freqs).T.astype(np.float32)
    d["cosk"] = np.ascontiguousarray(np.concatenate([cf, cf], 1))
    d["sink"] = np.ascontiguousarray(np.concatenate([sf, sf], 1))
    ut = np.zeros((128, 4 * 512), np.float32)
    for a in range(4):
        for tq in range(128):
            ut[tq, a * 512:(a + 1) * 512][np.arange(512) > tq + 128 * a] = -1e30
    d["ut"] = ut
    return d


def run_l1attn_launches(x_full, prep):
    i32, ibf = _ident_np()
    if "l1c" not in _NC_CACHE:
        _NC_CACHE["l1c"] = build_l1attnC_nc()
    if "l1d" not in _NC_CACHE:
        _NC_CACHE["l1d"] = build_l1attnD_nc()
    in_maps = []
    for i in range(N_CORES):
        m = {"x": np.ascontiguousarray(x_full[i * T_CORE:(i + 1) * T_CORE]),
             "id32": i32, "idbf": ibf}
        for k in ("wkv", "wcmp", "lgam", "lbet"):
            m[k] = prep[k]
        in_maps.append(m)
    res = run_bass_kernel_spmd(_NC_CACHE["l1c"], in_maps, core_ids=list(range(N_CORES)))
    chall = np.stack([res.results[i]["ch_out"] for i in range(N_CORES)])
    in_maps = []
    for i in range(N_CORES):
        acol = np.zeros((128, 128), np.float32)
        bcol = np.zeros((128, 128), np.float32)
        for qt in range(8):
            cstar = 2 * i + qt // 4
            for c in range(16):
                if c > cstar:
                    acol[:, qt * 16 + c] = -1e30
                elif c == cstar:
                    bcol[:, qt * 16 + c] = 1.0
        sl = slice(i * T_CORE, (i + 1) * T_CORE)
        cosq = np.ascontiguousarray(np.concatenate(
            [prep["cosk"][:, sl], prep["cosk"][:, sl]], 1))
        sinq = np.ascontiguousarray(np.concatenate(
            [prep["sink"][:, sl], prep["sink"][:, sl]], 1))
        m = {"x": np.ascontiguousarray(x_full[i * T_CORE:(i + 1) * T_CORE]),
             "ch_own": chall[i], "chall": chall, "cosq": cosq, "sinq": sinq,
             "acol": acol, "bcol": bcol, "id32": i32, "idbf": ibf}
        for k in ("wq", "wkvc", "w2l", "wtk", "wor", "cosk", "sink", "ut", "bout"):
            m[k] = prep[k]
        in_maps.append(m)
    res = run_bass_kernel_spmd(_NC_CACHE["l1d"], in_maps, core_ids=list(range(N_CORES)))
    return np.concatenate([res.results[i]["x_out"] for i in range(N_CORES)], axis=0)


def build_l1ffn_nc():
    nc = bass.Bass("TRN2", target_bir_lowering=False)
    T = T_CORE
    x_in = nc.dram_tensor("x", [T, D], F32, kind="ExternalInput").ap()
    wa = nc.dram_tensor("wa", [1, 128, 256], F32, kind="ExternalInput").ap()
    rg = nc.dram_tensor("rg", [18, 8, 128, D], F32, kind="ExternalInput").ap()
    rw1 = nc.dram_tensor("rw1", [18, 8, 128, D], F32, kind="ExternalInput").ap()
    rw2 = nc.dram_tensor("rw2", [18, 16, 128, 1024], F32, kind="ExternalInput").ap()
    rgbc = nc.dram_tensor("rgbc", [128, 18 * 8], F32, kind="ExternalInput").ap()
    rb1c = nc.dram_tensor("rb1c", [128, 18 * 8], F32, kind="ExternalInput").ap()
    rbc = nc.dram_tensor("rbc", [18, D], F32, kind="ExternalInput").ap()
    oh18 = nc.dram_tensor("oh18", [18, 18 * 128], F32, kind="ExternalInput").ap()
    id32 = nc.dram_tensor("id32", [128, 128], F32, kind="ExternalInput").ap()
    idbf = nc.dram_tensor("idbf", [128, 128], BF16, kind="ExternalInput").ap()
    x_out = nc.dram_tensor("x_out", [T, D], F32, kind="ExternalOutput").ap()

    with ExitStack() as ctx:
        p = Prog(nc)
        cb = CoreBuild(p, nc, ctx)
        cb.load_identity(id32, idbf)
        rgb_sb = cb.sb("rgb", [128, 18 * 8], F32)
        drgb = p.dma("sync", lambda e: e.dma_start(out=rgb_sb[:], in_=rgbc), tag="rgb")
        rb1_sb = cb.sb("rb1", [128, 18 * 8], F32)
        drb1 = p.dma("sync", lambda e: e.dma_start(out=rb1_sb[:], in_=rb1c), tag="rb1")
        rbc_sb = cb.sb("rbc", [18, D], F32)
        drbc = p.dma("sync", lambda e: e.dma_start(out=rbc_sb[:], in_=rbc), tag="rbc")
        oh_sb = cb.sb("oh18", [18, 18 * 128], F32)
        doh = p.dma("sync", lambda e: e.dma_start(out=oh_sb[:], in_=oh18), tag="oh18")
        stores = cb.emit_l1ffn(x_in, x_out, wa, rg, rw1, rw2,
                               rgb_sb, drgb, rb1_sb, drb1, rbc_sb, drbc,
                               oh_sb.ap(), doh, coeff=0.5)
        p.finalize_with_outputs(stores)
        _emit_prog(nc, p)
    return nc


def l1ffn_host_prep(inp):
    d = {}
    d["wa"] = pack_lhsT(np.ascontiguousarray(inp["L1_ffn_wa"]), mw=16)
    rg = np.concatenate([inp["L1_ffn_rg"], inp["L1_ffn_sg"]], 0)
    rw1 = np.concatenate([inp["L1_ffn_rw1"], inp["L1_ffn_sw1"]], 0)
    rw2 = np.concatenate([inp["L1_ffn_rw2"], inp["L1_ffn_sw2"]], 0)
    d["rg"] = np.stack([pack_lhsT(np.ascontiguousarray(rg[e])) for e in range(18)])
    d["rw1"] = np.stack([pack_lhsT(np.ascontiguousarray(rw1[e])) for e in range(18)])
    d["rw2"] = np.stack([pack_lhsT(np.ascontiguousarray(rw2[e])) for e in range(18)])
    _bias_cols = lambda a: np.ascontiguousarray(
        a.reshape(18, 8, 128).transpose(2, 0, 1).reshape(128, 144))
    d["rgbc"] = _bias_cols(np.concatenate([inp["L1_ffn_rgb"], inp["L1_ffn_sgb"]], 0))
    d["rb1c"] = _bias_cols(np.concatenate([inp["L1_ffn_rb1"], inp["L1_ffn_sb1"]], 0))
    d["rbc"] = np.concatenate([inp["L1_ffn_rb2"], inp["L1_ffn_sb2"]], 0)
    oh = np.zeros((18, 18 * 128), np.float32)
    for e in range(18):
        oh[e, e * 128:(e + 1) * 128] = 1.0
    d["oh18"] = oh
    return d


def run_l1ffn_launch(x_full, prep):
    i32, ibf = _ident_np()
    if "l1f" not in _NC_CACHE:
        _NC_CACHE["l1f"] = build_l1ffn_nc()
    nc = _NC_CACHE["l1f"]
    in_maps = []
    for i in range(N_CORES):
        m = {"x": np.ascontiguousarray(x_full[i * T_CORE:(i + 1) * T_CORE]),
             "id32": i32, "idbf": ibf}
        for k in ("wa", "rg", "rw1", "rw2", "rgbc", "rb1c", "rbc", "oh18"):
            m[k] = prep[k]
        in_maps.append(m)
    res = run_bass_kernel_spmd(nc, in_maps, core_ids=list(range(N_CORES)))
    return np.concatenate([res.results[i]["x_out"] for i in range(N_CORES)], axis=0)


_FFN0_CACHE = {}


def run_ffn0_launch(x_full, w_gate, w_up, w_down):
    i32, ibf = _ident_np()
    if "nc" not in _FFN0_CACHE:
        _FFN0_CACHE["nc"] = build_ffn0_nc()
    nc = _FFN0_CACHE["nc"]
    wg_pk = pack_lhsT(np.ascontiguousarray(w_gate.T))
    wu_pk = pack_lhsT(np.ascontiguousarray(w_up.T))
    wd_pk = pack_lhsT(np.ascontiguousarray(w_down.T))
    in_maps = []
    for i in range(N_CORES):
        in_maps.append({
            "x": np.ascontiguousarray(x_full[i * T_CORE:(i + 1) * T_CORE]),
            "wg": wg_pk, "wu": wu_pk, "wd": wd_pk, "id32": i32, "idbf": ibf,
        })
    res = run_bass_kernel_spmd(nc, in_maps, core_ids=list(range(N_CORES)))
    return np.concatenate([res.results[i]["x_out"] for i in range(N_CORES)], axis=0)


RG8 = [list(range(8))]


def build_fused_nc():
    nc = bass.Bass("TRN2", target_bir_lowering=False)
    T = T_CORE
    ei = lambda n, s, dt=F32: nc.dram_tensor(n, s, dt, kind="ExternalInput").ap()
    x_ext = ei("x", [T, D])
    wg_e = ei("wg", [64, 128, D]); wu_e = ei("wu", [64, 128, D])
    wd_e = ei("wd", [16, 128, 8192])
    wk_e = ei("wk", [16, 128, D]); wv_e = ei("wv", [16, 128, D])
    wig_e = ei("wig", [16, 128, D]); wq0_e = ei("wq0", [16, 128, D])
    wgp_e = ei("wgp", [1, 128, 256]); bg_e = ei("bg", [16, 1])
    oneh_e = ei("onehot", [16, D])
    cp_e = ei("colpack", [D, NCP])
    selh_e = ei("selh", [24, 3])
    mmask_e = ei("mmask", [16, 8]); ommask_e = ei("ommask", [16, 8])
    ones8_e = ei("ones8", [8, 1])
    mnw_e = ei("mnw", [128, D]); mnb_e = ei("mnb", [128, D])
    wog_e = ei("wog", [16, 128, D]); wo_e = ei("wo", [16, 128, D])
    wa_e = ei("wa", [1, 128, 256])
    rg_e = ei("rg", [18, 8, 128, D]); rw1_e = ei("rw1", [18, 8, 128, D])
    rw2_e = ei("rw2", [18, 16, 128, 1024])
    rgbc_e = ei("rgbc", [128, 144]); rb1c_e = ei("rb1c", [128, 144])
    rbc_e = ei("rbc", [18, D]); oh18_e = ei("oh18", [18, 18 * 128])
    wkv_e = ei("wkv", [4, 128, D]); wcmp_e = ei("wcmp", [2, 128, 512])
    lgam_e = ei("lgam", [128, 256]); lbet_e = ei("lbet", [128, 256])
    wq_e = ei("wqc", [2, 128, 256]); wkvc_e = ei("wkvc", [4, 128, 256])
    w2l_e = ei("w2l", [4, 128, 256]); wtk_e = ei("wtk", [16, 128, 512])
    wor_e = ei("wor", [16, 128, 512])
    cosq_e = ei("cosq", [128, 2 * T]); sinq_e = ei("sinq", [128, 2 * T])
    cosk_e = ei("cosk", [128, 2 * 8192]); sink_e = ei("sink", [128, 2 * 8192])
    acol_e = ei("acol", [128, 128]); bcol_e = ei("bcol", [128, 128])
    ut_e = ei("ut", [128, 4 * 512]); bout_e = ei("bout", [128, 16])
    id32_e = ei("id32", [128, 128])
    idbf_e = nc.dram_tensor("idbf", [128, 128], BF16, kind="ExternalInput").ap()
    x_out = nc.dram_tensor("x_out", [T, D], F32, kind="ExternalOutput").ap()
    xA = nc.dram_tensor("xA", [T, D], F32).ap()
    xB = nc.dram_tensor("xB", [T, D], F32).ap()
    memD = nc.dram_tensor("memD", [128, 16 * T], BF16).ap()
    qhD = nc.dram_tensor("qhD", [128, 8 * D], BF16).ap()
    xcD = nc.dram_tensor("xcD", [128, 16 * T], BF16).ap()
    gmD = nc.dram_tensor("gmD", [16, T], F32).ap()
    cch_in = nc.dram_tensor("cch_in", [3, D], F32)
    cch_out = nc.dram_tensor("cch_out", [24, D], F32)
    ccpm_in = nc.dram_tensor("ccpm_in", [17, 128], F32)
    ccpm_out = nc.dram_tensor("ccpm_out", [8 * 17, 128], F32)
    ccch_in = nc.dram_tensor("ccch_in", [128, 2 * T], BF16)
    ccch_out = nc.dram_tensor("ccch_out", [8 * 128, 2 * T], BF16)

    outer = ExitStack()
    with outer:
        p = Prog(nc)
        cb = CoreBuild(p, nc, outer)
        cb.load_identity(id32_e, idbf_e)

        def ffn0_section(x_src, x_dst):
            with ExitStack() as sctx:
                cb.ctx = sctx
                xres = cb.sb("xres", [128, 8 * D], F32)
                x_tiles = [xres.ap()[:, t * D:(t + 1) * D] for t in range(8)]
                x_deps = []
                for t in range(8):
                    dl = p.dma("sync", lambda e, t=t: e.dma_start(
                        out=x_tiles[t], in_=x_src[t * 128:(t + 1) * 128, :]),
                        tag=f"xload{t}")
                    x_deps.append([dl])
                xn_fm = cb.sb("xnfm", [128, 16 * 8 * 128], BF16)
                xn_ready, _, _ = cb.emit_ln_to_fm(x_tiles, x_deps, xn_fm, 8)
                add_ops = cb.emit_ffn0(x_tiles, xn_fm, xn_ready, wg_e, wu_e, wd_e,
                                       8, coeff=0.5)
                stores = []
                for t in range(8):
                    st = p.dma("sync", lambda e, t=t: e.dma_start(
                        out=x_dst[t * 128:(t + 1) * 128, :], in_=x_tiles[t]),
                        deps=tuple(add_ops[t]), tag=f"xstore{t}")
                    stores.append(st)
            _add_barrier(p)
            return stores

        s1 = ffn0_section(x_ext, xA)
        pub_h = p.dma("sync", lambda e: e.dma_start(
            out=cch_in[:, :], in_=xA[T - 3:T, :]), deps=tuple(s1), tag="pub_h")
        cc_h = p.op("gpsimd", lambda e: e.collective_compute(
            "AllGather", ALU.bypass, replica_groups=RG8,
            ins=[cch_in.ap().opt()], outs=[cch_out.ap().opt()]),
            deps=(pub_h,), tag="cc_h")
        with ExitStack() as sctx:
            cb.ctx = sctx
            xh_sb = cb.sb("xh", [3, D], F32)
            with ExitStack() as hctx:
                cb.ctx = hctx
                gh_sb = cb.sb("gh", [24, D], F32)
                dgh = p.dma("sync", lambda e: e.dma_start(
                    out=gh_sb[:], in_=cch_out[:, :]), deps=(cc_h,), tag="ld_gh")
                selh_sb = cb.sb("selh", [24, 3], F32)
                dselh = p.dma("sync", lambda e: e.dma_start(out=selh_sb[:], in_=selh_e),
                              tag="selh")
                pH = cb.ps("pH", [128, 512], F32)
                hsel_evs = []
                prev_ev = None
                for c4 in range(4):
                    mmh = p.op("tensor", lambda e, c4=c4: e.matmul(
                        pH[0:3, 0:512], selh_sb[:],
                        gh_sb.ap()[:, c4 * 512:(c4 + 1) * 512],
                        start=True, stop=True),
                        deps=(dgh, dselh) + ((prev_ev,) if prev_ev else ()), tag="mm_h")
                    evh = p.op("vector", lambda e, c4=c4: e.tensor_copy(
                        xh_sb.ap()[0:3, c4 * 512:(c4 + 1) * 512], pH[0:3, 0:512]),
                        deps=(mmh,), tag="ev_h")
                    prev_ev = evh
                    hsel_evs.append(evh)
            cb.ctx = sctx
            _add_barrier(p)
            cp_sb = cb.sb("cp", [128, 16 * NCP], F32)
            dcp = p.dma("sync", lambda e: e.dma_start(
                out=cp_sb.ap().rearrange("p (h c) -> p h c", h=16),
                in_=cp_e.rearrange("(h p) c -> p h c", p=128)), tag="cp")
            bg_sb = cb.sb("bg", [16, 1], F32)
            dbg = p.dma("sync", lambda e: e.dma_start(out=bg_sb[:], in_=bg_e), tag="bg")
            oh_sb = cb.sb("oneh", [16, D], F32)
            doh = p.dma("sync", lambda e: e.dma_start(out=oh_sb[:], in_=oneh_e),
                        tag="oh")
            qhat_buf = cb.sb("qhat", [128, 8 * D], BF16)
            xc_buf = cb.sb("xc", [128, 16 * T], BF16)
            gamma_sb = cb.sb("gamma", [16, T], F32)
            tp_sb = cb.sb("tp", [16, 1], F32)
            mr_sb = cb.sb("mrow", [16, 128], F32)
            done = cb.emit_l0attn_A(xA, xh_sb.ap(), hsel_evs[-1], cp_sb, dcp,
                                    wk_e, wv_e, wig_e, wq0_e, wgp_e, bg_sb.ap(),
                                    dbg, oh_sb.ap(), doh, memD, qhat_buf, xc_buf,
                                    gamma_sb, tp_sb, mr_sb, NCP=NCP)
            st_qh = p.dma("sync", lambda e: e.dma_start(out=qhD, in_=qhat_buf[:]),
                          deps=tuple(done["qhat"]), tag="st_qh")
            st_xc = p.dma("sync", lambda e: e.dma_start(out=xcD, in_=xc_buf[:]),
                          deps=tuple(done["xc"]), tag="st_xc")
            st_gm = p.dma("sync", lambda e: e.dma_start(out=gmD, in_=gamma_sb[:]),
                          deps=tuple(done["gamma"]), tag="st_gm")
            pub_tp = p.dma("sync", lambda e: e.dma_start(
                out=ccpm_in[0:1, 0:16].transpose([1, 0]), in_=tp_sb[:],
                allow_slow_non_contiguous=True),
                deps=tuple(done["totalP"]), tag="pub_tp")
            pub_m = p.dma("sync", lambda e: e.dma_start(
                out=ccpm_in[1:17, :], in_=mr_sb[:]),
                deps=tuple(done["Mrow"]), tag="pub_m")
        cc_pm = p.op("gpsimd", lambda e: e.collective_compute(
            "AllGather", ALU.bypass, replica_groups=RG8,
            ins=[ccpm_in.ap().opt()], outs=[ccpm_out.ap().opt()]),
            deps=(pub_tp, pub_m, st_qh, st_xc, st_gm), tag="cc_pm")
        _add_barrier(p)
        with ExitStack() as sctx:
            cb.ctx = sctx

            def ld(name, shape, dt, src):
                t = cb.sb(name, shape, dt)
                dep = p.dma("sync", lambda e, t=t, src=src: e.dma_start(
                    out=t[:], in_=src), tag=f"ld_{name}")
                return t, dep

            mem_sb, dmem = ld("mem", [128, 16 * T], BF16, memD)
            qh_sb, dqh = ld("qhat", [128, 8 * D], BF16, qhD)
            xc_sb, dxc = ld("xc", [128, 16 * T], BF16, xcD)
            gm_sb, dgm = ld("gamma", [16, T], F32, gmD)
            ap_sb = cb.sb("allP", [8, 16], F32)
            dap = p.dma("sync", lambda e: e.dma_start(
                out=ap_sb.ap().rearrange("r (o f) -> r o f", o=1),
                in_=ccpm_out.ap().rearrange("(r s) c -> r s c", s=17)[:, 0:1, 0:16]),
                deps=(cc_pm,), tag="ld_allP")
            am_sb = cb.sb("allM", [8, D], F32)
            dam = p.dma("sync", lambda e: e.dma_start(
                out=am_sb.ap().rearrange("r (s c) -> r s c", s=16),
                in_=ccpm_out.ap().rearrange("(r s) c -> r s c", s=17)[:, 1:17, :]),
                deps=(cc_pm,), tag="ld_allM")
            mm_sb, dmm = ld("mmask", [16, 8], F32, mmask_e)
            om_sb, dom = ld("ommask", [16, 8], F32, ommask_e)
            oh_sb2, doh2 = ld("oneh2", [16, D], F32, oneh_e)
            o8_sb, do8 = ld("ones8", [8, 1], F32, ones8_e)
            mnw_sb, dmnw = ld("mnw", [128, D], F32, mnw_e)
            mnb_sb, dmnb = ld("mnb", [128, D], F32, mnb_e)
            cp_sb2 = cb.sb("cp2", [128, 16 * NCP], F32)
            dcp2 = p.dma("sync", lambda e: e.dma_start(
                out=cp_sb2.ap().rearrange("p (h c) -> p h c", h=16),
                in_=cp_e.rearrange("(h p) c -> p h c", p=128)), tag="cp2")
            cb.emit_l0attn_B(
                xA, xB, mem_sb, dmem, qh_sb, dqh, xc_sb, dxc, gm_sb, dgm,
                ap_sb.ap(), dap, am_sb.ap(), dam, mm_sb.ap(), dmm, om_sb.ap(), dom,
                oh_sb2.ap(), doh2, o8_sb.ap(), do8, cp_sb2, dcp2,
                mnw_sb, dmnw, mnb_sb, dmnb, wog_e, wo_e, NCP=NCP)
        _add_barrier(p)
        ffn0_section(xB, xA)
        def l1ffn_section(x_src, x_dst):
            with ExitStack() as sctx:
                cb.ctx = sctx
                rgb_sb = cb.sb("rgb", [128, 144], F32)
                drgb = p.dma("sync", lambda e: e.dma_start(out=rgb_sb[:], in_=rgbc_e),
                             tag="rgb")
                rb1_sb = cb.sb("rb1", [128, 144], F32)
                drb1 = p.dma("sync", lambda e: e.dma_start(out=rb1_sb[:], in_=rb1c_e),
                             tag="rb1")
                rbc_sb = cb.sb("rbc", [18, D], F32)
                drbc = p.dma("sync", lambda e: e.dma_start(out=rbc_sb[:], in_=rbc_e),
                             tag="rbc")
                oh_sb3 = cb.sb("oh18", [18, 18 * 128], F32)
                doh3 = p.dma("sync", lambda e: e.dma_start(out=oh_sb3[:], in_=oh18_e),
                             tag="oh18")
                cb.emit_l1ffn(x_src, x_dst, wa_e, rg_e, rw1_e, rw2_e,
                              rgb_sb, drgb, rb1_sb, drb1, rbc_sb, drbc,
                              oh_sb3.ap(), doh3, coeff=0.5)
            _add_barrier(p)

        l1ffn_section(xA, xB)
        with ExitStack() as sctx:
            cb.ctx = sctx
            lg_sb = cb.sb("lg", [128, 256], F32)
            dlg = p.dma("sync", lambda e: e.dma_start(out=lg_sb[:], in_=lgam_e),
                        tag="lg")
            lb_sb = cb.sb("lb", [128, 256], F32)
            dlb = p.dma("sync", lambda e: e.dma_start(out=lb_sb[:], in_=lbet_e),
                        tag="lb")
            ch_sb = cb.sb("ch", [128, 2 * T], BF16)
            ready = cb.emit_l1attn_C(xB, wkv_e, wcmp_e, lg_sb.ap(), dlg,
                                     lb_sb.ap(), dlb, ch_sb)
            pub_ch = p.dma("sync", lambda e: e.dma_start(
                out=ccch_in[:, :], in_=ch_sb[:]), deps=tuple(ready), tag="pub_ch")
        cc_ch = p.op("gpsimd", lambda e: e.collective_compute(
            "AllGather", ALU.bypass, replica_groups=RG8,
            ins=[ccch_in.ap().opt()], outs=[ccch_out.ap().opt()]),
            deps=(pub_ch,), tag="cc_ch")
        _add_barrier(p)
        with ExitStack() as sctx:
            cb.ctx = sctx

            def ld2(name, shape, dt, src, deps=()):
                t = cb.sb(name, shape, dt)
                dep = p.dma("sync", lambda e, t=t, src=src: e.dma_start(
                    out=t[:], in_=src), deps=deps, tag=f"ld_{name}")
                return t, dep

            cho_sb, dcho = ld2("chown", [128, 2 * T], BF16, ccch_in.ap())
            cq_sb, dcq = ld2("cosq", [128, 2 * T], F32, cosq_e)
            sq_sb, dsq2 = ld2("sinq", [128, 2 * T], F32, sinq_e)
            ac_sb, dac = ld2("acol", [128, 128], F32, acol_e)
            bc_sb, dbc = ld2("bcol", [128, 128], F32, bcol_e)
            ut_sb, dut = ld2("ut", [128, 4 * 512], F32, ut_e)
            bo_sb, dbo = ld2("bout", [128, 16], F32, bout_e)
            chall_view = ccch_out.ap().rearrange("(r p) f -> r p f", p=128)
            cb.emit_l1attn_D(
                xB, xA, cho_sb, dcho, chall_view, wq_e, wkvc_e, w2l_e, wtk_e,
                wor_e, cq_sb, dcq, sq_sb, dsq2, cosk_e, sink_e, ac_sb, dac,
                bc_sb, dbc, ut_sb, dut, bo_sb, dbo)
        _add_barrier(p)
        l1ffn_section(xA, x_out)
        last_stores = [o for o in p.ops if o.is_dma and o.tag.startswith("xs2_")][-8:]
        p.finalize_with_outputs(last_stores)
        _emit_prog(nc, p)
    return nc


def _fused_in_map(inputs, core):
    inp = {k: np.asarray(v, np.float32) for k, v in inputs.items()}
    x = inp["L0_attn_x"].reshape(T_FULL, D)
    i32, ibf = _ident_np()
    prepA = l0attn_host_prep(inp)
    prep1 = l1ffn_host_prep(inp)
    prepL = l1attn_host_prep(inp)
    i = core
    m = {"x": np.ascontiguousarray(x[i * T_CORE:(i + 1) * T_CORE]),
         "id32": i32, "idbf": ibf}
    m["wg"] = pack_lhsT(np.ascontiguousarray(inp["L0_ffn_w_gate"].T))
    m["wu"] = pack_lhsT(np.ascontiguousarray(inp["L0_ffn_w_up"].T))
    m["wd"] = pack_lhsT(np.ascontiguousarray(inp["L0_ffn_w_down"].T))
    m["wk"] = prepA["wk"]; m["wv"] = prepA["wv"]; m["wig"] = prepA["wig"]
    m["wq0"] = prepA["wq"]; m["wgp"] = prepA["wgp"]; m["bg"] = prepA["bg"]
    m["onehot"] = prepA["onehot"]; m["colpack"] = prepA["colpack"]
    m["wog"] = prepA["wog"]; m["wo"] = prepA["wo"]
    m["mnw"] = prepA["mnw"]; m["mnb"] = prepA["mnb"]
    selh = np.zeros((24, 3), np.float32)
    if i > 0:
        for r in range(3):
            selh[3 * (i - 1) + r, r] = 1.0
    m["selh"] = selh
    mmask = np.zeros((16, 8), np.float32); mmask[:, :i] = 1.0
    m["mmask"] = mmask; m["ommask"] = np.ascontiguousarray(1.0 - mmask)
    m["ones8"] = np.ones((8, 1), np.float32)
    for k in ("wa", "rg", "rw1", "rw2", "rgbc", "rb1c", "rbc", "oh18"):
        m[k] = prep1[k]
    for k in ("wkv", "wcmp", "lgam", "lbet", "wkvc", "w2l", "wtk", "wor",
              "cosk", "sink", "ut", "bout"):
        m[k] = prepL[k]
    m["wqc"] = prepL["wq"]
    acol = np.zeros((128, 128), np.float32)
    bcol = np.zeros((128, 128), np.float32)
    for qt in range(8):
        cstar = 2 * i + qt // 4
        for c in range(16):
            if c > cstar:
                acol[:, qt * 16 + c] = -1e30
            elif c == cstar:
                bcol[:, qt * 16 + c] = 1.0
    m["acol"] = acol; m["bcol"] = bcol
    sl = slice(i * T_CORE, (i + 1) * T_CORE)
    m["cosq"] = np.ascontiguousarray(np.concatenate(
        [prepL["cosk"][:, sl], prepL["cosk"][:, sl]], 1))
    m["sinq"] = np.ascontiguousarray(np.concatenate(
        [prepL["sink"][:, sl], prepL["sink"][:, sl]], 1))
    return m


def kernel_fused(**inputs):
    if "fused" not in _NC_CACHE:
        _NC_CACHE["fused"] = build_fused_nc()
    nc = _NC_CACHE["fused"]
    in_maps = [_fused_in_map(inputs, i) for i in range(N_CORES)]
    res = run_bass_kernel_spmd(nc, in_maps, core_ids=list(range(N_CORES)))
    x = np.concatenate([res.results[i]["x_out"] for i in range(N_CORES)], axis=0)
    return np.ascontiguousarray(x).reshape(1, T_FULL, D)


def kernel(**inputs):
    return kernel_fused(**inputs)


def kernel_unfused(**inputs):
    x = np.ascontiguousarray(np.asarray(inputs["L0_attn_x"], dtype=np.float32)
                             .reshape(T_FULL, D))
    wg = np.asarray(inputs["L0_ffn_w_gate"], np.float32)
    wu = np.asarray(inputs["L0_ffn_w_up"], np.float32)
    wd = np.asarray(inputs["L0_ffn_w_down"], np.float32)
    inp = {k: np.asarray(v, np.float32) for k, v in inputs.items()}
    prepA = l0attn_host_prep(inp)
    prep1 = l1ffn_host_prep(inp)
    prepL = l1attn_host_prep(inp)
    x = run_ffn0_launch(x, wg, wu, wd)
    a_outs = run_l0attnA_launch(x, prepA)
    x = run_l0attnB_launch(x, prepA, a_outs)
    x = run_ffn0_launch(x, wg, wu, wd)
    x = run_l1ffn_launch(x, prep1)
    x = run_l1attn_launches(x, prepL)
    x = run_l1ffn_launch(x, prep1)
    return np.ascontiguousarray(x).reshape(1, T_FULL, D)
